# version 09
# baseline (speedup 1.0000x reference)
import contextlib

import numpy as np
import concourse.bass as bass
import concourse.mybir as mybir
from concourse.bass_utils import run_bass_kernel_spmd
from concourse.bass_core_rust import IndirectOffsetOnAxis

F32 = mybir.dt.float32
I32 = mybir.dt.int32
U32 = mybir.dt.uint32
AF = mybir.ActivationFunctionType
ALU = mybir.AluOpType
AX = mybir.AxisListType

D_MODEL = 1024
D_INNER = 2048
CONV_DIM = 2304
NEXP = 8
SEQ = 4096
T_TILE = 512
G_PAIR = 1024
N_PAIRS = SEQ // G_PAIR
KC = 8
TH = 256
NBLK = 24
NSLOT = NBLK * 128

ENGS = ("sync", "tensor", "vector", "scalar", "gpsimd")


class _Op:
    __slots__ = ("eng", "fn", "deps", "sdeps", "dma", "inc_val", "needs_inc",
                 "dma_cap")

    def __init__(self, eng, fn, dma):
        self.eng = eng
        self.fn = fn
        self.dma = dma
        self.deps = []
        self.sdeps = []
        self.inc_val = 0
        self.needs_inc = False
        self.dma_cap = None


def _needs_sync(d, consumer_eng, strong):
    return strong or d.dma or d.eng != consumer_eng


class Prog:
    def __init__(self):
        self.ops = {e: [] for e in ENGS}
        self.writer = {}
        self.readers = {}
        self.alias = {}
        self.last_dma = {}

    def op(self, eng, fn, reads=(), writes=(), dma=False):
        o = _Op(eng, fn, dma)
        seen = set()

        def add(lst, d):
            if d is not None and d is not o and id(d) not in seen:
                seen.add(id(d))
                lst.append(d)

        for r in reads:
            add(o.sdeps, self.writer.get(r))
            self.readers.setdefault(r, []).append(o)
        for r in writes:
            add(o.deps, self.writer.get(r))
            for rd in self.readers.get(r, ()):
                add(o.deps, rd)
            self.writer[r] = o
            self.readers[r] = []
            for ak in self.alias.get(r, ()):
                add(o.deps, self.writer.get(ak))
                for rd in self.readers.get(ak, ()):
                    add(o.deps, rd)
                self.writer[ak] = o
                self.readers[ak] = []
        o.dma_cap = dict(self.last_dma)
        if dma:
            self.last_dma[eng] = o
        self.ops[eng].append(o)
        return o

    def dma(self, eng, fn, reads=(), writes=()):
        return self.op(eng, fn, reads=reads, writes=writes, dma=True)

    def emit(self, block, sems):
        for e in ENGS:
            for o in self.ops[e]:
                if o.dma:
                    o.needs_inc = True
                for d in o.deps:
                    if _needs_sync(d, e, False):
                        d.needs_inc = True
                for d in o.sdeps:
                    d.needs_inc = True
        counters = {}
        for e in ENGS:
            for o in self.ops[e]:
                if o.needs_inc:
                    key = (e, o.dma)
                    counters[key] = counters.get(key, 0) + (16 if o.dma else 1)
                    o.inc_val = counters[key]
        highwater = {e: {} for e in ENGS}

        def emit_engine(eng_obj, e):
            hw = highwater[e]
            for o in self.ops[e]:
                needed = {}
                for strong, dl in ((False, o.deps), (True, o.sdeps)):
                    for d in dl:
                        if not _needs_sync(d, e, strong):
                            continue
                        k = (d.eng, d.dma)
                        v = d.inc_val
                        if d.dma:
                            cap_op = o.dma_cap.get(d.eng)
                            if cap_op is not None and cap_op.inc_val > v:
                                v = min(v + 16, cap_op.inc_val)
                        if needed.get(k, 0) < v:
                            needed[k] = v
                for k in sorted(needed):
                    v = needed[k]
                    if hw.get(k, 0) < v:
                        eng_obj.wait_ge(sems[k], v)
                        hw[k] = v
                inst = o.fn(eng_obj)
                if o.needs_inc:
                    inst.then_inc(sems[(e, o.dma)], 16 if o.dma else 1)

        for e in ENGS:
            if self.ops[e]:
                getattr(block, e)(lambda eng_obj, _e=e: emit_engine(eng_obj, _e))


def build(n_pairs=N_PAIRS, dbg_names=()):
    nc = bass.Bass("TRN2", target_bir_lowering=False)
    P = Prog()

    def din(name, shape, dt=F32):
        return nc.dram_tensor(name, list(shape), dt, kind="ExternalInput").ap()

    xdr = din("x", (SEQ, D_MODEL))
    w13t = din("w13t", (NEXP, 8, 128, 2 * 8 * 512))
    w2t = din("w2t", (NEXP, 8, 128, 4 * D_MODEL))
    gate8d = din("gate8", (128, 8, 8))
    inwt = din("inwt", (9, 128, 8 * 512))
    opwt = din("opwt", (16, 128, D_MODEL))
    gatewt = din("gatewt", (6, 128, 8 * 512))
    upwt = din("upwt", (6, 128, 8 * 512))
    downwt = din("downwt", (6, 128, 4 * D_MODEL))
    convw_d = din("convw", (128, 18, 4))
    convb_d = din("convb", (128, 18))
    normw_d = din("normw_bc", (128, D_MODEL))
    postw_d = din("postw_bc", (128, D_MODEL))
    ssmw_d = din("ssmw_col", (128, 16))
    dtb_d = din("dtb", (32, 1))
    alog_d = din("alog", (32, 1))
    d2_d = din("d2", (128, 16))
    ident_d = din("identc", (128, 128))
    cuml_d = din("cuml", (128, 128))
    negsl_d = din("negsl", (128, 128))
    lowm_d = din("lowm", (128, 128))
    ones_d = din("onesc", (128, 128))
    negone_d = din("negone", (128, 1))
    pairsel_d = din("pairsel", (32, 16 * 128))
    constv_d = din("constv", (128, 8))
    tokid_d = din("tokid", (128, 8))
    iota8_d = din("iota8", (128, 8))
    padf_d = din("padf", (128, 2 * NBLK))
    baseft_d = din("baseft", (128, 8))
    biota_d = din("biota", (1, NBLK))

    odr = nc.dram_tensor("out", [SEQ, D_MODEL], F32, kind="ExternalOutput").ap()
    xmir = nc.dram_tensor("xmir", [G_PAIR + 128, D_MODEL], F32).ap()
    slotmap = nc.dram_tensor("slotmap", [NSLOT, 2], F32).ap()
    slotout = nc.dram_tensor("slotout", [NSLOT, D_MODEL], F32).ap()
    slotmap3 = slotmap.rearrange("(v p) c -> p v c", p=128)
    slotout3 = slotout.rearrange("(v p) d -> p v d", p=128)

    dbg = {}
    for name, shape in dbg_names:
        dbg[name] = nc.dram_tensor("dbg_" + name, list(shape), F32,
                                   kind="ExternalOutput").ap()
    dbg_ctr = {}

    def dump(name, ap, reads):
        if name not in dbg:
            return
        i = dbg_ctr.get(name, 0)
        dbg_ctr[name] = i + 1
        d = dbg[name]
        dst = d[i] if len(d.shape) == len(ap.shape) + 1 else d
        P.dma("gpsimd", lambda g, _d=dst, _a=ap: g.dma_start(out=_d, in_=_a),
              reads=list(reads))

    sb = lambda name, shape, dt=F32: nc.alloc_sbuf_tensor(name, list(shape),
                                                          dt).ap()
    ident = sb("identS", (128, 128))
    cuml = sb("cumlS", (128, 128))
    negsl = sb("negslS", (128, 128))
    lowmask = sb("lowmS", (128, 128))
    ones = sb("onesS", (128, 128))
    negone = sb("negoneS", (128, 1))
    pairsel = sb("pairselS", (32, 16 * 128))
    normw = sb("normwS", (128, D_MODEL))
    postw = sb("postwS", (128, D_MODEL))
    ssmw = sb("ssmwS", (128, 16))
    convw = sb("convwS", (128, 18, 4))
    convb = sb("convbS", (128, 18))
    dtb = sb("dtbS", (32, 1))
    alog = sb("alogS", (32, 1))
    eA = sb("eAS", (32, 1))
    d2 = sb("d2S", (128, 16))
    gatew = sb("gatewS", (128, 8, 8))
    S = sb("Sstate", (128, 16, 128))
    tail = sb("tailS", (128, 18, 3))
    xb = sb("xbS", (128, 8, D_MODEL))
    vb = sb("vbS", (128, 4, D_MODEL))
    stat = sb("statS", (128, 16))
    rsc = sb("rscS", (128, 28))
    rsu = sb("rsuS", (128, 8), U32)
    rowstat = sb("rowstatS", (1, TH))
    tmsc = sb("tmscS", (128, D_MODEL))
    sqsc = tmsc
    dtxa = sb("dtxaS", (128, 128))
    dtxb = sb("dtxbS", (128, 128))
    tokid = sb("tokidS", (128, 8))
    iota8 = sb("iota8S", (128, 8))
    padf = sb("padfS", (128, 2 * NBLK))
    baseft = sb("baseftS", (128, 8))
    biota = sb("biotaS", (1, NBLK))
    constv = sb("constvS", (128, 8))
    for _i, _cval in enumerate([1e-5, 1e-6, 1.0, -1.0, 1.0 / 2048.0]):
        nc.const_aps.register(F32, float(_cval), constv[:, _i:_i + 1])

    arena_elems = nc.sbuf_bytes_remaining // 4 - 64
    arena = nc.alloc_sbuf_tensor("arena", [128, arena_elems], F32).ap()
    _ov_ranges = {"F": [], "A": []}

    class _OvAlloc:
        def __init__(self, mode):
            self.mode = mode
            self.off = 0

        def __call__(self, shape, keys, dt=F32):
            n = 1
            for d_ in shape[1:]:
                n *= d_
            n_pad = -(-n // 16) * 16
            assert self.off + n_pad <= arena_elems, \
                f"overlay {self.mode} overflow at {shape}"
            view = arena[0:shape[0], self.off:self.off + n]
            if dt is not F32:
                view = view.bitcast(dt)
            if len(shape) == 3:
                view = view.rearrange("p (a b) -> p a b", a=shape[1])
            _ov_ranges[self.mode].append((self.off, self.off + n_pad, keys))
            self.off += n_pad
            return view

    fal = _OvAlloc("F")
    aal = _OvAlloc("A")
    w13sb = fal((128, 2, 2 * 8 * 512), [("w13sb", 0), ("w13sb", 1)])
    w1sb = w13sb
    w2sb = fal((128, 2, 4 * D_MODEL), [("w2sb", 0), ("w2sb", 1)])
    stg = fal((128, D_MODEL), [("stg",)])
    xg = fal((128, 2, D_MODEL), [("xg", 0), ("xg", 1)])
    xgn = fal((128, D_MODEL), [("xgn",)])
    xgfm = fal((128, 2, 8 * 128), [("xgfm", 0), ("xgfm", 1)])
    hsp = fal((128, 4, 128), [("hsp", c) for c in range(4)])
    pst = fal((128, 8, 8), [("pst", t) for t in range(8)])
    sel1b = fal((128, 8, 8), [("sel1b", t) for t in range(8)])
    sel2b = fal((128, 8, 8), [("sel2b", t) for t in range(8)])
    wib = fal((128, 8, 8), [("wib", t) for t in range(8)])
    s12 = fal((128, 8, 4), [("s12", t) for t in range(8)])
    sidxi = fal((128, 8, 2), [("sidxi", t) for t in range(8)], dt=I32)
    asc = fal((128, 8), [("asc",)])
    asc2 = fal((128, 8), [("asc2",)])
    stage2 = fal((128, 2), [("stage2",)])
    slotbuf = fal((128, NBLK, 2), [("slotbuf",)])
    toki = fal((128, 2), [("toki", 0), ("toki", 1)], dt=I32)
    idxt = fal((128, 2, NBLK), [("idxt", 0), ("idxt", 1)])
    idxalli = fal((128, 8, NBLK), [("idxalli",)], dt=I32)
    crow = fal((1, 8), [("crow",)])
    r8a = fal((1, 8), [("r8a",)])
    r8b = fal((1, 8), [("r8b",)])
    r8c = fal((1, 8), [("r8c",)])
    sbrow = fal((1, 8), [("sbrow",)])
    zrow = fal((1, 8), [("zrow",)])
    ebrow = fal((1, NBLK), [("ebrow",)])
    ebi = fal((1, NBLK), [("ebi",)], dt=I32)
    eb1024 = fal((1, NBLK), [("eb1024",)])
    sbbc = fal((128, 8), [("sbbc",)])

    r1f = aal((128, KC, TH), [("r1f", c) for c in range(KC)])
    zf = aal((128, 16, TH), [("zf", j) for j in range(16)])
    ext = aal((128, 18, TH + 4), [("ext", c) for c in range(18)])
    xsy = aal((128, 16, TH), [("xsy", j) for j in range(16)])
    bcf = aal((128, 2, TH), [("bcf", 0), ("bcf", 1)])
    dtf = aal((32, T_TILE), [("dtf", 0), ("dtf", 1)])
    inw = aal((128, 2, 8 * 512), [("inw", 0), ("inw", 1)])
    opw = aal((128, 2, D_MODEL), [("opw", s) for s in range(2)])
    a_fm = aal((32, 256), [("a_fm",)])
    a_tm = aal((128, 32), [("a_tm",)])
    dt_tm = aal((128, 32), [("dt_tm",)])
    pcs_tm = aal((128, 32), [("pcs_tm",)])
    dcol = aal((128, 32), [("dcol",)])
    wdt_tm = aal((128, 32), [("wdt_tm",)])
    npcs_fm = aal((32, 128), [("npcs_fm",)])
    ecs_fm = aal((32, 128), [("ecs_fm",)])
    lam_dg = aal((32, 32), [("lam_dg",)])
    lam_rep = aal((128, 32), [("lam_rep",)])
    lam_col = aal((32, 1), [("lam_col",)])
    bct = aal((128, 128), [("bct",)])
    b_tm = aal((128, 128), [("b_tm",)])
    xs_tm = aal((128, 128), [("xs_tm",)])
    wdtx = aal((128, 128), [("wdtx",)])
    marg = aal((128, 2, 128), [("marg", 0), ("marg", 1)])
    mt_sb = marg
    gt_sb = aal((128, 2, 128), [("gt", 0), ("gt", 1)])
    esc = aal((128, 128), [("esc",)])
    t1s = aal((128, 128), [("t1s",)])
    t2s = aal((128, 128), [("t2s",)])
    zs = aal((128, 256), [("zs",)])
    rbc = aal((128, TH), [("rbc",)])
    xnfh = aal((128, 8, 256), [("xnfh", c) for c in range(8)])
    hsh = aal((128, 4, 256), [("hsh", c) for c in range(4)])
    _sw_lo = _ov_ranges["A"][-2][0]
    _f_wlimit = 16384 + 8192
    assert _sw_lo >= _f_wlimit, (_sw_lo, _f_wlimit)

    for o0, o1, keys in _ov_ranges["F"]:
        for p0, p1, keys2 in _ov_ranges["A"]:
            if o0 < p1 and p0 < o1:
                for k in keys:
                    P.alias.setdefault(k, set()).update(keys2)
                for k in keys2:
                    P.alias.setdefault(k, set()).update(keys)

    ps = [nc.alloc_psum_tensor(f"ps{i}", [128, 512], F32).ap() for i in range(8)]

    R = lambda *k: k

    def T_op(fn, reads, writes):
        P.op("tensor", fn, reads=reads, writes=writes)

    def V_op(fn, reads, writes):
        P.op("vector", fn, reads=reads, writes=writes)

    def S_op(fn, reads, writes):
        P.op("scalar", fn, reads=reads, writes=writes)

    def ld(dst, src, wr):
        P.dma("gpsimd", lambda g, _d=dst, _s=src: g.dma_start(out=_d, in_=_s),
              writes=[wr])

    ld(ident[:, :], ident_d[:, :], R("ident"))
    ld(cuml[:, :], cuml_d[:, :], R("cuml"))
    ld(negsl[:, :], negsl_d[:, :], R("negsl"))
    ld(lowmask[:, :], lowm_d[:, :], R("lowmask"))
    ld(ones[:, :], ones_d[:, :], R("ones"))
    ld(negone[:, :], negone_d[:, :], R("negone"))
    ld(pairsel[:, :], pairsel_d[:, :], R("pairsel"))
    ld(constv[:, :], constv_d[:, :], R("constv"))
    ld(normw[:, :], normw_d[:, :], R("normw"))
    ld(postw[:, :], postw_d[:, :], R("postw"))
    ld(ssmw[:, :], ssmw_d[:, :], R("ssmw"))
    ld(convw[:, :, :], convw_d[:, :, :], R("convw"))
    ld(convb[:, :], convb_d[:, :], R("convb"))
    ld(dtb[:, :], dtb_d[:, :], R("dtb"))
    ld(alog[:, :], alog_d[:, :], R("alog"))
    ld(d2[:, :], d2_d[:, :], R("d2"))
    ld(gatew[:, :, :], gate8d[:, :, :], R("gatew"))
    ld(tokid[:, :], tokid_d[:, :], R("tokid"))
    ld(iota8[:, :], iota8_d[:, :], R("iota8"))
    ld(padf[:, :], padf_d[:, :], R("padf"))
    ld(baseft[:, :], baseft_d[:, :], R("baseft"))
    ld(biota[:, :], biota_d[:, :], R("biota"))
    S_op(lambda s: s.activation(eA[:, :], alog[:, :], AF.Exp),
         reads=[R("alog")], writes=[R("eA")])
    P.op("vector", lambda v: v.memset(S[:, :, :], 0.0),
         writes=[R("S", j) for j in range(16)])
    P.op("vector", lambda v: v.memset(tail[:, :, :], 0.0),
         writes=[R("tail", c) for c in range(18)])
    P.op("vector", lambda v: v.memset(dtxa[:, :], 0.0), writes=[R("dtxa")])
    P.op("vector", lambda v: v.memset(dtxb[:, :], 0.0), writes=[R("dtxb")])
    P.op("vector", lambda v: v.memset(tmsc[:, :], 0.0), writes=[R("tmsc")])
    P.dma("gpsimd", lambda g: g.dma_start(out=xmir[G_PAIR:G_PAIR + 128, :],
                                          in_=tmsc[:, :]),
          reads=[R("tmsc")], writes=[R("xmir")])

    def transpose_to(psi, out_ap, in_ap, pin, reads, writes, copy_eng="scalar"):
        n = in_ap.shape[-1]
        T_op(lambda t, _i=in_ap, _n=n, _p=pin, _b=psi:
             t.transpose(ps[_b][0:_n, 0:_p], _i, ident[0:_p, 0:_p]),
             reads=list(reads) + [R("ident")], writes=[R("ps", psi)])
        if copy_eng == "scalar":
            S_op(lambda s, _o=out_ap, _n=n, _p=pin, _b=psi:
                 s.activation(_o, ps[_b][0:_n, 0:_p], AF.Copy),
                 reads=[R("ps", psi)], writes=writes)
        else:
            V_op(lambda v, _o=out_ap, _n=n, _p=pin, _b=psi:
                 v.tensor_copy(_o, ps[_b][0:_n, 0:_p]),
                 reads=[R("ps", psi)], writes=writes)

    def emit_norm(x_ap, out_ap, sset, reads_x, writes_out, rms=False, eps=1e-5):
        width = D_MODEL
        sm = stat[:, 4 * sset + 0:4 * sset + 1]
        qs = stat[:, 4 * sset + 1:4 * sset + 2]
        rs = stat[:, 4 * sset + 2:4 * sset + 3]
        nb = stat[:, 4 * sset + 3:4 * sset + 4]
        rS = R("stat", sset)
        if not rms:
            V_op(lambda v, _x=x_ap, _o=sm: v.tensor_reduce(_o, _x, AX.X,
                                                           ALU.add),
                 reads=list(reads_x), writes=[rS])
        S_op(lambda s, _x=x_ap, _o=qs: s.activation(sqsc[:, :], _x, AF.Square,
                                                    accum_out=_o),
             reads=list(reads_x), writes=[R("tmsc"), rS])
        if not rms:
            V_op(lambda v: v.tensor_scalar(sm, sm, 1.0 / width, None, ALU.mult),
                 reads=[rS], writes=[rS])
            V_op(lambda v: v.scalar_tensor_tensor(rs, sm, -1.0, sm, ALU.mult,
                                                  ALU.mult),
                 reads=[rS], writes=[rS])
            V_op(lambda v: v.tensor_scalar(qs, qs, 1.0 / width, None, ALU.mult),
                 reads=[rS], writes=[rS])
            V_op(lambda v: v.tensor_tensor(qs, qs, rs, ALU.add),
                 reads=[rS], writes=[rS])
        else:
            V_op(lambda v: v.tensor_scalar(qs, qs, 1.0 / width, None, ALU.mult),
                 reads=[rS], writes=[rS])
        S_op(lambda s: s.activation(rs, qs, AF.Sqrt, bias=float(eps)),
             reads=[rS, R("constv")], writes=[rS])
        V_op(lambda v: v.reciprocal(rs, rs), reads=[rS], writes=[rS])
        if not rms:
            V_op(lambda v: v.scalar_tensor_tensor(nb, sm, -1.0, rs, ALU.mult,
                                                  ALU.mult),
                 reads=[rS], writes=[rS])
            S_op(lambda s, _x=x_ap, _o=out_ap: s.activation(_o, _x, AF.Identity,
                                                            bias=nb, scale=rs),
                 reads=list(reads_x) + [rS], writes=list(writes_out))
        else:
            S_op(lambda s, _x=x_ap, _o=out_ap: s.activation(_o, _x, AF.Identity,
                                                            scale=rs),
                 reads=list(reads_x) + [rS], writes=list(writes_out))

    def tm_to_fm(src_fn, src_reads, dst, dst_tag, t0, pb0=6, nch=KC):
        for c in range(nch):
            pb = pb0 + (c % 2)
            T_op(lambda t, _s=src_fn(c), _b=pb:
                 t.transpose(ps[_b][:, 0:128], _s, ident[:, :]),
                 reads=list(src_reads) + [R("ident")], writes=[R("ps", pb)])
            S_op(lambda s, _c=c, _b=pb, _t0=t0:
                 s.activation(dst[:, _c, _t0:_t0 + 128], ps[_b][:, 0:128],
                              AF.Copy),
                 reads=[R("ps", pb)], writes=[R(dst_tag, c)])

    regcell = {}

    def emit_ffn_sparse(which):
        for g in range(8):
            P.dma("gpsimd", lambda q, _g=g: q.dma_start(
                out=xmir[128 * _g:128 * (_g + 1), :], in_=xb[:, _g, :]),
                reads=[R("xb", g)], writes=[R("xmir")])
        P.dma("gpsimd", lambda q: q.dma_start(out=slotmap3[:, :, :],
                                              in_=padf[:, :]),
              reads=[R("padf")], writes=[R("slotmap")])
        V_op(lambda v: v.memset(crow[:, :], 0.0), reads=[], writes=[R("crow")])
        for ts in range(8):
            emit_norm(xb[:, ts, :], tmsc[:, :], ts % 4, [R("xb", ts)],
                      [R("tmsc")])
            xnf8 = xgfm[:, 0, :].rearrange("p (a b) -> p a b", a=8)
            for c in range(8):
                pb2 = 6 + (c % 2)
                T_op(lambda t, _c=c, _b=pb2:
                     t.transpose(ps[_b][:, 0:128],
                                 tmsc[:, 128 * _c:128 * (_c + 1)],
                                 ident[:, :]),
                     reads=[R("tmsc"), R("ident")], writes=[R("ps", pb2)])
                S_op(lambda s, _c=c, _b=pb2, _x=xnf8:
                     s.activation(_x[:, _c, :], ps[_b][:, 0:128], AF.Copy),
                     reads=[R("ps", pb2)], writes=[R("xgfm", 0)])
            pb = 5
            for kc in range(KC):
                T_op(lambda t, _kc=kc, _b=pb, _x=xnf8:
                     t.matmul(ps[_b][:, 0:8], _x[:, _kc, :],
                              gatew[:, _kc, :], start=(_kc == 0),
                              stop=(_kc == KC - 1)),
                     reads=[R("xgfm", 0), R("gatew")], writes=[R("ps", pb)])
            lg = rsc[:, 0:8]
            vals = rsc[:, 8:16]
            idxf = rsc[:, 16:24]
            mx = rsc[:, 24:25]
            winv = rsc[:, 25:26]
            rr = R("rsc")
            V_op(lambda v: v.tensor_reduce(mx, ps[pb][:, 0:8], AX.X, ALU.max,
                                           negate=True),
                 reads=[R("ps", pb)], writes=[rr])
            S_op(lambda s: s.activation(lg, ps[pb][:, 0:8], AF.Exp, bias=mx),
                 reads=[R("ps", pb), rr], writes=[rr])
            V_op(lambda v: v.max(out=vals, in_=lg), reads=[rr], writes=[rr])
            V_op(lambda v: v.max_index(out=rsu[:, :], in_max=vals,
                                       in_values=lg),
                 reads=[rr], writes=[R("rsu")])
            V_op(lambda v: v.tensor_copy(idxf, rsu[:, :]),
                 reads=[R("rsu")], writes=[rr])
            V_op(lambda v: v.tensor_tensor(winv, rsc[:, 8:9], rsc[:, 9:10],
                                           ALU.add),
                 reads=[rr], writes=[rr])
            V_op(lambda v: v.reciprocal(winv, winv), reads=[rr], writes=[rr])
            V_op(lambda v, _t=ts: v.tensor_scalar(sel1b[:, _t, :], iota8[:, :],
                                                  rsc[:, 16:17], None,
                                                  ALU.is_equal),
                 reads=[rr, R("iota8")], writes=[R("sel1b", ts)])
            V_op(lambda v, _t=ts: v.tensor_scalar(sel2b[:, _t, :], iota8[:, :],
                                                  rsc[:, 17:18], None,
                                                  ALU.is_equal),
                 reads=[rr, R("iota8")], writes=[R("sel2b", ts)])
            V_op(lambda v, _t=ts: v.tensor_tensor(asc[:, :], sel1b[:, _t, :],
                                                  sel2b[:, _t, :], ALU.add),
                 reads=[R("sel1b", ts), R("sel2b", ts)], writes=[R("asc")])
            T_op(lambda t: t.matmul(ps[6][:, 0:8], cuml[:, :], asc[:, :],
                                    start=True, stop=False),
                 reads=[R("cuml"), R("asc")], writes=[R("ps", 6)])
            T_op(lambda t: t.matmul(ps[6][:, 0:8], ones[0:1, :], crow[:, :],
                                    start=False, stop=True),
                 reads=[R("ones"), R("crow")], writes=[R("ps", 6)])
            V_op(lambda v, _t=ts: v.tensor_copy(pst[:, _t, :], ps[6][:, 0:8]),
                 reads=[R("ps", 6)], writes=[R("pst", ts)])
            T_op(lambda t: t.matmul(ps[7][0:1, 0:8], ones[:, 0:1], asc[:, :],
                                    start=True, stop=True),
                 reads=[R("ones"), R("asc")], writes=[R("ps", 7)])
            V_op(lambda v: v.tensor_tensor(crow[:, :], crow[:, :],
                                           ps[7][0:1, 0:8], ALU.add),
                 reads=[R("ps", 7), R("crow")], writes=[R("crow")])
            V_op(lambda v: v.tensor_tensor(asc[:, :], asc[:, :], lg, ALU.mult),
                 reads=[R("asc"), rr], writes=[R("asc")])
            V_op(lambda v, _t=ts: v.tensor_scalar(wib[:, _t, :], asc[:, :],
                                                  winv, 0.5, ALU.mult,
                                                  ALU.mult),
                 reads=[R("asc"), rr], writes=[R("wib", ts)])
        V_op(lambda v: v.tensor_scalar(r8a[:, :], crow[:, :], 0.0, None,
                                       ALU.is_gt),
             reads=[R("crow")], writes=[R("r8a")])
        for j in range(1, 8):
            V_op(lambda v, _j=j: v.scalar_tensor_tensor(
                 r8a[:, :], crow[:, :], 128.0 * _j, r8a[:, :], ALU.is_gt,
                 ALU.add),
                 reads=[R("crow"), R("r8a")], writes=[R("r8a")])
        V_op(lambda v: v.memset(zrow[:, :], 0.0), reads=[], writes=[R("zrow")])
        V_op(lambda v: v.tensor_tensor_scan(r8c[:, :], r8a[:, :], zrow[:, :],
                                            0.0, ALU.add, ALU.add),
             reads=[R("r8a"), R("zrow")], writes=[R("r8c")])
        V_op(lambda v: v.tensor_tensor(r8b[:, :], r8c[:, :], r8a[:, :],
                                       ALU.subtract),
             reads=[R("r8c"), R("r8a")], writes=[R("r8b")])
        V_op(lambda v: v.tensor_scalar(sbrow[:, :], r8b[:, :], 128.0, None,
                                       ALU.mult),
             reads=[R("r8b")], writes=[R("sbrow")])
        V_op(lambda v: v.tensor_scalar(ebrow[:, :], biota[:, :],
                                       r8b[0:1, 0:1], None, ALU.is_ge),
             reads=[R("biota"), R("r8b")], writes=[R("ebrow")])
        for e in range(1, 8):
            V_op(lambda v, _e=e: v.scalar_tensor_tensor(
                 ebrow[:, :], biota[:, :], r8b[0:1, _e:_e + 1], ebrow[:, :],
                 ALU.is_ge, ALU.add),
                 reads=[R("biota"), R("r8b"), R("ebrow")], writes=[R("ebrow")])
        V_op(lambda v: v.tensor_scalar(ebrow[:, :], ebrow[:, :], -1.0, 7.0,
                                       ALU.add, ALU.min),
             reads=[R("ebrow")], writes=[R("ebrow")])
        V_op(lambda v: v.tensor_copy(ebi[:, :], ebrow[:, :]),
             reads=[R("ebrow")], writes=[R("ebi")])
        V_op(lambda v: v.tensor_scalar(eb1024[:, :], ebrow[:, :], 1024.0,
                                       None, ALU.mult),
             reads=[R("ebrow")], writes=[R("eb1024")])
        T_op(lambda t: t.matmul(ps[5][:, 0:8], ones[0:1, :], sbrow[:, :],
                                start=True, stop=True),
             reads=[R("ones"), R("sbrow")], writes=[R("ps", 5)])
        V_op(lambda v: v.tensor_copy(sbbc[:, :], ps[5][:, 0:8]),
             reads=[R("ps", 5)], writes=[R("sbbc")])
        T_op(lambda t: t.matmul(ps[6][:, 0:NBLK], ones[0:1, :],
                                eb1024[0:1, :], start=True, stop=True),
             reads=[R("ones"), R("eb1024")], writes=[R("ps", 6)])
        for ftx in range(8):
            fp = ftx % 2
            V_op(lambda v, _f=ftx, _p=fp: v.tensor_scalar(
                 idxt[:, _p, :], ps[6][:, 0:NBLK], baseft[:, _f:_f + 1],
                 None, ALU.add),
                 reads=[R("ps", 6), R("baseft")], writes=[R("idxt", fp)])
            V_op(lambda v, _f=ftx, _p=fp: v.tensor_copy(
                 idxalli[:, _f, :], idxt[:, _p, :]),
                 reads=[R("idxt", fp)], writes=[R("idxalli")])
        for ts in range(8):
            for k in range(2):
                selb = sel1b if k == 0 else sel2b
                stag = ("sel1b", ts) if k == 0 else ("sel2b", ts)
                V_op(lambda v, _t=ts: v.tensor_tensor(asc[:, :],
                                                      pst[:, _t, :],
                                                      sbbc[:, :], ALU.add),
                     reads=[R("pst", ts), R("sbbc")], writes=[R("asc")])
                V_op(lambda v, _t=ts, _s=selb: v.tensor_tensor(
                     asc2[:, :], asc[:, :], _s[:, _t, :], ALU.mult),
                     reads=[R("asc"), R(*stag)], writes=[R("asc2")])
                V_op(lambda v, _t=ts, _k=k: v.tensor_reduce(
                     s12[:, _t, _k:_k + 1], asc2[:, :], AX.X, ALU.add),
                     reads=[R("asc2")], writes=[R("s12", ts)])
                V_op(lambda v, _t=ts, _k=k: v.tensor_scalar(
                     s12[:, _t, _k:_k + 1], s12[:, _t, _k:_k + 1], -1.0, None,
                     ALU.add),
                     reads=[R("s12", ts)], writes=[R("s12", ts)])
                V_op(lambda v, _t=ts, _s=selb: v.tensor_tensor(
                     asc2[:, :], wib[:, _t, :], _s[:, _t, :], ALU.mult),
                     reads=[R("wib", ts), R(*stag)], writes=[R("asc2")])
                V_op(lambda v, _t=ts, _k=k: v.tensor_reduce(
                     s12[:, _t, 2 + _k:3 + _k], asc2[:, :], AX.X, ALU.add),
                     reads=[R("asc2")], writes=[R("s12", ts)])
            V_op(lambda v, _t=ts: v.tensor_copy(sidxi[:, _t, :],
                                                s12[:, _t, 0:2]),
                 reads=[R("s12", ts)], writes=[R("sidxi", ts)])
            for k in range(2):
                V_op(lambda v, _t=ts: v.tensor_copy(stage2[:, 0:1],
                                                    tokid[:, _t:_t + 1]),
                     reads=[R("tokid")], writes=[R("stage2")])
                V_op(lambda v, _t=ts, _k=k: v.tensor_copy(
                     stage2[:, 1:2], s12[:, _t, 2 + _k:3 + _k]),
                     reads=[R("s12", ts)], writes=[R("stage2")])
                P.dma("gpsimd", lambda q, _t=ts, _k=k: q.indirect_dma_start(
                      out=slotmap[:, :],
                      out_offset=IndirectOffsetOnAxis(sidxi[:, _t, _k:_k + 1],
                                                      0),
                      in_=stage2[:, :], in_offset=None),
                      reads=[R("stage2"), R("sidxi", ts)],
                      writes=[R("slotmap")])
        P.dma("gpsimd", lambda q: q.dma_start(out=slotbuf[:, :, :],
                                              in_=slotmap3[:, :, :]),
              reads=[R("slotmap")], writes=[R("slotbuf")])
        dump("crow", crow[:, :], [R("crow")])
        dump("sbrow", sbrow[:, :], [R("sbrow")])
        dump("ebrow", ebrow[:, :], [R("ebrow")])
        dump("s12", s12[:, :, :], [R("s12", t) for t in range(8)])
        dump("slotbuf", slotbuf[:, :, :], [R("slotbuf")])
        dump("wib", wib[:, :, :], [R("wib", t) for t in range(8)])
        w13flat = w13t.rearrange("e f p m -> (e f p) m")
        w2flat = w2t.rearrange("e f p m -> (e f p) m")

        def emit_wgather(step):
            if step >= 8 * NBLK:
                return
            sb_, sf = divmod(step, 8)
            par = step % 2
            P.dma("gpsimd", lambda q, _b=sb_, _f=sf, _p=par:
                  q.indirect_dma_start(
                  out=w13sb[:, _p, :], out_offset=None, in_=w13flat,
                  in_offset=IndirectOffsetOnAxis(idxalli[:, _f, _b:_b + 1],
                                                 0)),
                  reads=[R("idxalli")], writes=[R("w13sb", par)])
            P.dma("gpsimd", lambda q, _b=sb_, _f=sf, _p=par:
                  q.indirect_dma_start(
                  out=w2sb[:, _p, :], out_offset=None, in_=w2flat,
                  in_offset=IndirectOffsetOnAxis(idxalli[:, _f, _b:_b + 1],
                                                 0)),
                  reads=[R("idxalli")], writes=[R("w2sb", par)])

        def emit_prologue_dma(b):
            p2 = b % 2
            V_op(lambda v, _b=b, _p=p2: v.tensor_copy(toki[:, _p:_p + 1],
                                                      slotbuf[:, _b, 0:1]),
                 reads=[R("slotbuf")], writes=[R("toki", p2)])
            P.dma("gpsimd", lambda q, _p=p2: q.indirect_dma_start(
                  out=xg[:, _p, :], out_offset=None, in_=xmir[:, :],
                  in_offset=IndirectOffsetOnAxis(toki[:, _p:_p + 1], 0)),
                  reads=[R("toki", p2), R("xmir")], writes=[R("xg", p2)])

        def emit_prologue_ln(b):
            p2 = b % 2
            emit_norm(xg[:, p2, :], xgn[:, :], b % 4, [R("xg", p2)],
                      [R("xgn")])
            xgfmb = xgfm[:, p2, :].rearrange("p (a b) -> p a b", a=8)
            for c in range(8):
                pb2 = 4 + 2 * p2 + (c % 2)
                T_op(lambda t, _c=c, _b=pb2:
                     t.transpose(ps[_b][:, 0:128],
                                 xgn[:, 128 * _c:128 * (_c + 1)],
                                 ident[:, :]),
                     reads=[R("xgn"), R("ident")], writes=[R("ps", pb2)])
                S_op(lambda s, _c=c, _b=pb2, _x=xgfmb, _p=p2:
                     s.activation(_x[:, _c, :], ps[_b][:, 0:128], AF.Copy),
                     reads=[R("ps", pb2)], writes=[R("xgfm", p2)])

        def emit_compute(b):
            p2 = b % 2
            tb = 4 + 2 * p2
            xgfmb = xgfm[:, p2, :].rearrange("p (a b) -> p a b", a=8)
            for ft in range(8):
                step = 8 * b + ft
                par = step % 2
                w1v = w13sb[:, par, 0:4096].rearrange("p (k m) -> p k m", k=8)
                w3v = w13sb[:, par, 4096:8192].rearrange("p (k m) -> p k m",
                                                         k=8)
                w2v = w2sb[:, par, :].rearrange("p (k m) -> p k m", k=4)
                for fc in range(4):
                    bg = 2 * (fc % 2)
                    bu = bg + 1
                    for kc in range(KC):
                        T_op(lambda t, _w=w1v, _fc=fc, _kc=kc, _b=bg,
                             _x=xgfmb, _p=par:
                             t.matmul(ps[_b][:, 0:128],
                                      _w[:, _kc, 128 * _fc:128 * (_fc + 1)],
                                      _x[:, _kc, :], start=(_kc == 0),
                                      stop=(_kc == KC - 1)),
                             reads=[R("w13sb", par), R("xgfm", p2)],
                             writes=[R("ps", bg)])
                    for kc in range(KC):
                        T_op(lambda t, _w=w3v, _fc=fc, _kc=kc, _b=bu,
                             _x=xgfmb, _p=par:
                             t.matmul(ps[_b][:, 0:128],
                                      _w[:, _kc, 128 * _fc:128 * (_fc + 1)],
                                      _x[:, _kc, :], start=(_kc == 0),
                                      stop=(_kc == KC - 1)),
                             reads=[R("w13sb", par), R("xgfm", p2)],
                             writes=[R("ps", bu)])
                    S_op(lambda s, _fc=fc, _b=bg:
                         s.activation(hsp[:, _fc, :], ps[_b][:, 0:128],
                                      AF.Silu),
                         reads=[R("ps", bg)], writes=[R("hsp", fc)])
                    V_op(lambda v, _fc=fc, _b=bu:
                         v.tensor_tensor(hsp[:, _fc, :], hsp[:, _fc, :],
                                         ps[_b][:, 0:128], ALU.mult),
                         reads=[R("ps", bu), R("hsp", fc)],
                         writes=[R("hsp", fc)])
                    for dh in range(2):
                        bd = tb + dh
                        T_op(lambda t, _w=w2v, _fc=fc, _dh=dh, _b=bd, _ft=ft,
                             _p=par:
                             t.matmul(ps[_b][:, :], hsp[:, _fc, :],
                                      _w[:, _fc, 512 * _dh:512 * (_dh + 1)],
                                      start=(_ft == 0 and _fc == 0),
                                      stop=(_ft == 7 and _fc == 3)),
                             reads=[R("hsp", fc), R("w2sb", par)],
                             writes=[R("ps", bd)])
                emit_wgather(step + 2)
            for dh in range(2):
                S_op(lambda s, _dh=dh, _b=tb + dh, _bb=b:
                     s.activation(stg[:, 512 * _dh:512 * (_dh + 1)],
                                  ps[_b][:, :], AF.Copy,
                                  scale=slotbuf[:, _bb, 1:2]),
                     reads=[R("ps", tb + dh), R("slotbuf")], writes=[R("stg")])
            dump("sod", stg[:, :], [R("stg")])
            P.dma("sync", lambda q, _b=b: q.dma_start(
                  out=slotout3[:, _b, :], in_=stg[:, :]),
                  reads=[R("stg")], writes=[R("slotout")])

        emit_wgather(0)
        emit_wgather(1)
        emit_prologue_dma(0)
        emit_prologue_dma(1)
        emit_prologue_ln(0)
        emit_prologue_ln(1)
        for b in range(NBLK):
            if b + 2 < NBLK:
                emit_prologue_dma(b + 2)
            emit_compute(b)
            if b + 2 < NBLK:
                emit_prologue_ln(b + 2)
        for ts in range(8):
            for k in range(2):
                P.dma("gpsimd", lambda q, _t=ts, _k=k: q.indirect_dma_start(
                      out=xg[:, _k, :], out_offset=None, in_=slotout[:, :],
                      in_offset=IndirectOffsetOnAxis(
                          sidxi[:, _t, _k:_k + 1], 0)),
                      reads=[R("sidxi", ts), R("slotout")],
                      writes=[R("xg", k)])
                V_op(lambda v, _t=ts, _k=k: v.tensor_tensor(
                     xb[:, _t, :], xb[:, _t, :], xg[:, _k, :], ALU.add),
                     reads=[R("xg", k), R("xb", ts)], writes=[R("xb", ts)])

    def emit_swiglu(gb):
        fts = [512] * 5 + [192]
        for hh in range(2):
            for i, tsu in enumerate((2 * hh, 2 * hh + 1)):
                emit_norm(vb[:, tsu, :], tmsc[:, :], tsu, [R("vb", tsu)],
                          [R("tmsc")], rms=True, eps=1e-6)
                V_op(lambda v: v.tensor_tensor(tmsc[:, :], tmsc[:, :],
                                               postw[:, :], ALU.mult),
                     reads=[R("tmsc"), R("postw")], writes=[R("tmsc")])
                tm_to_fm(lambda c: tmsc[:, 128 * c:128 * (c + 1)],
                         [R("tmsc")], xnfh, "xnfh", 128 * i, pb0=2)
            for ft in range(6):
                par = ft % 2
                P.dma("sync", lambda q, _s=gatewt[ft], _p=par: q.dma_start(
                      out=w13sb[:, _p, 0:4096], in_=_s),
                      writes=[R("w13sb", par)])
                P.dma("sync", lambda q, _s=upwt[ft], _p=par: q.dma_start(
                      out=w13sb[:, _p, 4096:8192], in_=_s),
                      writes=[R("w13sb", par)])
                P.dma("sync", lambda q, _s=downwt[ft], _p=par: q.dma_start(
                      out=w2sb[:, _p, :], in_=_s), writes=[R("w2sb", par)])
                w1v = w13sb[:, par, 0:4096].rearrange("p (k m) -> p k m", k=8)
                w3v = w13sb[:, par, 4096:8192].rearrange("p (k m) -> p k m",
                                                         k=8)
                w2v = w2sb[:, par, :].rearrange("p (k m) -> p k m", k=4)
                fsz = fts[ft]
                nfc = (fsz + 127) // 128
                for fc in range(nfc):
                    m = min(128, fsz - 128 * fc)
                    bg = 2 * (fc % 2)
                    bu = bg + 1
                    for kc in range(KC):
                        T_op(lambda t, _w=w1v, _fc=fc, _kc=kc, _m=m, _b=bg:
                             t.matmul(ps[_b][0:_m, 0:256],
                                      _w[:, _kc, 128 * _fc:128 * _fc + _m],
                                      xnfh[:, _kc, :], start=(_kc == 0),
                                      stop=(_kc == KC - 1)),
                             reads=[R("w13sb", par), R("xnfh", kc)],
                             writes=[R("ps", bg)])
                    for kc in range(KC):
                        T_op(lambda t, _w=w3v, _fc=fc, _kc=kc, _m=m, _b=bu:
                             t.matmul(ps[_b][0:_m, 0:256],
                                      _w[:, _kc, 128 * _fc:128 * _fc + _m],
                                      xnfh[:, _kc, :], start=(_kc == 0),
                                      stop=(_kc == KC - 1)),
                             reads=[R("w13sb", par), R("xnfh", kc)],
                             writes=[R("ps", bu)])
                    S_op(lambda s, _fc=fc, _m=m, _b=bg:
                         s.activation(hsh[0:_m, _fc, :], ps[_b][0:_m, 0:256],
                                      AF.Silu),
                         reads=[R("ps", bg)], writes=[R("hsh", fc)])
                    V_op(lambda v, _fc=fc, _m=m, _b=bu:
                         v.tensor_tensor(hsh[0:_m, _fc, :], hsh[0:_m, _fc, :],
                                         ps[_b][0:_m, 0:256], ALU.mult),
                         reads=[R("ps", bu), R("hsh", fc)],
                         writes=[R("hsh", fc)])
                    for tsl in range(2):
                        for dh in range(2):
                            bd = 4 + 2 * tsl + dh
                            T_op(lambda t, _w=w2v, _fc=fc, _dh=dh, _b=bd,
                                 _t=tsl, _m=m, _ft=ft, _n=nfc:
                                 t.matmul(ps[_b][:, :],
                                          hsh[0:_m, _fc,
                                              128 * _t:128 * (_t + 1)],
                                          _w[0:_m, _fc,
                                             512 * _dh:512 * (_dh + 1)],
                                          start=(_ft == 0 and _fc == 0),
                                          stop=(_ft == 5 and _fc == _n - 1)),
                                 reads=[R("hsh", fc), R("w2sb", par)],
                                 writes=[R("ps", bd)])
            for tsl in range(2):
                tsu = 2 * hh + tsl
                for dh in range(2):
                    bd = 4 + 2 * tsl + dh
                    xa = xb[:, gb + tsu, 512 * dh:512 * (dh + 1)]
                    V_op(lambda v, _b=bd, _x=xa, _g=gb + tsu:
                         v.tensor_tensor(_x, _x, ps[_b][:, :], ALU.add),
                         reads=[R("ps", bd), R("xb", gb + tsu)],
                         writes=[R("xb", gb + tsu)])
        for tsu in range(4):
            V_op(lambda v, _t=tsu, _g=gb + tsu:
                 v.tensor_tensor(xb[:, _g, :], xb[:, _g, :], vb[:, _t, :],
                                 ALU.add),
                 reads=[R("xb", gb + tsu), R("vb", tsu)],
                 writes=[R("xb", gb + tsu)])

    def emit_attn(gb):
        for tsu in range(4):
            emit_norm(xb[:, gb + tsu, :], vb[:, tsu, :], tsu,
                      [R("xb", gb + tsu)], [R("vb", tsu)])
        dump("u", vb[:, :, :], [R("vb", t) for t in range(4)])
        for half in range(2):
            for i, tsu in enumerate((2 * half, 2 * half + 1)):
                emit_norm(vb[:, tsu, :], tmsc[:, :], tsu, [R("vb", tsu)],
                          [R("tmsc")], rms=True, eps=1e-6)
                V_op(lambda v: v.tensor_tensor(tmsc[:, :], tmsc[:, :],
                                               normw[:, :], ALU.mult),
                     reads=[R("tmsc"), R("normw")], writes=[R("tmsc")])
                tm_to_fm(lambda c: tmsc[:, 128 * c:128 * (c + 1)], [R("tmsc")],
                         r1f, "r1f", 128 * i)
            for mt in range(9):
                par = mt % 2
                wr = R("inw", par)
                P.dma("sync", lambda g, _d=inw[:, par, :], _s=inwt[mt]:
                      g.dma_start(out=_d, in_=_s), writes=[wr])
                inwv = inw[:, par, :].rearrange("p (k m) -> p k m", k=8)
                msizes = [128] * 4 if mt < 8 else [128, 128, 32]
                for fs, msz in enumerate(msizes):
                    m0 = 512 * mt + 128 * fs
                    pb = fs % 2
                    for kc in range(KC):
                        T_op(lambda t, _w=inwv, _fs=fs, _kc=kc, _m=msz, _b=pb:
                             t.matmul(ps[_b][0:_m, 0:TH],
                                      _w[:, _kc, 128 * _fs:128 * _fs + _m],
                                      r1f[:, _kc, :], start=(_kc == 0),
                                      stop=(_kc == KC - 1)),
                             reads=[wr, R("r1f", kc)], writes=[R("ps", pb)])
                    if m0 < D_INNER:
                        S_op(lambda s, _c=m0 // 128, _b=pb:
                             s.activation(zf[:, _c, :], ps[_b][:, 0:TH],
                                          AF.Copy),
                             reads=[R("ps", pb)], writes=[R("zf", m0 // 128)])
                    elif m0 < D_INNER + CONV_DIM:
                        c = (m0 - D_INNER) // 128
                        S_op(lambda s, _c=c, _b=pb:
                             s.activation(ext[:, _c, 3:3 + TH],
                                          ps[_b][:, 0:TH], AF.Copy),
                             reads=[R("ps", pb)], writes=[R("ext", c)])
                    else:
                        S_op(lambda s, _b=pb:
                             s.activation(a_fm[:, 0:TH], ps[_b][0:32, 0:TH],
                                          AF.Exp, bias=dtb[:, 0:1]),
                             reads=[R("ps", pb), R("dtb")], writes=[R("a_fm")])
                        S_op(lambda s, _h=half:
                             s.activation(dtf[:, TH * _h:TH * (_h + 1)],
                                          a_fm[:, 0:TH], AF.Ln, bias=1.0),
                             reads=[R("a_fm"), R("constv")],
                             writes=[R("dtf", half)])
            for c in range(18):
                V_op(lambda v, _c=c: v.tensor_copy(ext[:, _c, 0:3],
                                                   tail[:, _c, :]),
                     reads=[R("tail", c)], writes=[R("ext", c)])
                V_op(lambda v, _c=c: v.tensor_copy(tail[:, _c, :],
                                                   ext[:, _c, TH:TH + 3]),
                     reads=[R("ext", c)], writes=[R("tail", c)])
                V_op(lambda v, _c=c: v.tensor_scalar(zs[:, 0:TH],
                                                     ext[:, _c, 0:TH],
                                                     convw[:, _c, 0:1], None,
                                                     ALU.mult),
                     reads=[R("ext", c), R("convw")], writes=[R("zs")])
                for k in range(1, 4):
                    V_op(lambda v, _c=c, _k=k:
                         v.scalar_tensor_tensor(zs[:, 0:TH],
                                                ext[:, _c, _k:_k + TH],
                                                convw[:, _c, _k:_k + 1],
                                                zs[:, 0:TH], ALU.mult,
                                                ALU.add),
                         reads=[R("ext", c), R("convw"), R("zs")],
                         writes=[R("zs")])
                if c < 16:
                    dst, wt = xsy[:, c, :], R("xsy", c)
                else:
                    dst, wt = bcf[:, c - 16, :], R("bcf", c - 16)
                S_op(lambda s, _d=dst, _c=c:
                     s.activation(_d, zs[:, 0:TH], AF.Silu,
                                  bias=convb[:, _c:_c + 1]),
                     reads=[R("zs"), R("convb")], writes=[wt])
            for lc in range(2):
                tc0 = 128 * lc
                dts = dtf[:, TH * half + tc0:TH * half + tc0 + 128]
                rdt = R("dtf", half)
                V_op(lambda v, _d=dts: v.tensor_scalar(a_fm[:, 0:128], _d,
                                                       eA[:, 0:1], None,
                                                       ALU.mult),
                     reads=[rdt, R("eA")], writes=[R("a_fm")])
                transpose_to(7, a_tm[:, :], a_fm[0:32, 0:128], 32, [R("a_fm")],
                             [R("a_tm")], copy_eng="vector")
                transpose_to(7, dt_tm[:, :], dts, 32, [rdt], [R("dt_tm")],
                             copy_eng="vector")
                T_op(lambda t: t.matmul(ps[6][:, 0:32], cuml[:, :], a_tm[:, :],
                                        start=True, stop=True),
                     reads=[R("cuml"), R("a_tm")], writes=[R("ps", 6)])
                V_op(lambda v: v.tensor_copy(pcs_tm[:, :], ps[6][:, 0:32]),
                     reads=[R("ps", 6)], writes=[R("pcs_tm")])
                T_op(lambda t: t.matmul(ps[6][:, 0:32], negsl[:, :],
                                        a_tm[:, :], start=True, stop=True),
                     reads=[R("negsl"), R("a_tm")], writes=[R("ps", 6)])
                S_op(lambda s: s.activation(dcol[:, :], ps[6][:, 0:32],
                                            AF.Exp),
                     reads=[R("ps", 6)], writes=[R("dcol")])
                V_op(lambda v: v.tensor_tensor(wdt_tm[:, :], dt_tm[:, :],
                                               dcol[:, :], ALU.mult),
                     reads=[R("dt_tm"), R("dcol")], writes=[R("wdt_tm")])
                T_op(lambda t: t.transpose(ps[6][0:32, 0:128], pcs_tm[:, :],
                                           ident[:, :]),
                     reads=[R("pcs_tm"), R("ident")], writes=[R("ps", 6)])
                S_op(lambda s: s.activation(npcs_fm[:, :], ps[6][0:32, 0:128],
                                            AF.Identity, scale=-1.0),
                     reads=[R("ps", 6), R("constv")], writes=[R("npcs_fm")])
                S_op(lambda s: s.activation(ecs_fm[:, :], npcs_fm[:, :],
                                            AF.Exp),
                     reads=[R("npcs_fm")], writes=[R("ecs_fm")])
                T_op(lambda t: t.matmul(ps[6][0:32, 0:1], a_tm[:, :],
                                        negone[:, :], start=True, stop=True),
                     reads=[R("a_tm"), R("negone")], writes=[R("ps", 6)])
                S_op(lambda s: s.activation(lam_col[:, :], ps[6][0:32, 0:1],
                                            AF.Exp),
                     reads=[R("ps", 6)], writes=[R("lam_col")])
                V_op(lambda v: v.tensor_scalar(lam_dg[:, :], ident[0:32, 0:32],
                                               lam_col[:, 0:1], None,
                                               ALU.mult),
                     reads=[R("ident"), R("lam_col")], writes=[R("lam_dg")])
                T_op(lambda t: t.matmul(ps[6][:, 0:32], ones[0:32, :],
                                        lam_dg[:, :], start=True, stop=True),
                     reads=[R("ones"), R("lam_dg")], writes=[R("ps", 6)])
                V_op(lambda v: v.tensor_copy(lam_rep[:, :], ps[6][:, 0:32]),
                     reads=[R("ps", 6)], writes=[R("lam_rep")])
                Bc = bcf[:, 0, tc0:tc0 + 128]
                Cc = bcf[:, 1, tc0:tc0 + 128]
                T_op(lambda t, _B=Bc, _C=Cc: t.matmul(ps[6][:, 0:128], _B, _C,
                                                      start=True, stop=True),
                     reads=[R("bcf", 0), R("bcf", 1)], writes=[R("ps", 6)])
                S_op(lambda s: s.activation(bct[:, :], ps[6][:, 0:128],
                                            AF.Copy),
                     reads=[R("ps", 6)], writes=[R("bct")])
                transpose_to(6, b_tm[:, :], Bc, 128, [R("bcf", 0)],
                             [R("b_tm")])
                for j in range(16):
                    xs_ap = xsy[:, j, tc0:tc0 + 128]
                    rxs = R("xsy", j)
                    transpose_to(0, xs_tm[:, :], xs_ap, 128, [rxs],
                                 [R("xs_tm")])
                    V_op(lambda v, _j=j: v.tensor_scalar(
                         dtxa[:, 0:64], xs_tm[:, 0:64],
                         dt_tm[:, 2 * _j:2 * _j + 1], None, ALU.mult),
                         reads=[R("xs_tm"), R("dt_tm")], writes=[R("dtxa")])
                    V_op(lambda v, _j=j: v.tensor_scalar(
                         dtxb[:, 64:128], xs_tm[:, 64:128],
                         dt_tm[:, 2 * _j + 1:2 * _j + 2], None, ALU.mult),
                         reads=[R("xs_tm"), R("dt_tm")], writes=[R("dtxb")])
                    V_op(lambda v, _j=j: v.tensor_scalar(
                         wdtx[:, 0:64], xs_tm[:, 0:64],
                         wdt_tm[:, 2 * _j:2 * _j + 1], None, ALU.mult),
                         reads=[R("xs_tm"), R("wdt_tm")], writes=[R("wdtx")])
                    V_op(lambda v, _j=j: v.tensor_scalar(
                         wdtx[:, 64:128], xs_tm[:, 64:128],
                         wdt_tm[:, 2 * _j + 1:2 * _j + 2], None, ALU.mult),
                         reads=[R("xs_tm"), R("wdt_tm")], writes=[R("wdtx")])
                    for hh in range(2):
                        h = 2 * j + hh
                        pb = 1 + hh
                        T_op(lambda t, _h=h, _b=pb:
                             t.matmul(ps[_b][:, 0:128],
                                      ident[0:32, _h:_h + 1]
                                      .broadcast_to((32, 128)),
                                      npcs_fm[:, :], start=True, stop=True),
                             reads=[R("ident"), R("npcs_fm")],
                             writes=[R("ps", pb)])
                        V_op(lambda v, _hh=hh, _b=pb:
                             v.tensor_tensor(marg[:, _hh, :],
                                             ps[_b][:, 0:128],
                                             lowmask[:, :], ALU.add),
                             reads=[R("ps", pb), R("lowmask")],
                             writes=[R("marg", hh)])
                        S_op(lambda s, _hh=hh, _h=h:
                             s.activation(mt_sb[:, _hh, :], marg[:, _hh, :],
                                          AF.Exp, bias=pcs_tm[:, _h:_h + 1]),
                             reads=[R("marg", hh), R("pcs_tm")],
                             writes=[R("mt", hh)])
                        V_op(lambda v, _hh=hh:
                             v.tensor_tensor(gt_sb[:, _hh, :],
                                             mt_sb[:, _hh, :], bct[:, :],
                                             ALU.mult),
                             reads=[R("mt", hh), R("bct")],
                             writes=[R("gt", hh)])
                    T_op(lambda t: t.matmul(ps[3][:, 0:128], dtxa[:, :],
                                            gt_sb[:, 0, :], start=True,
                                            stop=False),
                         reads=[R("dtxa"), R("gt", 0)], writes=[R("ps", 3)])
                    T_op(lambda t: t.matmul(ps[3][:, 0:128], dtxb[:, :],
                                            gt_sb[:, 1, :], start=False,
                                            stop=True),
                         reads=[R("dtxb"), R("gt", 1)], writes=[R("ps", 3)])
                    T_op(lambda t: t.matmul(ps[4][:, 0:128], b_tm[:, :],
                                            wdtx[:, :], start=True,
                                            stop=True),
                         reads=[R("b_tm"), R("wdtx")], writes=[R("ps", 4)])
                    T_op(lambda t, _j=j, _C=Cc:
                         t.matmul(ps[5][:, 0:128], S[:, _j, :], _C,
                                  start=True, stop=True),
                         reads=[R("S", j), R("bcf", 1)], writes=[R("ps", 5)])
                    T_op(lambda t, _j=j:
                         t.matmul(ps[6][:, 0:128],
                                  pairsel[:, 128 * _j:128 * (_j + 1)],
                                  ecs_fm[:, :], start=True, stop=True),
                         reads=[R("pairsel"), R("ecs_fm")],
                         writes=[R("ps", 6)])
                    S_op(lambda s: s.activation(esc[:, :], ps[6][:, 0:128],
                                                AF.Copy),
                         reads=[R("ps", 6)], writes=[R("esc")])
                    V_op(lambda v: v.tensor_tensor(t1s[:, :], esc[:, :],
                                                   ps[5][:, 0:128], ALU.mult),
                         reads=[R("esc"), R("ps", 5)], writes=[R("t1s")])
                    V_op(lambda v, _j=j, _x=xs_ap:
                         v.scalar_tensor_tensor(t2s[:, :], _x,
                                                d2[:, _j:_j + 1], t1s[:, :],
                                                ALU.mult, ALU.add),
                         reads=[rxs, R("d2"), R("t1s")], writes=[R("t2s")])
                    V_op(lambda v, _x=xs_ap: v.tensor_tensor(_x, t2s[:, :],
                                                             ps[3][:, 0:128],
                                                             ALU.add),
                         reads=[R("t2s"), R("ps", 3)], writes=[rxs])
                    for hh in range(2):
                        sl = slice(64 * hh, 64 * (hh + 1))
                        V_op(lambda v, _j=j, _sl=sl, _h=2 * j + hh:
                             v.scalar_tensor_tensor(S[:, _j, _sl],
                                                    S[:, _j, _sl],
                                                    lam_rep[:, _h:_h + 1],
                                                    ps[4][:, _sl], ALU.mult,
                                                    ALU.add),
                             reads=[R("S", j), R("lam_rep"), R("ps", 4)],
                             writes=[R("S", j)])
            for j in range(16):
                S_op(lambda s, _j=j: s.activation(zs[:, 0:TH], zf[:, _j, :],
                                                  AF.Silu),
                     reads=[R("zf", j)], writes=[R("zs")])
                V_op(lambda v, _j=j: v.tensor_tensor(xsy[:, _j, :],
                                                     xsy[:, _j, :],
                                                     zs[:, 0:TH], ALU.mult),
                     reads=[R("xsy", j), R("zs")], writes=[R("xsy", j)])
            for j in range(16):
                S_op(lambda s, _j=j: s.activation(zs[:, 0:TH], xsy[:, _j, :],
                                                  AF.Square),
                     reads=[R("xsy", j)], writes=[R("zs")])
                T_op(lambda t, _j=j: t.matmul(ps[7][0:1, 0:TH], ones[:, 0:1],
                                              zs[:, 0:TH], start=(_j == 0),
                                              stop=(_j == 15)),
                     reads=[R("zs"), R("ones")], writes=[R("ps", 7)])
            S_op(lambda s: s.activation(rowstat[0:1, :], ps[7][0:1, 0:TH],
                                        AF.Sqrt, scale=1.0 / D_INNER,
                                        bias=1e-6),
                 reads=[R("ps", 7), R("constv")], writes=[R("rowstat")])
            V_op(lambda v: v.reciprocal(rowstat[0:1, :], rowstat[0:1, :]),
                 reads=[R("rowstat")], writes=[R("rowstat")])
            T_op(lambda t: t.matmul(ps[7][:, 0:TH], ones[0:1, :],
                                    rowstat[0:1, :], start=True, stop=True),
                 reads=[R("rowstat"), R("ones")], writes=[R("ps", 7)])
            S_op(lambda s: s.activation(rbc[:, :], ps[7][:, 0:TH], AF.Copy),
                 reads=[R("ps", 7)], writes=[R("rbc")])
            for j in range(16):
                V_op(lambda v, _j=j:
                     v.scalar_tensor_tensor(xsy[:, _j, :], xsy[:, _j, :],
                                            ssmw[:, _j:_j + 1], rbc[:, :],
                                            ALU.mult, ALU.mult),
                     reads=[R("xsy", j), R("ssmw"), R("rbc")],
                     writes=[R("xsy", j)])
            for slot in range(2):
                P.dma("sync", lambda g, _d=opw[:, slot, :], _s=opwt[slot]:
                      g.dma_start(out=_d, in_=_s), writes=[R("opw", slot)])
            for j in range(16):
                for tsl in range(2):
                    for dh in range(2):
                        bd = 4 + 2 * tsl + dh
                        T_op(lambda t, _j=j, _b=bd, _t=tsl, _dh=dh:
                             t.matmul(ps[_b][:, :],
                                      xsy[:, _j, 128 * _t:128 * (_t + 1)],
                                      opw[:, _j % 2,
                                          512 * _dh:512 * (_dh + 1)],
                                      start=(_j == 0), stop=(_j == 15)),
                             reads=[R("xsy", j), R("opw", j % 2)],
                             writes=[R("ps", bd)])
                if j + 2 <= 15:
                    P.dma("sync",
                          lambda g, _d=opw[:, j % 2, :], _s=opwt[j + 2]:
                          g.dma_start(out=_d, in_=_s),
                          writes=[R("opw", j % 2)])
            for tsl in range(2):
                tsu = 2 * half + tsl
                for dh in range(2):
                    bd = 4 + 2 * tsl + dh
                    V_op(lambda v, _b=bd, _t=tsu, _dh=dh:
                         v.tensor_tensor(vb[:, _t,
                                            512 * _dh:512 * (_dh + 1)],
                                         vb[:, _t,
                                            512 * _dh:512 * (_dh + 1)],
                                         ps[_b][:, :], ALU.add),
                         reads=[R("ps", bd), R("vb", tsu)],
                         writes=[R("vb", tsu)])
        dump("v", vb[:, :, :], [R("vb", t) for t in range(4)])
        emit_swiglu(gb)

    store_regions = []
    for pair in range(n_pairs):
        for g in range(8):
            P.dma("gpsimd",
                  lambda q, _p=pair, _g=g: q.dma_start(
                      out=xb[:, _g, :],
                      in_=xdr[G_PAIR * _p + 128 * _g:
                              G_PAIR * _p + 128 * (_g + 1), :]),
                  writes=[R("xb", g)])
        emit_ffn_sparse(0)
        dump("xffn1", xb[:, :, :], [R("xb", g) for g in range(8)])
        emit_attn(0)
        emit_attn(4)
        dump("xattn", xb[:, :, :], [R("xb", g) for g in range(8)])
        emit_ffn_sparse(1)
        for g in range(8):
            sr = R("stored", pair, g)
            store_regions.append(sr)
            P.dma("gpsimd",
                  lambda q, _p=pair, _g=g: q.dma_start(
                      out=odr[G_PAIR * _p + 128 * _g:
                              G_PAIR * _p + 128 * (_g + 1), :],
                      in_=xb[:, _g, :]),
                  reads=[R("xb", g)], writes=[sr])
    P.op("gpsimd", lambda g: g.nop(), reads=store_regions)

    with contextlib.ExitStack() as ctx:
        sems = {}
        for e in ENGS:
            sems[(e, False)] = ctx.enter_context(nc.semaphore(f"sc_{e}"))
            sems[(e, True)] = ctx.enter_context(nc.semaphore(f"sq_{e}"))
        block = ctx.enter_context(nc.Block())
        P.emit(block, sems)
    return nc



def _host_layouts(inputs):
    f = np.float32
    c = np.ascontiguousarray
    w1 = np.asarray(inputs["L0_ffn_w1"], f)
    w3 = np.asarray(inputs["L0_ffn_w3"], f)
    w2 = np.asarray(inputs["L0_ffn_w2"], f)
    hw = {}
    w1l = w1.reshape(8, 8, 128, 8, 512).transpose(0, 3, 2, 1, 4) \
        .reshape(8, 8, 128, 8 * 512)
    w3l = w3.reshape(8, 8, 128, 8, 512).transpose(0, 3, 2, 1, 4) \
        .reshape(8, 8, 128, 8 * 512)
    hw["w13t"] = c(np.concatenate([w1l, w3l], axis=-1))
    hw["w2t"] = c(w2.reshape(8, 8, 4, 128, 1024).transpose(0, 1, 3, 2, 4)
                  .reshape(8, 8, 128, 4 * 1024))
    gw = np.asarray(inputs["L0_ffn_gate_w"], f)
    hw["gate8"] = c(gw.reshape(8, 128, 8).transpose(1, 0, 2))
    ipw = np.asarray(inputs["L0_attn_in_proj_w"], f)
    t = np.zeros((1024, 4608), f)
    t[:, :4384] = ipw.T
    hw["inwt"] = c(t.reshape(8, 128, 9, 512).transpose(2, 1, 0, 3)
                   .reshape(9, 128, 8 * 512))
    opw = np.asarray(inputs["L0_attn_out_proj_w"], f)
    hw["opwt"] = c(opw.T.reshape(16, 128, 1024))
    gw2 = np.asarray(inputs["L0_attn_gate_w"], f)
    uw = np.asarray(inputs["L0_attn_up_w"], f)
    dw = np.asarray(inputs["L0_attn_down_w"], f)

    def updown_lhsT(w):
        t = np.zeros((1024, 3072), f)
        t[:, :2752] = w.T
        return c(t.reshape(8, 128, 6, 512).transpose(2, 1, 0, 3)
                 .reshape(6, 128, 8 * 512))

    hw["gatewt"] = updown_lhsT(gw2)
    hw["upwt"] = updown_lhsT(uw)
    t = np.zeros((3072, 1024), f)
    t[:2752, :] = dw.T
    hw["downwt"] = c(t.reshape(6, 4, 128, 1024).transpose(0, 2, 1, 3)
                     .reshape(6, 128, 4 * 1024))
    cw = np.asarray(inputs["L0_attn_conv_w"], f)
    hw["convw"] = c(cw.reshape(18, 128, 4).transpose(1, 0, 2))
    cb = np.asarray(inputs["L0_attn_conv_b"], f)
    hw["convb"] = c(cb.reshape(18, 128).T)
    hw["normw_bc"] = c(np.broadcast_to(
        np.asarray(inputs["L0_attn_norm_w"], f), (128, 1024)))
    hw["postw_bc"] = c(np.broadcast_to(
        np.asarray(inputs["L0_attn_post_norm_w"], f), (128, 1024)))
    hw["ssmw_col"] = c(np.asarray(inputs["L0_attn_ssm_norm_w"], f)
                       .reshape(16, 128).T)
    hw["dtb"] = c(np.asarray(inputs["L0_attn_dt_bias"], f).reshape(32, 1))
    hw["alog"] = c(np.asarray(inputs["L0_attn_A_log"], f).reshape(32, 1))
    Dv = np.asarray(inputs["L0_attn_D"], f)
    p = np.arange(128)[:, None] // 64
    j = np.arange(16)[None, :]
    hw["d2"] = c(Dv[2 * j + p].astype(f))
    hw["constv"] = c(np.tile(np.array(
        [1e-5, 1e-6, 1.0, -1.0, 1.0 / 2048.0, 0.0, 0.0, 0.0], f), (128, 1)))
    hw["identc"] = np.eye(128, dtype=f)
    si = np.arange(128)[:, None]
    ti = np.arange(128)[None, :]
    hw["cuml"] = (si <= ti).astype(f)
    hw["negsl"] = -((si > ti).astype(f))
    hw["lowm"] = np.where(si <= ti, 0.0, -1e30).astype(f)
    hw["onesc"] = np.ones((128, 128), f)
    hw["negone"] = np.full((128, 1), -1.0, f)
    hhh = np.arange(32)[:, None]
    jj = np.arange(16 * 128)[None, :]
    hw["pairsel"] = (hhh == (2 * (jj // 128) + (jj % 128) // 64)).astype(f)
    pp = np.arange(128)[:, None]
    tt = np.arange(8)[None, :]
    hw["tokid"] = c((128 * tt + pp).astype(f))
    hw["iota8"] = c(np.tile(np.arange(8, dtype=f), (128, 1)))
    hw["padf"] = c(np.tile(np.array([float(G_PAIR), 0.0], f), (128, NBLK)))
    hw["biota"] = c(np.arange(NBLK, dtype=f).reshape(1, NBLK))
    hw["baseft"] = c((128 * tt + pp).astype(f))
    return hw


_NC_CACHE = {}


def kernel(**inputs):
    x = np.ascontiguousarray(np.asarray(inputs["L0_attn_hidden_states"],
                                        np.float32))
    hw = _host_layouts(inputs)
    if "nc" not in _NC_CACHE:
        _NC_CACHE["nc"] = build()
    nc = _NC_CACHE["nc"]
    in_maps = [dict(hw, x=np.ascontiguousarray(x[c0])) for c0 in range(8)]
    res = run_bass_kernel_spmd(nc, in_maps, core_ids=list(range(8)))
    return np.stack([res.results[c0]["out"] for c0 in range(8)], axis=0)


# version 10
# speedup vs baseline: 1.1432x; 1.1432x over previous
import contextlib

import numpy as np
import concourse.bass as bass
import concourse.mybir as mybir
from concourse.bass_utils import run_bass_kernel_spmd
from concourse.bass_core_rust import IndirectOffsetOnAxis

F32 = mybir.dt.float32
I32 = mybir.dt.int32
U32 = mybir.dt.uint32
AF = mybir.ActivationFunctionType
ALU = mybir.AluOpType
AX = mybir.AxisListType

D_MODEL = 1024
D_INNER = 2048
CONV_DIM = 2304
NEXP = 8
SEQ = 4096
T_TILE = 512
G_PAIR = 1024
N_PAIRS = SEQ // G_PAIR
KC = 8
TH = 256
NBLK = 24
NSLOT = NBLK * 128

ENGS = ("sync", "tensor", "vector", "scalar", "gpsimd")


class _Op:
    __slots__ = ("eng", "fn", "deps", "sdeps", "dma", "inc_val", "needs_inc",
                 "dma_cap")

    def __init__(self, eng, fn, dma):
        self.eng = eng
        self.fn = fn
        self.dma = dma
        self.deps = []
        self.sdeps = []
        self.inc_val = 0
        self.needs_inc = False
        self.dma_cap = None


def _needs_sync(d, consumer_eng, strong):
    return strong or d.dma or d.eng != consumer_eng


class Prog:
    def __init__(self):
        self.ops = {e: [] for e in ENGS}
        self.writer = {}
        self.readers = {}
        self.alias = {}
        self.last_dma = {}

    def op(self, eng, fn, reads=(), writes=(), dma=False):
        o = _Op(eng, fn, dma)
        seen = set()

        def add(lst, d):
            if d is not None and d is not o and id(d) not in seen:
                seen.add(id(d))
                lst.append(d)

        for r in reads:
            add(o.sdeps, self.writer.get(r))
            self.readers.setdefault(r, []).append(o)
        for r in writes:
            add(o.deps, self.writer.get(r))
            for rd in self.readers.get(r, ()):
                add(o.deps, rd)
            self.writer[r] = o
            self.readers[r] = []
            for ak in self.alias.get(r, ()):
                add(o.deps, self.writer.get(ak))
                for rd in self.readers.get(ak, ()):
                    add(o.deps, rd)
                self.writer[ak] = o
                self.readers[ak] = []
        o.dma_cap = dict(self.last_dma)
        if dma:
            self.last_dma[eng] = o
        self.ops[eng].append(o)
        return o

    def dma(self, eng, fn, reads=(), writes=()):
        return self.op(eng, fn, reads=reads, writes=writes, dma=True)

    def emit(self, block, sems):
        for e in ENGS:
            for o in self.ops[e]:
                if o.dma:
                    o.needs_inc = True
                for d in o.deps:
                    if _needs_sync(d, e, False):
                        d.needs_inc = True
                for d in o.sdeps:
                    d.needs_inc = True
        counters = {}
        for e in ENGS:
            for o in self.ops[e]:
                if o.needs_inc:
                    key = (e, o.dma)
                    counters[key] = counters.get(key, 0) + (16 if o.dma else 1)
                    o.inc_val = counters[key]
        highwater = {e: {} for e in ENGS}

        def emit_engine(eng_obj, e):
            hw = highwater[e]
            for o in self.ops[e]:
                needed = {}
                for strong, dl in ((False, o.deps), (True, o.sdeps)):
                    for d in dl:
                        if not _needs_sync(d, e, strong):
                            continue
                        k = (d.eng, d.dma)
                        v = d.inc_val
                        if d.dma:
                            cap_op = o.dma_cap.get(d.eng)
                            if cap_op is not None and cap_op.inc_val > v:
                                v = min(v + 16, cap_op.inc_val)
                        if needed.get(k, 0) < v:
                            needed[k] = v
                for k in sorted(needed):
                    v = needed[k]
                    if hw.get(k, 0) < v:
                        eng_obj.wait_ge(sems[k], v)
                        hw[k] = v
                inst = o.fn(eng_obj)
                if o.needs_inc:
                    inst.then_inc(sems[(e, o.dma)], 16 if o.dma else 1)

        for e in ENGS:
            if self.ops[e]:
                getattr(block, e)(lambda eng_obj, _e=e: emit_engine(eng_obj, _e))


def build(n_pairs=N_PAIRS, dbg_names=()):
    nc = bass.Bass("TRN2", target_bir_lowering=False)
    P = Prog()

    def din(name, shape, dt=F32):
        return nc.dram_tensor(name, list(shape), dt, kind="ExternalInput").ap()

    xdr = din("x", (SEQ, D_MODEL))
    w13t = din("w13t", (NEXP, 8, 128, 2 * 8 * 512))
    w2t = din("w2t", (NEXP, 8, 128, 4 * D_MODEL))
    gate8d = din("gate8", (128, 8, 8))
    inwt = din("inwt", (9, 128, 8 * 512))
    opwt = din("opwt", (16, 128, D_MODEL))
    gatewt = din("gatewt", (6, 128, 8 * 512))
    upwt = din("upwt", (6, 128, 8 * 512))
    downwt = din("downwt", (6, 128, 4 * D_MODEL))
    convw_d = din("convw", (128, 18, 4))
    convb_d = din("convb", (128, 18))
    normw_d = din("normw_bc", (128, D_MODEL))
    postw_d = din("postw_bc", (128, D_MODEL))
    ssmw_d = din("ssmw_col", (128, 16))
    dtb_d = din("dtb", (32, 1))
    alog_d = din("alog", (32, 1))
    d2_d = din("d2", (128, 16))
    ident_d = din("identc", (128, 128))
    cuml_d = din("cuml", (128, 128))
    negsl_d = din("negsl", (128, 128))
    lowm_d = din("lowm", (128, 128))
    ones_d = din("onesc", (128, 128))
    negone_d = din("negone", (128, 1))
    pairsel_d = din("pairsel", (32, 16 * 128))
    constv_d = din("constv", (128, 8))
    tokid_d = din("tokid", (128, 8))
    iota8_d = din("iota8", (128, 8))
    padf_d = din("padf", (128, 2 * NBLK))
    baseft_d = din("baseft", (128, 8))
    biota_d = din("biota", (1, NBLK))

    odr = nc.dram_tensor("out", [SEQ, D_MODEL], F32, kind="ExternalOutput").ap()
    xmir = nc.dram_tensor("xmir", [G_PAIR + 128, D_MODEL], F32).ap()
    slotmap = nc.dram_tensor("slotmap", [NSLOT, 2], F32).ap()
    slotout = nc.dram_tensor("slotout", [NSLOT, D_MODEL], F32).ap()
    slotmap3 = slotmap.rearrange("(v p) c -> p v c", p=128)
    slotout3 = slotout.rearrange("(v p) d -> p v d", p=128)

    dbg = {}
    for name, shape in dbg_names:
        dbg[name] = nc.dram_tensor("dbg_" + name, list(shape), F32,
                                   kind="ExternalOutput").ap()
    dbg_ctr = {}

    def dump(name, ap, reads):
        if name not in dbg:
            return
        i = dbg_ctr.get(name, 0)
        dbg_ctr[name] = i + 1
        d = dbg[name]
        dst = d[i] if len(d.shape) == len(ap.shape) + 1 else d
        P.dma("gpsimd", lambda g, _d=dst, _a=ap: g.dma_start(out=_d, in_=_a),
              reads=list(reads))

    sb = lambda name, shape, dt=F32: nc.alloc_sbuf_tensor(name, list(shape),
                                                          dt).ap()
    ident = sb("identS", (128, 128))
    cuml = sb("cumlS", (128, 128))
    negsl = sb("negslS", (128, 128))
    lowmask = sb("lowmS", (128, 128))
    ones = sb("onesS", (128, 128))
    negone = sb("negoneS", (128, 1))
    pairsel = sb("pairselS", (32, 16 * 128))
    normw = sb("normwS", (128, D_MODEL))
    postw = sb("postwS", (128, D_MODEL))
    ssmw = sb("ssmwS", (128, 16))
    convw = sb("convwS", (128, 18, 4))
    convb = sb("convbS", (128, 18))
    dtb = sb("dtbS", (32, 1))
    alog = sb("alogS", (32, 1))
    eA = sb("eAS", (32, 1))
    d2 = sb("d2S", (128, 16))
    gatew = sb("gatewS", (128, 8, 8))
    S = sb("Sstate", (128, 16, 128))
    tail = sb("tailS", (128, 18, 3))
    xb = sb("xbS", (128, 8, D_MODEL))
    vb = sb("vbS", (128, 4, D_MODEL))
    stat = sb("statS", (128, 16))
    rsc = sb("rscS", (128, 28))
    rsu = sb("rsuS", (128, 8), U32)
    rowstat = sb("rowstatS", (1, TH))
    tmsc = sb("tmscS", (128, D_MODEL))
    sqsc = tmsc
    dtxa = sb("dtxaS", (128, 128))
    dtxb = sb("dtxbS", (128, 128))
    tokid = sb("tokidS", (128, 8))
    iota8 = sb("iota8S", (128, 8))
    padf = sb("padfS", (128, 2 * NBLK))
    baseft = sb("baseftS", (128, 8))
    biota = sb("biotaS", (1, NBLK))
    constv = sb("constvS", (128, 8))
    for _i, _cval in enumerate([1e-5, 1e-6, 1.0, -1.0, 1.0 / 2048.0]):
        nc.const_aps.register(F32, float(_cval), constv[:, _i:_i + 1])

    arena_elems = nc.sbuf_bytes_remaining // 4 - 64
    arena = nc.alloc_sbuf_tensor("arena", [128, arena_elems], F32).ap()
    _ov_ranges = {"F": [], "A": []}

    class _OvAlloc:
        def __init__(self, mode):
            self.mode = mode
            self.off = 0

        def __call__(self, shape, keys, dt=F32):
            n = 1
            for d_ in shape[1:]:
                n *= d_
            n_pad = -(-n // 16) * 16
            assert self.off + n_pad <= arena_elems, \
                f"overlay {self.mode} overflow at {shape}"
            view = arena[0:shape[0], self.off:self.off + n]
            if dt is not F32:
                view = view.bitcast(dt)
            if len(shape) == 3:
                view = view.rearrange("p (a b) -> p a b", a=shape[1])
            _ov_ranges[self.mode].append((self.off, self.off + n_pad, keys))
            self.off += n_pad
            return view

    fal = _OvAlloc("F")
    aal = _OvAlloc("A")
    w13sb = fal((128, 2, 2 * 8 * 512), [("w13sb", 0), ("w13sb", 1)])
    w1sb = w13sb
    w2sb = fal((128, 2, 4 * D_MODEL), [("w2sb", 0), ("w2sb", 1)])
    stg = fal((128, D_MODEL), [("stg",)])
    xg = fal((128, 2, D_MODEL), [("xg", 0), ("xg", 1)])
    xgn = fal((128, D_MODEL), [("xgn",)])
    xgfm = fal((128, 2, 8 * 128), [("xgfm", 0), ("xgfm", 1)])
    hsp = fal((128, 4, 128), [("hsp", c) for c in range(4)])
    pst = fal((128, 8, 8), [("pst", t) for t in range(8)])
    sel1b = fal((128, 8, 8), [("sel1b", t) for t in range(8)])
    sel2b = fal((128, 8, 8), [("sel2b", t) for t in range(8)])
    wib = fal((128, 8, 8), [("wib", t) for t in range(8)])
    s12 = fal((128, 8, 4), [("s12", t) for t in range(8)])
    sidxi = fal((128, 8, 2), [("sidxi", t) for t in range(8)], dt=I32)
    asc = fal((128, 8), [("asc",)])
    asc2 = fal((128, 8), [("asc2",)])
    stage2 = fal((128, 2), [("stage2",)])
    slotbuf = fal((128, NBLK, 2), [("slotbuf",)])
    toki = fal((128, 2), [("toki", 0), ("toki", 1)], dt=I32)
    idxt = fal((128, 2, NBLK), [("idxt", 0), ("idxt", 1)])
    idxalli = fal((128, 8, NBLK), [("idxalli",)], dt=I32)
    crow = fal((1, 8), [("crow",)])
    r8a = fal((1, 8), [("r8a",)])
    r8b = fal((1, 8), [("r8b",)])
    r8c = fal((1, 8), [("r8c",)])
    sbrow = fal((1, 8), [("sbrow",)])
    zrow = fal((1, 8), [("zrow",)])
    ebrow = fal((1, NBLK), [("ebrow",)])
    ebi = fal((1, NBLK), [("ebi",)], dt=I32)
    eb1024 = fal((1, NBLK), [("eb1024",)])
    sbbc = fal((128, 8), [("sbbc",)])

    r1f = aal((128, KC, TH), [("r1f", c) for c in range(KC)])
    zf = aal((128, 16, TH), [("zf", j) for j in range(16)])
    ext = aal((128, 18, TH + 4), [("ext", c) for c in range(18)])
    xsy = aal((128, 16, TH), [("xsy", j) for j in range(16)])
    bcf = aal((128, 2, TH), [("bcf", 0), ("bcf", 1)])
    dtf = aal((32, T_TILE), [("dtf", 0), ("dtf", 1)])
    inw = aal((128, 2, 8 * 512), [("inw", 0), ("inw", 1)])
    opw = aal((128, 2, D_MODEL), [("opw", s) for s in range(2)])
    a_fm = aal((32, 256), [("a_fm",)])
    a_tm = aal((128, 32), [("a_tm",)])
    dt_tm = aal((128, 32), [("dt_tm",)])
    pcs_tm = aal((128, 32), [("pcs_tm",)])
    dcol = aal((128, 32), [("dcol",)])
    wdt_tm = aal((128, 32), [("wdt_tm",)])
    npcs_fm = aal((32, 128), [("npcs_fm",)])
    ecs_fm = aal((32, 128), [("ecs_fm",)])
    lam_dg = aal((32, 32), [("lam_dg",)])
    lam_rep = aal((128, 32), [("lam_rep",)])
    lam_col = aal((32, 1), [("lam_col",)])
    bct = aal((128, 128), [("bct",)])
    b_tm = aal((128, 128), [("b_tm",)])
    xs_tm = aal((128, 128), [("xs_tm",)])
    wdtx = aal((128, 128), [("wdtx",)])
    marg = aal((128, 2, 128), [("marg", 0), ("marg", 1)])
    mt_sb = marg
    gt_sb = aal((128, 2, 128), [("gt", 0), ("gt", 1)])
    esc = aal((128, 128), [("esc",)])
    t1s = aal((128, 128), [("t1s",)])
    t2s = aal((128, 128), [("t2s",)])
    zs = aal((128, 256), [("zs",)])
    rbc = aal((128, TH), [("rbc",)])
    xnfh = aal((128, 8, 256), [("xnfh", c) for c in range(8)])
    hsh = aal((128, 4, 256), [("hsh", c) for c in range(4)])
    _sw_lo = _ov_ranges["A"][-2][0]
    _f_wlimit = 16384 + 8192
    assert _sw_lo >= _f_wlimit, (_sw_lo, _f_wlimit)

    for o0, o1, keys in _ov_ranges["F"]:
        for p0, p1, keys2 in _ov_ranges["A"]:
            if o0 < p1 and p0 < o1:
                for k in keys:
                    P.alias.setdefault(k, set()).update(keys2)
                for k in keys2:
                    P.alias.setdefault(k, set()).update(keys)

    ps = [nc.alloc_psum_tensor(f"ps{i}", [128, 512], F32).ap() for i in range(8)]

    R = lambda *k: k

    def T_op(fn, reads, writes):
        P.op("tensor", fn, reads=reads, writes=writes)

    def V_op(fn, reads, writes):
        P.op("vector", fn, reads=reads, writes=writes)

    def S_op(fn, reads, writes):
        P.op("scalar", fn, reads=reads, writes=writes)

    def ld(dst, src, wr):
        P.dma("gpsimd", lambda g, _d=dst, _s=src: g.dma_start(out=_d, in_=_s),
              writes=[wr])

    ld(ident[:, :], ident_d[:, :], R("ident"))
    ld(cuml[:, :], cuml_d[:, :], R("cuml"))
    ld(negsl[:, :], negsl_d[:, :], R("negsl"))
    ld(lowmask[:, :], lowm_d[:, :], R("lowmask"))
    ld(ones[:, :], ones_d[:, :], R("ones"))
    ld(negone[:, :], negone_d[:, :], R("negone"))
    ld(pairsel[:, :], pairsel_d[:, :], R("pairsel"))
    ld(constv[:, :], constv_d[:, :], R("constv"))
    ld(normw[:, :], normw_d[:, :], R("normw"))
    ld(postw[:, :], postw_d[:, :], R("postw"))
    ld(ssmw[:, :], ssmw_d[:, :], R("ssmw"))
    ld(convw[:, :, :], convw_d[:, :, :], R("convw"))
    ld(convb[:, :], convb_d[:, :], R("convb"))
    ld(dtb[:, :], dtb_d[:, :], R("dtb"))
    ld(alog[:, :], alog_d[:, :], R("alog"))
    ld(d2[:, :], d2_d[:, :], R("d2"))
    ld(gatew[:, :, :], gate8d[:, :, :], R("gatew"))
    ld(tokid[:, :], tokid_d[:, :], R("tokid"))
    ld(iota8[:, :], iota8_d[:, :], R("iota8"))
    ld(padf[:, :], padf_d[:, :], R("padf"))
    ld(baseft[:, :], baseft_d[:, :], R("baseft"))
    ld(biota[:, :], biota_d[:, :], R("biota"))
    S_op(lambda s: s.activation(eA[:, :], alog[:, :], AF.Exp),
         reads=[R("alog")], writes=[R("eA")])
    P.op("vector", lambda v: v.memset(S[:, :, :], 0.0),
         writes=[R("S", j) for j in range(16)])
    P.op("vector", lambda v: v.memset(tail[:, :, :], 0.0),
         writes=[R("tail", c) for c in range(18)])
    P.op("vector", lambda v: v.memset(dtxa[:, :], 0.0), writes=[R("dtxa")])
    P.op("vector", lambda v: v.memset(dtxb[:, :], 0.0), writes=[R("dtxb")])
    P.op("vector", lambda v: v.memset(tmsc[:, :], 0.0), writes=[R("tmsc")])
    P.dma("gpsimd", lambda g: g.dma_start(out=xmir[G_PAIR:G_PAIR + 128, :],
                                          in_=tmsc[:, :]),
          reads=[R("tmsc")], writes=[R("xmir")])

    def transpose_to(psi, out_ap, in_ap, pin, reads, writes, copy_eng="scalar"):
        n = in_ap.shape[-1]
        T_op(lambda t, _i=in_ap, _n=n, _p=pin, _b=psi:
             t.transpose(ps[_b][0:_n, 0:_p], _i, ident[0:_p, 0:_p]),
             reads=list(reads) + [R("ident")], writes=[R("ps", psi)])
        if copy_eng == "scalar":
            S_op(lambda s, _o=out_ap, _n=n, _p=pin, _b=psi:
                 s.activation(_o, ps[_b][0:_n, 0:_p], AF.Copy),
                 reads=[R("ps", psi)], writes=writes)
        else:
            V_op(lambda v, _o=out_ap, _n=n, _p=pin, _b=psi:
                 v.tensor_copy(_o, ps[_b][0:_n, 0:_p]),
                 reads=[R("ps", psi)], writes=writes)

    def emit_norm(x_ap, out_ap, sset, reads_x, writes_out, rms=False, eps=1e-5):
        width = D_MODEL
        sm = stat[:, 4 * sset + 0:4 * sset + 1]
        qs = stat[:, 4 * sset + 1:4 * sset + 2]
        rs = stat[:, 4 * sset + 2:4 * sset + 3]
        nb = stat[:, 4 * sset + 3:4 * sset + 4]
        rS = R("stat", sset)
        if not rms:
            V_op(lambda v, _x=x_ap, _o=sm: v.tensor_reduce(_o, _x, AX.X,
                                                           ALU.add),
                 reads=list(reads_x), writes=[rS])
        S_op(lambda s, _x=x_ap, _o=qs: s.activation(sqsc[:, :], _x, AF.Square,
                                                    accum_out=_o),
             reads=list(reads_x), writes=[R("tmsc"), rS])
        if not rms:
            V_op(lambda v: v.tensor_scalar(sm, sm, 1.0 / width, None, ALU.mult),
                 reads=[rS], writes=[rS])
            V_op(lambda v: v.scalar_tensor_tensor(rs, sm, -1.0, sm, ALU.mult,
                                                  ALU.mult),
                 reads=[rS], writes=[rS])
            V_op(lambda v: v.tensor_scalar(qs, qs, 1.0 / width, None, ALU.mult),
                 reads=[rS], writes=[rS])
            V_op(lambda v: v.tensor_tensor(qs, qs, rs, ALU.add),
                 reads=[rS], writes=[rS])
        else:
            V_op(lambda v: v.tensor_scalar(qs, qs, 1.0 / width, None, ALU.mult),
                 reads=[rS], writes=[rS])
        S_op(lambda s: s.activation(rs, qs, AF.Sqrt, bias=float(eps)),
             reads=[rS, R("constv")], writes=[rS])
        V_op(lambda v: v.reciprocal(rs, rs), reads=[rS], writes=[rS])
        if not rms:
            V_op(lambda v: v.scalar_tensor_tensor(nb, sm, -1.0, rs, ALU.mult,
                                                  ALU.mult),
                 reads=[rS], writes=[rS])
            S_op(lambda s, _x=x_ap, _o=out_ap: s.activation(_o, _x, AF.Identity,
                                                            bias=nb, scale=rs),
                 reads=list(reads_x) + [rS], writes=list(writes_out))
        else:
            S_op(lambda s, _x=x_ap, _o=out_ap: s.activation(_o, _x, AF.Identity,
                                                            scale=rs),
                 reads=list(reads_x) + [rS], writes=list(writes_out))

    def tm_to_fm(src_fn, src_reads, dst, dst_tag, t0, pb0=6, nch=KC):
        for c in range(nch):
            pb = pb0 + (c % 2)
            T_op(lambda t, _s=src_fn(c), _b=pb:
                 t.transpose(ps[_b][:, 0:128], _s, ident[:, :]),
                 reads=list(src_reads) + [R("ident")], writes=[R("ps", pb)])
            S_op(lambda s, _c=c, _b=pb, _t0=t0:
                 s.activation(dst[:, _c, _t0:_t0 + 128], ps[_b][:, 0:128],
                              AF.Copy),
                 reads=[R("ps", pb)], writes=[R(dst_tag, c)])

    regcell = {}

    def emit_ffn_sparse(which):
        for g in range(8):
            P.dma("gpsimd", lambda q, _g=g: q.dma_start(
                out=xmir[128 * _g:128 * (_g + 1), :], in_=xb[:, _g, :]),
                reads=[R("xb", g)], writes=[R("xmir")])
        P.dma("gpsimd", lambda q: q.dma_start(out=slotmap3[:, :, :],
                                              in_=padf[:, :]),
              reads=[R("padf")], writes=[R("slotmap")])
        V_op(lambda v: v.memset(crow[:, :], 0.0), reads=[], writes=[R("crow")])
        for ts in range(8):
            emit_norm(xb[:, ts, :], tmsc[:, :], ts % 4, [R("xb", ts)],
                      [R("tmsc")])
            xnf8 = xgfm[:, 0, :].rearrange("p (a b) -> p a b", a=8)
            for c in range(8):
                pb2 = 6 + (c % 2)
                T_op(lambda t, _c=c, _b=pb2:
                     t.transpose(ps[_b][:, 0:128],
                                 tmsc[:, 128 * _c:128 * (_c + 1)],
                                 ident[:, :]),
                     reads=[R("tmsc"), R("ident")], writes=[R("ps", pb2)])
                S_op(lambda s, _c=c, _b=pb2, _x=xnf8:
                     s.activation(_x[:, _c, :], ps[_b][:, 0:128], AF.Copy),
                     reads=[R("ps", pb2)], writes=[R("xgfm", 0)])
            pb = 5
            for kc in range(KC):
                T_op(lambda t, _kc=kc, _b=pb, _x=xnf8:
                     t.matmul(ps[_b][:, 0:8], _x[:, _kc, :],
                              gatew[:, _kc, :], start=(_kc == 0),
                              stop=(_kc == KC - 1)),
                     reads=[R("xgfm", 0), R("gatew")], writes=[R("ps", pb)])
            lg = rsc[:, 0:8]
            vals = rsc[:, 8:16]
            idxf = rsc[:, 16:24]
            mx = rsc[:, 24:25]
            winv = rsc[:, 25:26]
            rr = R("rsc")
            V_op(lambda v: v.tensor_reduce(mx, ps[pb][:, 0:8], AX.X, ALU.max,
                                           negate=True),
                 reads=[R("ps", pb)], writes=[rr])
            S_op(lambda s: s.activation(lg, ps[pb][:, 0:8], AF.Exp, bias=mx),
                 reads=[R("ps", pb), rr], writes=[rr])
            V_op(lambda v: v.max(out=vals, in_=lg), reads=[rr], writes=[rr])
            V_op(lambda v: v.max_index(out=rsu[:, :], in_max=vals,
                                       in_values=lg),
                 reads=[rr], writes=[R("rsu")])
            V_op(lambda v: v.tensor_copy(idxf, rsu[:, :]),
                 reads=[R("rsu")], writes=[rr])
            V_op(lambda v: v.tensor_tensor(winv, rsc[:, 8:9], rsc[:, 9:10],
                                           ALU.add),
                 reads=[rr], writes=[rr])
            V_op(lambda v: v.reciprocal(winv, winv), reads=[rr], writes=[rr])
            V_op(lambda v, _t=ts: v.tensor_scalar(sel1b[:, _t, :], iota8[:, :],
                                                  rsc[:, 16:17], None,
                                                  ALU.is_equal),
                 reads=[rr, R("iota8")], writes=[R("sel1b", ts)])
            V_op(lambda v, _t=ts: v.tensor_scalar(sel2b[:, _t, :], iota8[:, :],
                                                  rsc[:, 17:18], None,
                                                  ALU.is_equal),
                 reads=[rr, R("iota8")], writes=[R("sel2b", ts)])
            V_op(lambda v, _t=ts: v.tensor_tensor(asc[:, :], sel1b[:, _t, :],
                                                  sel2b[:, _t, :], ALU.add),
                 reads=[R("sel1b", ts), R("sel2b", ts)], writes=[R("asc")])
            T_op(lambda t: t.matmul(ps[6][:, 0:8], cuml[:, :], asc[:, :],
                                    start=True, stop=False),
                 reads=[R("cuml"), R("asc")], writes=[R("ps", 6)])
            T_op(lambda t: t.matmul(ps[6][:, 0:8], ones[0:1, :], crow[:, :],
                                    start=False, stop=True),
                 reads=[R("ones"), R("crow")], writes=[R("ps", 6)])
            V_op(lambda v, _t=ts: v.tensor_copy(pst[:, _t, :], ps[6][:, 0:8]),
                 reads=[R("ps", 6)], writes=[R("pst", ts)])
            T_op(lambda t: t.matmul(ps[7][0:1, 0:8], ones[:, 0:1], asc[:, :],
                                    start=True, stop=True),
                 reads=[R("ones"), R("asc")], writes=[R("ps", 7)])
            V_op(lambda v: v.tensor_tensor(crow[:, :], crow[:, :],
                                           ps[7][0:1, 0:8], ALU.add),
                 reads=[R("ps", 7), R("crow")], writes=[R("crow")])
            V_op(lambda v: v.tensor_tensor(asc[:, :], asc[:, :], lg, ALU.mult),
                 reads=[R("asc"), rr], writes=[R("asc")])
            V_op(lambda v, _t=ts: v.tensor_scalar(wib[:, _t, :], asc[:, :],
                                                  winv, 0.5, ALU.mult,
                                                  ALU.mult),
                 reads=[R("asc"), rr], writes=[R("wib", ts)])
        V_op(lambda v: v.tensor_scalar(r8a[:, :], crow[:, :], 0.0, None,
                                       ALU.is_gt),
             reads=[R("crow")], writes=[R("r8a")])
        for j in range(1, 8):
            V_op(lambda v, _j=j: v.scalar_tensor_tensor(
                 r8a[:, :], crow[:, :], 128.0 * _j, r8a[:, :], ALU.is_gt,
                 ALU.add),
                 reads=[R("crow"), R("r8a")], writes=[R("r8a")])
        V_op(lambda v: v.memset(zrow[:, :], 0.0), reads=[], writes=[R("zrow")])
        V_op(lambda v: v.tensor_tensor_scan(r8c[:, :], r8a[:, :], zrow[:, :],
                                            0.0, ALU.add, ALU.add),
             reads=[R("r8a"), R("zrow")], writes=[R("r8c")])
        V_op(lambda v: v.tensor_tensor(r8b[:, :], r8c[:, :], r8a[:, :],
                                       ALU.subtract),
             reads=[R("r8c"), R("r8a")], writes=[R("r8b")])
        V_op(lambda v: v.tensor_scalar(sbrow[:, :], r8b[:, :], 128.0, None,
                                       ALU.mult),
             reads=[R("r8b")], writes=[R("sbrow")])
        V_op(lambda v: v.tensor_scalar(ebrow[:, :], biota[:, :],
                                       r8b[0:1, 0:1], None, ALU.is_ge),
             reads=[R("biota"), R("r8b")], writes=[R("ebrow")])
        for e in range(1, 8):
            V_op(lambda v, _e=e: v.scalar_tensor_tensor(
                 ebrow[:, :], biota[:, :], r8b[0:1, _e:_e + 1], ebrow[:, :],
                 ALU.is_ge, ALU.add),
                 reads=[R("biota"), R("r8b"), R("ebrow")], writes=[R("ebrow")])
        V_op(lambda v: v.tensor_scalar(ebrow[:, :], ebrow[:, :], -1.0, 7.0,
                                       ALU.add, ALU.min),
             reads=[R("ebrow")], writes=[R("ebrow")])
        V_op(lambda v: v.tensor_copy(ebi[:, :], ebrow[:, :]),
             reads=[R("ebrow")], writes=[R("ebi")])
        V_op(lambda v: v.tensor_scalar(eb1024[:, :], ebrow[:, :], 1024.0,
                                       None, ALU.mult),
             reads=[R("ebrow")], writes=[R("eb1024")])
        T_op(lambda t: t.matmul(ps[5][:, 0:8], ones[0:1, :], sbrow[:, :],
                                start=True, stop=True),
             reads=[R("ones"), R("sbrow")], writes=[R("ps", 5)])
        V_op(lambda v: v.tensor_copy(sbbc[:, :], ps[5][:, 0:8]),
             reads=[R("ps", 5)], writes=[R("sbbc")])
        T_op(lambda t: t.matmul(ps[6][:, 0:NBLK], ones[0:1, :],
                                eb1024[0:1, :], start=True, stop=True),
             reads=[R("ones"), R("eb1024")], writes=[R("ps", 6)])
        for ftx in range(8):
            fp = ftx % 2
            V_op(lambda v, _f=ftx, _p=fp: v.tensor_scalar(
                 idxt[:, _p, :], ps[6][:, 0:NBLK], baseft[:, _f:_f + 1],
                 None, ALU.add),
                 reads=[R("ps", 6), R("baseft")], writes=[R("idxt", fp)])
            V_op(lambda v, _f=ftx, _p=fp: v.tensor_copy(
                 idxalli[:, _f, :], idxt[:, _p, :]),
                 reads=[R("idxt", fp)], writes=[R("idxalli")])
        for ts in range(8):
            for k in range(2):
                selb = sel1b if k == 0 else sel2b
                stag = ("sel1b", ts) if k == 0 else ("sel2b", ts)
                V_op(lambda v, _t=ts: v.tensor_tensor(asc[:, :],
                                                      pst[:, _t, :],
                                                      sbbc[:, :], ALU.add),
                     reads=[R("pst", ts), R("sbbc")], writes=[R("asc")])
                V_op(lambda v, _t=ts, _s=selb: v.tensor_tensor(
                     asc2[:, :], asc[:, :], _s[:, _t, :], ALU.mult),
                     reads=[R("asc"), R(*stag)], writes=[R("asc2")])
                V_op(lambda v, _t=ts, _k=k: v.tensor_reduce(
                     s12[:, _t, _k:_k + 1], asc2[:, :], AX.X, ALU.add),
                     reads=[R("asc2")], writes=[R("s12", ts)])
                V_op(lambda v, _t=ts, _k=k: v.tensor_scalar(
                     s12[:, _t, _k:_k + 1], s12[:, _t, _k:_k + 1], -1.0, None,
                     ALU.add),
                     reads=[R("s12", ts)], writes=[R("s12", ts)])
                V_op(lambda v, _t=ts, _s=selb: v.tensor_tensor(
                     asc2[:, :], wib[:, _t, :], _s[:, _t, :], ALU.mult),
                     reads=[R("wib", ts), R(*stag)], writes=[R("asc2")])
                V_op(lambda v, _t=ts, _k=k: v.tensor_reduce(
                     s12[:, _t, 2 + _k:3 + _k], asc2[:, :], AX.X, ALU.add),
                     reads=[R("asc2")], writes=[R("s12", ts)])
            V_op(lambda v, _t=ts: v.tensor_copy(sidxi[:, _t, :],
                                                s12[:, _t, 0:2]),
                 reads=[R("s12", ts)], writes=[R("sidxi", ts)])
            for k in range(2):
                V_op(lambda v, _t=ts: v.tensor_copy(stage2[:, 0:1],
                                                    tokid[:, _t:_t + 1]),
                     reads=[R("tokid")], writes=[R("stage2")])
                V_op(lambda v, _t=ts, _k=k: v.tensor_copy(
                     stage2[:, 1:2], s12[:, _t, 2 + _k:3 + _k]),
                     reads=[R("s12", ts)], writes=[R("stage2")])
                P.dma("gpsimd", lambda q, _t=ts, _k=k: q.indirect_dma_start(
                      out=slotmap[:, :],
                      out_offset=IndirectOffsetOnAxis(sidxi[:, _t, _k:_k + 1],
                                                      0),
                      in_=stage2[:, :], in_offset=None),
                      reads=[R("stage2"), R("sidxi", ts)],
                      writes=[R("slotmap")])
        P.dma("gpsimd", lambda q: q.dma_start(out=slotbuf[:, :, :],
                                              in_=slotmap3[:, :, :]),
              reads=[R("slotmap")], writes=[R("slotbuf")])
        dump("crow", crow[:, :], [R("crow")])
        dump("sbrow", sbrow[:, :], [R("sbrow")])
        dump("ebrow", ebrow[:, :], [R("ebrow")])
        dump("s12", s12[:, :, :], [R("s12", t) for t in range(8)])
        dump("slotbuf", slotbuf[:, :, :], [R("slotbuf")])
        dump("wib", wib[:, :, :], [R("wib", t) for t in range(8)])
        w13flat = w13t.rearrange("e f p m -> (e f p) m")
        w2flat = w2t.rearrange("e f p m -> (e f p) m")

        def emit_wgather(step):
            if step >= 8 * NBLK:
                return
            sb_, sf = divmod(step, 8)
            par = step % 2
            P.dma("gpsimd", lambda q, _b=sb_, _f=sf, _p=par:
                  q.indirect_dma_start(
                  out=w13sb[:, _p, :], out_offset=None, in_=w13flat,
                  in_offset=IndirectOffsetOnAxis(idxalli[:, _f, _b:_b + 1],
                                                 0)),
                  reads=[R("idxalli")], writes=[R("w13sb", par)])
            P.dma("gpsimd", lambda q, _b=sb_, _f=sf, _p=par:
                  q.indirect_dma_start(
                  out=w2sb[:, _p, :], out_offset=None, in_=w2flat,
                  in_offset=IndirectOffsetOnAxis(idxalli[:, _f, _b:_b + 1],
                                                 0)),
                  reads=[R("idxalli")], writes=[R("w2sb", par)])

        def emit_prologue_dma(b):
            p2 = b % 2
            V_op(lambda v, _b=b, _p=p2: v.tensor_copy(toki[:, _p:_p + 1],
                                                      slotbuf[:, _b, 0:1]),
                 reads=[R("slotbuf")], writes=[R("toki", p2)])
            P.dma("gpsimd", lambda q, _p=p2: q.indirect_dma_start(
                  out=xg[:, _p, :], out_offset=None, in_=xmir[:, :],
                  in_offset=IndirectOffsetOnAxis(toki[:, _p:_p + 1], 0)),
                  reads=[R("toki", p2), R("xmir")], writes=[R("xg", p2)])

        def emit_prologue_ln(b):
            p2 = b % 2
            emit_norm(xg[:, p2, :], xgn[:, :], b % 4, [R("xg", p2)],
                      [R("xgn")])
            xgfmb = xgfm[:, p2, :].rearrange("p (a b) -> p a b", a=8)
            for c in range(8):
                pb2 = 4 + 2 * p2 + (c % 2)
                T_op(lambda t, _c=c, _b=pb2:
                     t.transpose(ps[_b][:, 0:128],
                                 xgn[:, 128 * _c:128 * (_c + 1)],
                                 ident[:, :]),
                     reads=[R("xgn"), R("ident")], writes=[R("ps", pb2)])
                S_op(lambda s, _c=c, _b=pb2, _x=xgfmb, _p=p2:
                     s.activation(_x[:, _c, :], ps[_b][:, 0:128], AF.Copy),
                     reads=[R("ps", pb2)], writes=[R("xgfm", p2)])

        def emit_compute(b):
            p2 = b % 2
            tb = 4 + 2 * p2
            xgfmb = xgfm[:, p2, :].rearrange("p (a b) -> p a b", a=8)
            for ft in range(8):
                step = 8 * b + ft
                par = step % 2
                emit_wgather(step)
                w1v = w13sb[:, par, 0:4096].rearrange("p (k m) -> p k m", k=8)
                w3v = w13sb[:, par, 4096:8192].rearrange("p (k m) -> p k m",
                                                         k=8)
                w2v = w2sb[:, par, :].rearrange("p (k m) -> p k m", k=4)
                for fc in range(4):
                    bg = 2 * (fc % 2)
                    bu = bg + 1
                    for kc in range(KC):
                        T_op(lambda t, _w=w1v, _fc=fc, _kc=kc, _b=bg,
                             _x=xgfmb, _p=par:
                             t.matmul(ps[_b][:, 0:128],
                                      _w[:, _kc, 128 * _fc:128 * (_fc + 1)],
                                      _x[:, _kc, :], start=(_kc == 0),
                                      stop=(_kc == KC - 1)),
                             reads=[R("w13sb", par), R("xgfm", p2)],
                             writes=[R("ps", bg)])
                    for kc in range(KC):
                        T_op(lambda t, _w=w3v, _fc=fc, _kc=kc, _b=bu,
                             _x=xgfmb, _p=par:
                             t.matmul(ps[_b][:, 0:128],
                                      _w[:, _kc, 128 * _fc:128 * (_fc + 1)],
                                      _x[:, _kc, :], start=(_kc == 0),
                                      stop=(_kc == KC - 1)),
                             reads=[R("w13sb", par), R("xgfm", p2)],
                             writes=[R("ps", bu)])
                    S_op(lambda s, _fc=fc, _b=bg:
                         s.activation(hsp[:, _fc, :], ps[_b][:, 0:128],
                                      AF.Silu),
                         reads=[R("ps", bg)], writes=[R("hsp", fc)])
                    V_op(lambda v, _fc=fc, _b=bu:
                         v.tensor_tensor(hsp[:, _fc, :], hsp[:, _fc, :],
                                         ps[_b][:, 0:128], ALU.mult),
                         reads=[R("ps", bu), R("hsp", fc)],
                         writes=[R("hsp", fc)])
                    for dh in range(2):
                        bd = tb + dh
                        T_op(lambda t, _w=w2v, _fc=fc, _dh=dh, _b=bd, _ft=ft,
                             _p=par:
                             t.matmul(ps[_b][:, :], hsp[:, _fc, :],
                                      _w[:, _fc, 512 * _dh:512 * (_dh + 1)],
                                      start=(_ft == 0 and _fc == 0),
                                      stop=(_ft == 7 and _fc == 3)),
                             reads=[R("hsp", fc), R("w2sb", par)],
                             writes=[R("ps", bd)])
            for dh in range(2):
                S_op(lambda s, _dh=dh, _b=tb + dh, _bb=b:
                     s.activation(stg[:, 512 * _dh:512 * (_dh + 1)],
                                  ps[_b][:, :], AF.Copy,
                                  scale=slotbuf[:, _bb, 1:2]),
                     reads=[R("ps", tb + dh), R("slotbuf")], writes=[R("stg")])
            dump("sod", stg[:, :], [R("stg")])
            P.dma("sync", lambda q, _b=b: q.dma_start(
                  out=slotout3[:, _b, :], in_=stg[:, :]),
                  reads=[R("stg")], writes=[R("slotout")])

        emit_prologue_dma(0)
        emit_prologue_dma(1)
        emit_prologue_ln(0)
        emit_prologue_ln(1)
        for b in range(NBLK):
            if b + 2 < NBLK:
                emit_prologue_dma(b + 2)
            emit_compute(b)
            if b + 2 < NBLK:
                emit_prologue_ln(b + 2)
        for ts in range(8):
            for k in range(2):
                P.dma("gpsimd", lambda q, _t=ts, _k=k: q.indirect_dma_start(
                      out=xg[:, _k, :], out_offset=None, in_=slotout[:, :],
                      in_offset=IndirectOffsetOnAxis(
                          sidxi[:, _t, _k:_k + 1], 0)),
                      reads=[R("sidxi", ts), R("slotout")],
                      writes=[R("xg", k)])
                V_op(lambda v, _t=ts, _k=k: v.tensor_tensor(
                     xb[:, _t, :], xb[:, _t, :], xg[:, _k, :], ALU.add),
                     reads=[R("xg", k), R("xb", ts)], writes=[R("xb", ts)])

    def emit_swiglu(gb):
        fts = [512] * 5 + [192]
        for hh in range(2):
            for i, tsu in enumerate((2 * hh, 2 * hh + 1)):
                emit_norm(vb[:, tsu, :], tmsc[:, :], tsu, [R("vb", tsu)],
                          [R("tmsc")], rms=True, eps=1e-6)
                V_op(lambda v: v.tensor_tensor(tmsc[:, :], tmsc[:, :],
                                               postw[:, :], ALU.mult),
                     reads=[R("tmsc"), R("postw")], writes=[R("tmsc")])
                tm_to_fm(lambda c: tmsc[:, 128 * c:128 * (c + 1)],
                         [R("tmsc")], xnfh, "xnfh", 128 * i, pb0=2)
            for ft in range(6):
                par = ft % 2
                P.dma("sync", lambda q, _s=gatewt[ft], _p=par: q.dma_start(
                      out=w13sb[:, _p, 0:4096], in_=_s),
                      writes=[R("w13sb", par)])
                P.dma("sync", lambda q, _s=upwt[ft], _p=par: q.dma_start(
                      out=w13sb[:, _p, 4096:8192], in_=_s),
                      writes=[R("w13sb", par)])
                P.dma("sync", lambda q, _s=downwt[ft], _p=par: q.dma_start(
                      out=w2sb[:, _p, :], in_=_s), writes=[R("w2sb", par)])
                w1v = w13sb[:, par, 0:4096].rearrange("p (k m) -> p k m", k=8)
                w3v = w13sb[:, par, 4096:8192].rearrange("p (k m) -> p k m",
                                                         k=8)
                w2v = w2sb[:, par, :].rearrange("p (k m) -> p k m", k=4)
                fsz = fts[ft]
                nfc = (fsz + 127) // 128
                for fc in range(nfc):
                    m = min(128, fsz - 128 * fc)
                    bg = 2 * (fc % 2)
                    bu = bg + 1
                    for kc in range(KC):
                        T_op(lambda t, _w=w1v, _fc=fc, _kc=kc, _m=m, _b=bg:
                             t.matmul(ps[_b][0:_m, 0:256],
                                      _w[:, _kc, 128 * _fc:128 * _fc + _m],
                                      xnfh[:, _kc, :], start=(_kc == 0),
                                      stop=(_kc == KC - 1)),
                             reads=[R("w13sb", par), R("xnfh", kc)],
                             writes=[R("ps", bg)])
                    for kc in range(KC):
                        T_op(lambda t, _w=w3v, _fc=fc, _kc=kc, _m=m, _b=bu:
                             t.matmul(ps[_b][0:_m, 0:256],
                                      _w[:, _kc, 128 * _fc:128 * _fc + _m],
                                      xnfh[:, _kc, :], start=(_kc == 0),
                                      stop=(_kc == KC - 1)),
                             reads=[R("w13sb", par), R("xnfh", kc)],
                             writes=[R("ps", bu)])
                    S_op(lambda s, _fc=fc, _m=m, _b=bg:
                         s.activation(hsh[0:_m, _fc, :], ps[_b][0:_m, 0:256],
                                      AF.Silu),
                         reads=[R("ps", bg)], writes=[R("hsh", fc)])
                    V_op(lambda v, _fc=fc, _m=m, _b=bu:
                         v.tensor_tensor(hsh[0:_m, _fc, :], hsh[0:_m, _fc, :],
                                         ps[_b][0:_m, 0:256], ALU.mult),
                         reads=[R("ps", bu), R("hsh", fc)],
                         writes=[R("hsh", fc)])
                    for tsl in range(2):
                        for dh in range(2):
                            bd = 4 + 2 * tsl + dh
                            T_op(lambda t, _w=w2v, _fc=fc, _dh=dh, _b=bd,
                                 _t=tsl, _m=m, _ft=ft, _n=nfc:
                                 t.matmul(ps[_b][:, :],
                                          hsh[0:_m, _fc,
                                              128 * _t:128 * (_t + 1)],
                                          _w[0:_m, _fc,
                                             512 * _dh:512 * (_dh + 1)],
                                          start=(_ft == 0 and _fc == 0),
                                          stop=(_ft == 5 and _fc == _n - 1)),
                                 reads=[R("hsh", fc), R("w2sb", par)],
                                 writes=[R("ps", bd)])
            for tsl in range(2):
                tsu = 2 * hh + tsl
                for dh in range(2):
                    bd = 4 + 2 * tsl + dh
                    xa = xb[:, gb + tsu, 512 * dh:512 * (dh + 1)]
                    V_op(lambda v, _b=bd, _x=xa, _g=gb + tsu:
                         v.tensor_tensor(_x, _x, ps[_b][:, :], ALU.add),
                         reads=[R("ps", bd), R("xb", gb + tsu)],
                         writes=[R("xb", gb + tsu)])
        for tsu in range(4):
            V_op(lambda v, _t=tsu, _g=gb + tsu:
                 v.tensor_tensor(xb[:, _g, :], xb[:, _g, :], vb[:, _t, :],
                                 ALU.add),
                 reads=[R("xb", gb + tsu), R("vb", tsu)],
                 writes=[R("xb", gb + tsu)])

    def emit_attn(gb):
        for tsu in range(4):
            emit_norm(xb[:, gb + tsu, :], vb[:, tsu, :], tsu,
                      [R("xb", gb + tsu)], [R("vb", tsu)])
        dump("u", vb[:, :, :], [R("vb", t) for t in range(4)])
        for half in range(2):
            for i, tsu in enumerate((2 * half, 2 * half + 1)):
                emit_norm(vb[:, tsu, :], tmsc[:, :], tsu, [R("vb", tsu)],
                          [R("tmsc")], rms=True, eps=1e-6)
                V_op(lambda v: v.tensor_tensor(tmsc[:, :], tmsc[:, :],
                                               normw[:, :], ALU.mult),
                     reads=[R("tmsc"), R("normw")], writes=[R("tmsc")])
                tm_to_fm(lambda c: tmsc[:, 128 * c:128 * (c + 1)], [R("tmsc")],
                         r1f, "r1f", 128 * i)
            for mt in range(9):
                par = mt % 2
                wr = R("inw", par)
                P.dma("sync", lambda g, _d=inw[:, par, :], _s=inwt[mt]:
                      g.dma_start(out=_d, in_=_s), writes=[wr])
                inwv = inw[:, par, :].rearrange("p (k m) -> p k m", k=8)
                msizes = [128] * 4 if mt < 8 else [128, 128, 32]
                for fs, msz in enumerate(msizes):
                    m0 = 512 * mt + 128 * fs
                    pb = fs % 2
                    for kc in range(KC):
                        T_op(lambda t, _w=inwv, _fs=fs, _kc=kc, _m=msz, _b=pb:
                             t.matmul(ps[_b][0:_m, 0:TH],
                                      _w[:, _kc, 128 * _fs:128 * _fs + _m],
                                      r1f[:, _kc, :], start=(_kc == 0),
                                      stop=(_kc == KC - 1)),
                             reads=[wr, R("r1f", kc)], writes=[R("ps", pb)])
                    if m0 < D_INNER:
                        S_op(lambda s, _c=m0 // 128, _b=pb:
                             s.activation(zf[:, _c, :], ps[_b][:, 0:TH],
                                          AF.Copy),
                             reads=[R("ps", pb)], writes=[R("zf", m0 // 128)])
                    elif m0 < D_INNER + CONV_DIM:
                        c = (m0 - D_INNER) // 128
                        S_op(lambda s, _c=c, _b=pb:
                             s.activation(ext[:, _c, 3:3 + TH],
                                          ps[_b][:, 0:TH], AF.Copy),
                             reads=[R("ps", pb)], writes=[R("ext", c)])
                    else:
                        S_op(lambda s, _b=pb:
                             s.activation(a_fm[:, 0:TH], ps[_b][0:32, 0:TH],
                                          AF.Exp, bias=dtb[:, 0:1]),
                             reads=[R("ps", pb), R("dtb")], writes=[R("a_fm")])
                        S_op(lambda s, _h=half:
                             s.activation(dtf[:, TH * _h:TH * (_h + 1)],
                                          a_fm[:, 0:TH], AF.Ln, bias=1.0),
                             reads=[R("a_fm"), R("constv")],
                             writes=[R("dtf", half)])
            for c in range(18):
                V_op(lambda v, _c=c: v.tensor_copy(ext[:, _c, 0:3],
                                                   tail[:, _c, :]),
                     reads=[R("tail", c)], writes=[R("ext", c)])
                V_op(lambda v, _c=c: v.tensor_copy(tail[:, _c, :],
                                                   ext[:, _c, TH:TH + 3]),
                     reads=[R("ext", c)], writes=[R("tail", c)])
                V_op(lambda v, _c=c: v.tensor_scalar(zs[:, 0:TH],
                                                     ext[:, _c, 0:TH],
                                                     convw[:, _c, 0:1], None,
                                                     ALU.mult),
                     reads=[R("ext", c), R("convw")], writes=[R("zs")])
                for k in range(1, 4):
                    V_op(lambda v, _c=c, _k=k:
                         v.scalar_tensor_tensor(zs[:, 0:TH],
                                                ext[:, _c, _k:_k + TH],
                                                convw[:, _c, _k:_k + 1],
                                                zs[:, 0:TH], ALU.mult,
                                                ALU.add),
                         reads=[R("ext", c), R("convw"), R("zs")],
                         writes=[R("zs")])
                if c < 16:
                    dst, wt = xsy[:, c, :], R("xsy", c)
                else:
                    dst, wt = bcf[:, c - 16, :], R("bcf", c - 16)
                S_op(lambda s, _d=dst, _c=c:
                     s.activation(_d, zs[:, 0:TH], AF.Silu,
                                  bias=convb[:, _c:_c + 1]),
                     reads=[R("zs"), R("convb")], writes=[wt])
            for lc in range(2):
                tc0 = 128 * lc
                dts = dtf[:, TH * half + tc0:TH * half + tc0 + 128]
                rdt = R("dtf", half)
                V_op(lambda v, _d=dts: v.tensor_scalar(a_fm[:, 0:128], _d,
                                                       eA[:, 0:1], None,
                                                       ALU.mult),
                     reads=[rdt, R("eA")], writes=[R("a_fm")])
                transpose_to(7, a_tm[:, :], a_fm[0:32, 0:128], 32, [R("a_fm")],
                             [R("a_tm")], copy_eng="vector")
                transpose_to(7, dt_tm[:, :], dts, 32, [rdt], [R("dt_tm")],
                             copy_eng="vector")
                T_op(lambda t: t.matmul(ps[6][:, 0:32], cuml[:, :], a_tm[:, :],
                                        start=True, stop=True),
                     reads=[R("cuml"), R("a_tm")], writes=[R("ps", 6)])
                V_op(lambda v: v.tensor_copy(pcs_tm[:, :], ps[6][:, 0:32]),
                     reads=[R("ps", 6)], writes=[R("pcs_tm")])
                T_op(lambda t: t.matmul(ps[6][:, 0:32], negsl[:, :],
                                        a_tm[:, :], start=True, stop=True),
                     reads=[R("negsl"), R("a_tm")], writes=[R("ps", 6)])
                S_op(lambda s: s.activation(dcol[:, :], ps[6][:, 0:32],
                                            AF.Exp),
                     reads=[R("ps", 6)], writes=[R("dcol")])
                V_op(lambda v: v.tensor_tensor(wdt_tm[:, :], dt_tm[:, :],
                                               dcol[:, :], ALU.mult),
                     reads=[R("dt_tm"), R("dcol")], writes=[R("wdt_tm")])
                T_op(lambda t: t.transpose(ps[6][0:32, 0:128], pcs_tm[:, :],
                                           ident[:, :]),
                     reads=[R("pcs_tm"), R("ident")], writes=[R("ps", 6)])
                S_op(lambda s: s.activation(npcs_fm[:, :], ps[6][0:32, 0:128],
                                            AF.Identity, scale=-1.0),
                     reads=[R("ps", 6), R("constv")], writes=[R("npcs_fm")])
                S_op(lambda s: s.activation(ecs_fm[:, :], npcs_fm[:, :],
                                            AF.Exp),
                     reads=[R("npcs_fm")], writes=[R("ecs_fm")])
                T_op(lambda t: t.matmul(ps[6][0:32, 0:1], a_tm[:, :],
                                        negone[:, :], start=True, stop=True),
                     reads=[R("a_tm"), R("negone")], writes=[R("ps", 6)])
                S_op(lambda s: s.activation(lam_col[:, :], ps[6][0:32, 0:1],
                                            AF.Exp),
                     reads=[R("ps", 6)], writes=[R("lam_col")])
                V_op(lambda v: v.tensor_scalar(lam_dg[:, :], ident[0:32, 0:32],
                                               lam_col[:, 0:1], None,
                                               ALU.mult),
                     reads=[R("ident"), R("lam_col")], writes=[R("lam_dg")])
                T_op(lambda t: t.matmul(ps[6][:, 0:32], ones[0:32, :],
                                        lam_dg[:, :], start=True, stop=True),
                     reads=[R("ones"), R("lam_dg")], writes=[R("ps", 6)])
                V_op(lambda v: v.tensor_copy(lam_rep[:, :], ps[6][:, 0:32]),
                     reads=[R("ps", 6)], writes=[R("lam_rep")])
                Bc = bcf[:, 0, tc0:tc0 + 128]
                Cc = bcf[:, 1, tc0:tc0 + 128]
                T_op(lambda t, _B=Bc, _C=Cc: t.matmul(ps[6][:, 0:128], _B, _C,
                                                      start=True, stop=True),
                     reads=[R("bcf", 0), R("bcf", 1)], writes=[R("ps", 6)])
                S_op(lambda s: s.activation(bct[:, :], ps[6][:, 0:128],
                                            AF.Copy),
                     reads=[R("ps", 6)], writes=[R("bct")])
                transpose_to(6, b_tm[:, :], Bc, 128, [R("bcf", 0)],
                             [R("b_tm")])
                for j in range(16):
                    xs_ap = xsy[:, j, tc0:tc0 + 128]
                    rxs = R("xsy", j)
                    transpose_to(0, xs_tm[:, :], xs_ap, 128, [rxs],
                                 [R("xs_tm")])
                    V_op(lambda v, _j=j: v.tensor_scalar(
                         dtxa[:, 0:64], xs_tm[:, 0:64],
                         dt_tm[:, 2 * _j:2 * _j + 1], None, ALU.mult),
                         reads=[R("xs_tm"), R("dt_tm")], writes=[R("dtxa")])
                    V_op(lambda v, _j=j: v.tensor_scalar(
                         dtxb[:, 64:128], xs_tm[:, 64:128],
                         dt_tm[:, 2 * _j + 1:2 * _j + 2], None, ALU.mult),
                         reads=[R("xs_tm"), R("dt_tm")], writes=[R("dtxb")])
                    V_op(lambda v, _j=j: v.tensor_scalar(
                         wdtx[:, 0:64], xs_tm[:, 0:64],
                         wdt_tm[:, 2 * _j:2 * _j + 1], None, ALU.mult),
                         reads=[R("xs_tm"), R("wdt_tm")], writes=[R("wdtx")])
                    V_op(lambda v, _j=j: v.tensor_scalar(
                         wdtx[:, 64:128], xs_tm[:, 64:128],
                         wdt_tm[:, 2 * _j + 1:2 * _j + 2], None, ALU.mult),
                         reads=[R("xs_tm"), R("wdt_tm")], writes=[R("wdtx")])
                    for hh in range(2):
                        h = 2 * j + hh
                        pb = 1 + hh
                        T_op(lambda t, _h=h, _b=pb:
                             t.matmul(ps[_b][:, 0:128],
                                      ident[0:32, _h:_h + 1]
                                      .broadcast_to((32, 128)),
                                      npcs_fm[:, :], start=True, stop=True),
                             reads=[R("ident"), R("npcs_fm")],
                             writes=[R("ps", pb)])
                        V_op(lambda v, _hh=hh, _b=pb:
                             v.tensor_tensor(marg[:, _hh, :],
                                             ps[_b][:, 0:128],
                                             lowmask[:, :], ALU.add),
                             reads=[R("ps", pb), R("lowmask")],
                             writes=[R("marg", hh)])
                        S_op(lambda s, _hh=hh, _h=h:
                             s.activation(mt_sb[:, _hh, :], marg[:, _hh, :],
                                          AF.Exp, bias=pcs_tm[:, _h:_h + 1]),
                             reads=[R("marg", hh), R("pcs_tm")],
                             writes=[R("mt", hh)])
                        V_op(lambda v, _hh=hh:
                             v.tensor_tensor(gt_sb[:, _hh, :],
                                             mt_sb[:, _hh, :], bct[:, :],
                                             ALU.mult),
                             reads=[R("mt", hh), R("bct")],
                             writes=[R("gt", hh)])
                    T_op(lambda t: t.matmul(ps[3][:, 0:128], dtxa[:, :],
                                            gt_sb[:, 0, :], start=True,
                                            stop=False),
                         reads=[R("dtxa"), R("gt", 0)], writes=[R("ps", 3)])
                    T_op(lambda t: t.matmul(ps[3][:, 0:128], dtxb[:, :],
                                            gt_sb[:, 1, :], start=False,
                                            stop=True),
                         reads=[R("dtxb"), R("gt", 1)], writes=[R("ps", 3)])
                    T_op(lambda t: t.matmul(ps[4][:, 0:128], b_tm[:, :],
                                            wdtx[:, :], start=True,
                                            stop=True),
                         reads=[R("b_tm"), R("wdtx")], writes=[R("ps", 4)])
                    T_op(lambda t, _j=j, _C=Cc:
                         t.matmul(ps[5][:, 0:128], S[:, _j, :], _C,
                                  start=True, stop=True),
                         reads=[R("S", j), R("bcf", 1)], writes=[R("ps", 5)])
                    T_op(lambda t, _j=j:
                         t.matmul(ps[6][:, 0:128],
                                  pairsel[:, 128 * _j:128 * (_j + 1)],
                                  ecs_fm[:, :], start=True, stop=True),
                         reads=[R("pairsel"), R("ecs_fm")],
                         writes=[R("ps", 6)])
                    S_op(lambda s: s.activation(esc[:, :], ps[6][:, 0:128],
                                                AF.Copy),
                         reads=[R("ps", 6)], writes=[R("esc")])
                    V_op(lambda v: v.tensor_tensor(t1s[:, :], esc[:, :],
                                                   ps[5][:, 0:128], ALU.mult),
                         reads=[R("esc"), R("ps", 5)], writes=[R("t1s")])
                    V_op(lambda v, _j=j, _x=xs_ap:
                         v.scalar_tensor_tensor(t2s[:, :], _x,
                                                d2[:, _j:_j + 1], t1s[:, :],
                                                ALU.mult, ALU.add),
                         reads=[rxs, R("d2"), R("t1s")], writes=[R("t2s")])
                    V_op(lambda v, _x=xs_ap: v.tensor_tensor(_x, t2s[:, :],
                                                             ps[3][:, 0:128],
                                                             ALU.add),
                         reads=[R("t2s"), R("ps", 3)], writes=[rxs])
                    for hh in range(2):
                        sl = slice(64 * hh, 64 * (hh + 1))
                        V_op(lambda v, _j=j, _sl=sl, _h=2 * j + hh:
                             v.scalar_tensor_tensor(S[:, _j, _sl],
                                                    S[:, _j, _sl],
                                                    lam_rep[:, _h:_h + 1],
                                                    ps[4][:, _sl], ALU.mult,
                                                    ALU.add),
                             reads=[R("S", j), R("lam_rep"), R("ps", 4)],
                             writes=[R("S", j)])
            for j in range(16):
                S_op(lambda s, _j=j: s.activation(zs[:, 0:TH], zf[:, _j, :],
                                                  AF.Silu),
                     reads=[R("zf", j)], writes=[R("zs")])
                V_op(lambda v, _j=j: v.tensor_tensor(xsy[:, _j, :],
                                                     xsy[:, _j, :],
                                                     zs[:, 0:TH], ALU.mult),
                     reads=[R("xsy", j), R("zs")], writes=[R("xsy", j)])
            for j in range(16):
                S_op(lambda s, _j=j: s.activation(zs[:, 0:TH], xsy[:, _j, :],
                                                  AF.Square),
                     reads=[R("xsy", j)], writes=[R("zs")])
                T_op(lambda t, _j=j: t.matmul(ps[7][0:1, 0:TH], ones[:, 0:1],
                                              zs[:, 0:TH], start=(_j == 0),
                                              stop=(_j == 15)),
                     reads=[R("zs"), R("ones")], writes=[R("ps", 7)])
            S_op(lambda s: s.activation(rowstat[0:1, :], ps[7][0:1, 0:TH],
                                        AF.Sqrt, scale=1.0 / D_INNER,
                                        bias=1e-6),
                 reads=[R("ps", 7), R("constv")], writes=[R("rowstat")])
            V_op(lambda v: v.reciprocal(rowstat[0:1, :], rowstat[0:1, :]),
                 reads=[R("rowstat")], writes=[R("rowstat")])
            T_op(lambda t: t.matmul(ps[7][:, 0:TH], ones[0:1, :],
                                    rowstat[0:1, :], start=True, stop=True),
                 reads=[R("rowstat"), R("ones")], writes=[R("ps", 7)])
            S_op(lambda s: s.activation(rbc[:, :], ps[7][:, 0:TH], AF.Copy),
                 reads=[R("ps", 7)], writes=[R("rbc")])
            for j in range(16):
                V_op(lambda v, _j=j:
                     v.scalar_tensor_tensor(xsy[:, _j, :], xsy[:, _j, :],
                                            ssmw[:, _j:_j + 1], rbc[:, :],
                                            ALU.mult, ALU.mult),
                     reads=[R("xsy", j), R("ssmw"), R("rbc")],
                     writes=[R("xsy", j)])
            for slot in range(2):
                P.dma("sync", lambda g, _d=opw[:, slot, :], _s=opwt[slot]:
                      g.dma_start(out=_d, in_=_s), writes=[R("opw", slot)])
            for j in range(16):
                for tsl in range(2):
                    for dh in range(2):
                        bd = 4 + 2 * tsl + dh
                        T_op(lambda t, _j=j, _b=bd, _t=tsl, _dh=dh:
                             t.matmul(ps[_b][:, :],
                                      xsy[:, _j, 128 * _t:128 * (_t + 1)],
                                      opw[:, _j % 2,
                                          512 * _dh:512 * (_dh + 1)],
                                      start=(_j == 0), stop=(_j == 15)),
                             reads=[R("xsy", j), R("opw", j % 2)],
                             writes=[R("ps", bd)])
                if j + 2 <= 15:
                    P.dma("sync",
                          lambda g, _d=opw[:, j % 2, :], _s=opwt[j + 2]:
                          g.dma_start(out=_d, in_=_s),
                          writes=[R("opw", j % 2)])
            for tsl in range(2):
                tsu = 2 * half + tsl
                for dh in range(2):
                    bd = 4 + 2 * tsl + dh
                    V_op(lambda v, _b=bd, _t=tsu, _dh=dh:
                         v.tensor_tensor(vb[:, _t,
                                            512 * _dh:512 * (_dh + 1)],
                                         vb[:, _t,
                                            512 * _dh:512 * (_dh + 1)],
                                         ps[_b][:, :], ALU.add),
                         reads=[R("ps", bd), R("vb", tsu)],
                         writes=[R("vb", tsu)])
        dump("v", vb[:, :, :], [R("vb", t) for t in range(4)])
        emit_swiglu(gb)

    store_regions = []
    for pair in range(n_pairs):
        for g in range(8):
            P.dma("gpsimd",
                  lambda q, _p=pair, _g=g: q.dma_start(
                      out=xb[:, _g, :],
                      in_=xdr[G_PAIR * _p + 128 * _g:
                              G_PAIR * _p + 128 * (_g + 1), :]),
                  writes=[R("xb", g)])
        emit_ffn_sparse(0)
        dump("xffn1", xb[:, :, :], [R("xb", g) for g in range(8)])
        emit_attn(0)
        emit_attn(4)
        dump("xattn", xb[:, :, :], [R("xb", g) for g in range(8)])
        emit_ffn_sparse(1)
        for g in range(8):
            sr = R("stored", pair, g)
            store_regions.append(sr)
            P.dma("gpsimd",
                  lambda q, _p=pair, _g=g: q.dma_start(
                      out=odr[G_PAIR * _p + 128 * _g:
                              G_PAIR * _p + 128 * (_g + 1), :],
                      in_=xb[:, _g, :]),
                  reads=[R("xb", g)], writes=[sr])
    P.op("gpsimd", lambda g: g.nop(), reads=store_regions)

    with contextlib.ExitStack() as ctx:
        sems = {}
        for e in ENGS:
            sems[(e, False)] = ctx.enter_context(nc.semaphore(f"sc_{e}"))
            sems[(e, True)] = ctx.enter_context(nc.semaphore(f"sq_{e}"))
        block = ctx.enter_context(nc.Block())
        P.emit(block, sems)
    return nc



def _host_layouts(inputs):
    f = np.float32
    c = np.ascontiguousarray
    w1 = np.asarray(inputs["L0_ffn_w1"], f)
    w3 = np.asarray(inputs["L0_ffn_w3"], f)
    w2 = np.asarray(inputs["L0_ffn_w2"], f)
    hw = {}
    w1l = w1.reshape(8, 8, 128, 8, 512).transpose(0, 3, 2, 1, 4) \
        .reshape(8, 8, 128, 8 * 512)
    w3l = w3.reshape(8, 8, 128, 8, 512).transpose(0, 3, 2, 1, 4) \
        .reshape(8, 8, 128, 8 * 512)
    hw["w13t"] = c(np.concatenate([w1l, w3l], axis=-1))
    hw["w2t"] = c(w2.reshape(8, 8, 4, 128, 1024).transpose(0, 1, 3, 2, 4)
                  .reshape(8, 8, 128, 4 * 1024))
    gw = np.asarray(inputs["L0_ffn_gate_w"], f)
    hw["gate8"] = c(gw.reshape(8, 128, 8).transpose(1, 0, 2))
    ipw = np.asarray(inputs["L0_attn_in_proj_w"], f)
    t = np.zeros((1024, 4608), f)
    t[:, :4384] = ipw.T
    hw["inwt"] = c(t.reshape(8, 128, 9, 512).transpose(2, 1, 0, 3)
                   .reshape(9, 128, 8 * 512))
    opw = np.asarray(inputs["L0_attn_out_proj_w"], f)
    hw["opwt"] = c(opw.T.reshape(16, 128, 1024))
    gw2 = np.asarray(inputs["L0_attn_gate_w"], f)
    uw = np.asarray(inputs["L0_attn_up_w"], f)
    dw = np.asarray(inputs["L0_attn_down_w"], f)

    def updown_lhsT(w):
        t = np.zeros((1024, 3072), f)
        t[:, :2752] = w.T
        return c(t.reshape(8, 128, 6, 512).transpose(2, 1, 0, 3)
                 .reshape(6, 128, 8 * 512))

    hw["gatewt"] = updown_lhsT(gw2)
    hw["upwt"] = updown_lhsT(uw)
    t = np.zeros((3072, 1024), f)
    t[:2752, :] = dw.T
    hw["downwt"] = c(t.reshape(6, 4, 128, 1024).transpose(0, 2, 1, 3)
                     .reshape(6, 128, 4 * 1024))
    cw = np.asarray(inputs["L0_attn_conv_w"], f)
    hw["convw"] = c(cw.reshape(18, 128, 4).transpose(1, 0, 2))
    cb = np.asarray(inputs["L0_attn_conv_b"], f)
    hw["convb"] = c(cb.reshape(18, 128).T)
    hw["normw_bc"] = c(np.broadcast_to(
        np.asarray(inputs["L0_attn_norm_w"], f), (128, 1024)))
    hw["postw_bc"] = c(np.broadcast_to(
        np.asarray(inputs["L0_attn_post_norm_w"], f), (128, 1024)))
    hw["ssmw_col"] = c(np.asarray(inputs["L0_attn_ssm_norm_w"], f)
                       .reshape(16, 128).T)
    hw["dtb"] = c(np.asarray(inputs["L0_attn_dt_bias"], f).reshape(32, 1))
    hw["alog"] = c(np.asarray(inputs["L0_attn_A_log"], f).reshape(32, 1))
    Dv = np.asarray(inputs["L0_attn_D"], f)
    p = np.arange(128)[:, None] // 64
    j = np.arange(16)[None, :]
    hw["d2"] = c(Dv[2 * j + p].astype(f))
    hw["constv"] = c(np.tile(np.array(
        [1e-5, 1e-6, 1.0, -1.0, 1.0 / 2048.0, 0.0, 0.0, 0.0], f), (128, 1)))
    hw["identc"] = np.eye(128, dtype=f)
    si = np.arange(128)[:, None]
    ti = np.arange(128)[None, :]
    hw["cuml"] = (si <= ti).astype(f)
    hw["negsl"] = -((si > ti).astype(f))
    hw["lowm"] = np.where(si <= ti, 0.0, -1e30).astype(f)
    hw["onesc"] = np.ones((128, 128), f)
    hw["negone"] = np.full((128, 1), -1.0, f)
    hhh = np.arange(32)[:, None]
    jj = np.arange(16 * 128)[None, :]
    hw["pairsel"] = (hhh == (2 * (jj // 128) + (jj % 128) // 64)).astype(f)
    pp = np.arange(128)[:, None]
    tt = np.arange(8)[None, :]
    hw["tokid"] = c((128 * tt + pp).astype(f))
    hw["iota8"] = c(np.tile(np.arange(8, dtype=f), (128, 1)))
    hw["padf"] = c(np.tile(np.array([float(G_PAIR), 0.0], f), (128, NBLK)))
    hw["biota"] = c(np.arange(NBLK, dtype=f).reshape(1, NBLK))
    hw["baseft"] = c((128 * tt + pp).astype(f))
    return hw


_NC_CACHE = {}


def kernel(**inputs):
    x = np.ascontiguousarray(np.asarray(inputs["L0_attn_hidden_states"],
                                        np.float32))
    hw = _host_layouts(inputs)
    if "nc" not in _NC_CACHE:
        _NC_CACHE["nc"] = build()
    nc = _NC_CACHE["nc"]
    in_maps = [dict(hw, x=np.ascontiguousarray(x[c0])) for c0 in range(8)]
    res = run_bass_kernel_spmd(nc, in_maps, core_ids=list(range(8)))
    return np.stack([res.results[c0]["out"] for c0 in range(8)], axis=0)


# version 11 (final)
# speedup vs baseline: 1.1963x; 1.0464x over previous
import contextlib

import numpy as np
import concourse.bass as bass
import concourse.mybir as mybir
from concourse.bass_utils import run_bass_kernel_spmd
from concourse.bass_core_rust import IndirectOffsetOnAxis

F32 = mybir.dt.float32
I32 = mybir.dt.int32
U32 = mybir.dt.uint32
AF = mybir.ActivationFunctionType
ALU = mybir.AluOpType
AX = mybir.AxisListType

D_MODEL = 1024
D_INNER = 2048
CONV_DIM = 2304
NEXP = 8
SEQ = 4096
T_TILE = 512
G_PAIR = 1024
N_PAIRS = SEQ // G_PAIR
KC = 8
TH = 256
NBLK = 24
NSLOT = NBLK * 128

ENGS = ("sync", "tensor", "vector", "scalar", "gpsimd")


class _Op:
    __slots__ = ("eng", "fn", "deps", "sdeps", "dma", "inc_val", "needs_inc",
                 "dma_cap")

    def __init__(self, eng, fn, dma):
        self.eng = eng
        self.fn = fn
        self.dma = dma
        self.deps = []
        self.sdeps = []
        self.inc_val = 0
        self.needs_inc = False
        self.dma_cap = None


def _needs_sync(d, consumer_eng, strong):
    return strong or d.dma or d.eng != consumer_eng


class Prog:
    def __init__(self):
        self.ops = {e: [] for e in ENGS}
        self.writer = {}
        self.readers = {}
        self.alias = {}
        self.last_dma = {}

    def op(self, eng, fn, reads=(), writes=(), dma=False):
        o = _Op(eng, fn, dma)
        seen = set()

        def add(lst, d):
            if d is not None and d is not o and id(d) not in seen:
                seen.add(id(d))
                lst.append(d)

        for r in reads:
            add(o.sdeps, self.writer.get(r))
            self.readers.setdefault(r, []).append(o)
        for r in writes:
            add(o.deps, self.writer.get(r))
            for rd in self.readers.get(r, ()):
                add(o.deps, rd)
            self.writer[r] = o
            self.readers[r] = []
            for ak in self.alias.get(r, ()):
                add(o.deps, self.writer.get(ak))
                for rd in self.readers.get(ak, ()):
                    add(o.deps, rd)
                self.writer[ak] = o
                self.readers[ak] = []
        o.dma_cap = dict(self.last_dma)
        if dma:
            self.last_dma[eng] = o
        self.ops[eng].append(o)
        return o

    def dma(self, eng, fn, reads=(), writes=()):
        return self.op(eng, fn, reads=reads, writes=writes, dma=True)

    def emit(self, block, sems):
        for e in ENGS:
            for o in self.ops[e]:
                if o.dma:
                    o.needs_inc = True
                for d in o.deps:
                    if _needs_sync(d, e, False):
                        d.needs_inc = True
                for d in o.sdeps:
                    d.needs_inc = True
        counters = {}
        for e in ENGS:
            for o in self.ops[e]:
                if o.needs_inc:
                    key = (e, o.dma)
                    counters[key] = counters.get(key, 0) + (16 if o.dma else 1)
                    o.inc_val = counters[key]
        highwater = {e: {} for e in ENGS}

        def emit_engine(eng_obj, e):
            hw = highwater[e]
            for o in self.ops[e]:
                needed = {}
                for strong, dl in ((False, o.deps), (True, o.sdeps)):
                    for d in dl:
                        if not _needs_sync(d, e, strong):
                            continue
                        k = (d.eng, d.dma)
                        v = d.inc_val
                        if d.dma:
                            cap_op = o.dma_cap.get(d.eng)
                            if cap_op is not None and cap_op.inc_val > v:
                                v = min(v + 16, cap_op.inc_val)
                        if needed.get(k, 0) < v:
                            needed[k] = v
                for k in sorted(needed):
                    v = needed[k]
                    if hw.get(k, 0) < v:
                        eng_obj.wait_ge(sems[k], v)
                        hw[k] = v
                inst = o.fn(eng_obj)
                if o.needs_inc:
                    inst.then_inc(sems[(e, o.dma)], 16 if o.dma else 1)

        for e in ENGS:
            if self.ops[e]:
                getattr(block, e)(lambda eng_obj, _e=e: emit_engine(eng_obj, _e))


def build(n_pairs=N_PAIRS, dbg_names=()):
    nc = bass.Bass("TRN2", target_bir_lowering=False)
    P = Prog()

    def din(name, shape, dt=F32):
        return nc.dram_tensor(name, list(shape), dt, kind="ExternalInput").ap()

    xdr = din("x", (SEQ, D_MODEL))
    w13t = din("w13t", (NEXP, 8, 128, 2 * 8 * 512))
    w2t = din("w2t", (NEXP, 8, 128, 4 * D_MODEL))
    gate8d = din("gate8", (128, 8, 8))
    inwt = din("inwt", (9, 128, 8 * 512))
    opwt = din("opwt", (16, 128, D_MODEL))
    gatewt = din("gatewt", (6, 128, 8 * 512))
    upwt = din("upwt", (6, 128, 8 * 512))
    downwt = din("downwt", (6, 128, 4 * D_MODEL))
    convw_d = din("convw", (128, 18, 4))
    convb_d = din("convb", (128, 18))
    normw_d = din("normw_bc", (128, D_MODEL))
    postw_d = din("postw_bc", (128, D_MODEL))
    ssmw_d = din("ssmw_col", (128, 16))
    dtb_d = din("dtb", (32, 1))
    alog_d = din("alog", (32, 1))
    d2_d = din("d2", (128, 16))
    ident_d = din("identc", (128, 128))
    cuml_d = din("cuml", (128, 128))
    negsl_d = din("negsl", (128, 128))
    lowm_d = din("lowm", (128, 128))
    ones_d = din("onesc", (128, 128))
    negone_d = din("negone", (128, 1))
    pairsel_d = din("pairsel", (32, 16 * 128))
    constv_d = din("constv", (128, 8))
    tokid_d = din("tokid", (128, 8))
    iota8_d = din("iota8", (128, 8))
    padf_d = din("padf", (128, 2 * NBLK))
    baseft_d = din("baseft", (128, 8))
    biota_d = din("biota", (1, NBLK))

    odr = nc.dram_tensor("out", [SEQ, D_MODEL], F32, kind="ExternalOutput").ap()
    xmir = nc.dram_tensor("xmir", [G_PAIR + 128, D_MODEL], F32).ap()
    slotmap = nc.dram_tensor("slotmap", [NSLOT, 2], F32).ap()
    slotout = nc.dram_tensor("slotout", [NSLOT, D_MODEL], F32).ap()
    slotmap3 = slotmap.rearrange("(v p) c -> p v c", p=128)
    slotout3 = slotout.rearrange("(v p) d -> p v d", p=128)

    dbg = {}
    for name, shape in dbg_names:
        dbg[name] = nc.dram_tensor("dbg_" + name, list(shape), F32,
                                   kind="ExternalOutput").ap()
    dbg_ctr = {}

    def dump(name, ap, reads):
        if name not in dbg:
            return
        i = dbg_ctr.get(name, 0)
        dbg_ctr[name] = i + 1
        d = dbg[name]
        dst = d[i] if len(d.shape) == len(ap.shape) + 1 else d
        P.dma("gpsimd", lambda g, _d=dst, _a=ap: g.dma_start(out=_d, in_=_a),
              reads=list(reads))

    sb = lambda name, shape, dt=F32: nc.alloc_sbuf_tensor(name, list(shape),
                                                          dt).ap()
    ident = sb("identS", (128, 128))
    cuml = sb("cumlS", (128, 128))
    negsl = sb("negslS", (128, 128))
    lowmask = sb("lowmS", (128, 128))
    ones = sb("onesS", (128, 128))
    negone = sb("negoneS", (128, 1))
    pairsel = sb("pairselS", (32, 16 * 128))
    normw = sb("normwS", (128, D_MODEL))
    postw = sb("postwS", (128, D_MODEL))
    ssmw = sb("ssmwS", (128, 16))
    convw = sb("convwS", (128, 18, 4))
    convb = sb("convbS", (128, 18))
    dtb = sb("dtbS", (32, 1))
    alog = sb("alogS", (32, 1))
    eA = sb("eAS", (32, 1))
    d2 = sb("d2S", (128, 16))
    gatew = sb("gatewS", (128, 8, 8))
    S = sb("Sstate", (128, 16, 128))
    tail = sb("tailS", (128, 18, 3))
    xb = sb("xbS", (128, 8, D_MODEL))
    vb = sb("vbS", (128, 4, D_MODEL))
    stat = sb("statS", (128, 16))
    rsc = sb("rscS", (128, 28))
    rsu = sb("rsuS", (128, 8), U32)
    rowstat = sb("rowstatS", (1, TH))
    tmsc = sb("tmscS", (128, D_MODEL))
    sqsc = tmsc
    dtxa = sb("dtxaS", (128, 128))
    dtxb = sb("dtxbS", (128, 128))
    tokid = sb("tokidS", (128, 8))
    iota8 = sb("iota8S", (128, 8))
    padf = sb("padfS", (128, 2 * NBLK))
    baseft = sb("baseftS", (128, 8))
    biota = sb("biotaS", (1, NBLK))
    constv = sb("constvS", (128, 8))
    for _i, _cval in enumerate([1e-5, 1e-6, 1.0, -1.0, 1.0 / 2048.0]):
        nc.const_aps.register(F32, float(_cval), constv[:, _i:_i + 1])

    arena_elems = nc.sbuf_bytes_remaining // 4 - 64
    arena = nc.alloc_sbuf_tensor("arena", [128, arena_elems], F32).ap()
    _ov_ranges = {"F": [], "A": []}

    class _OvAlloc:
        def __init__(self, mode):
            self.mode = mode
            self.off = 0

        def __call__(self, shape, keys, dt=F32):
            n = 1
            for d_ in shape[1:]:
                n *= d_
            n_pad = -(-n // 16) * 16
            assert self.off + n_pad <= arena_elems, \
                f"overlay {self.mode} overflow at {shape}"
            view = arena[0:shape[0], self.off:self.off + n]
            if dt is not F32:
                view = view.bitcast(dt)
            if len(shape) == 3:
                view = view.rearrange("p (a b) -> p a b", a=shape[1])
            _ov_ranges[self.mode].append((self.off, self.off + n_pad, keys))
            self.off += n_pad
            return view

    fal = _OvAlloc("F")
    aal = _OvAlloc("A")
    w13sb = fal((128, 2, 2 * 8 * 512), [("w13sb", 0), ("w13sb", 1)])
    w1sb = w13sb
    w2sb = fal((128, 2, 4 * D_MODEL), [("w2sb", 0), ("w2sb", 1)])
    stg = fal((128, D_MODEL), [("stg",)])
    xg = fal((128, 2, D_MODEL), [("xg", 0), ("xg", 1)])
    xgn = fal((128, D_MODEL), [("xgn",)])
    xgfm = fal((128, 2, 8 * 128), [("xgfm", 0), ("xgfm", 1)])
    hsp = fal((128, 4, 128), [("hsp", c) for c in range(4)])
    pst = fal((128, 8, 8), [("pst", t) for t in range(8)])
    sel1b = fal((128, 8, 8), [("sel1b", t) for t in range(8)])
    sel2b = fal((128, 8, 8), [("sel2b", t) for t in range(8)])
    wib = fal((128, 8, 8), [("wib", t) for t in range(8)])
    s12 = fal((128, 8, 4), [("s12", t) for t in range(8)])
    sidxi = fal((128, 8, 2), [("sidxi", t) for t in range(8)], dt=I32)
    asc = fal((128, 8), [("asc",)])
    asc2 = fal((128, 8), [("asc2",)])
    stage2 = fal((128, 2), [("stage2",)])
    slotbuf = fal((128, NBLK, 2), [("slotbuf",)])
    toki = fal((128, 2), [("toki", 0), ("toki", 1)], dt=I32)
    idxt = fal((128, 2, NBLK), [("idxt", 0), ("idxt", 1)])
    idxalli = fal((128, 8, NBLK), [("idxalli",)], dt=I32)
    crow = fal((1, 8), [("crow",)])
    r8a = fal((1, 8), [("r8a",)])
    r8b = fal((1, 8), [("r8b",)])
    r8c = fal((1, 8), [("r8c",)])
    sbrow = fal((1, 8), [("sbrow",)])
    zrow = fal((1, 8), [("zrow",)])
    ebrow = fal((1, NBLK), [("ebrow",)])
    ebi = fal((1, NBLK), [("ebi",)], dt=I32)
    eb1024 = fal((1, NBLK), [("eb1024",)])
    sbbc = fal((128, 8), [("sbbc",)])

    r1f = aal((128, KC, TH), [("r1f", c) for c in range(KC)])
    zf = aal((128, 16, TH), [("zf", j) for j in range(16)])
    ext = aal((128, 18, TH + 4), [("ext", c) for c in range(18)])
    xsy = aal((128, 16, TH), [("xsy", j) for j in range(16)])
    bcf = aal((128, 2, TH), [("bcf", 0), ("bcf", 1)])
    dtf = aal((32, T_TILE), [("dtf", 0), ("dtf", 1)])
    inw = aal((128, 2, 8 * 512), [("inw", 0), ("inw", 1)])
    opw = aal((128, 2, D_MODEL), [("opw", s) for s in range(2)])
    a_fm = aal((32, 256), [("a_fm",)])
    a_tm = aal((128, 32), [("a_tm",)])
    dt_tm = aal((128, 32), [("dt_tm",)])
    pcs_tm = aal((128, 32), [("pcs_tm",)])
    dcol = aal((128, 32), [("dcol",)])
    wdt_tm = aal((128, 32), [("wdt_tm",)])
    npcs_fm = aal((32, 128), [("npcs_fm",)])
    ecs_fm = aal((32, 128), [("ecs_fm",)])
    lam_dg = aal((32, 32), [("lam_dg",)])
    lam_rep = aal((128, 32), [("lam_rep",)])
    lam_col = aal((32, 1), [("lam_col",)])
    bct = aal((128, 128), [("bct",)])
    b_tm = aal((128, 128), [("b_tm",)])
    xs_tm = aal((128, 128), [("xs_tm",)])
    wdtx = aal((128, 128), [("wdtx",)])
    marg = aal((128, 2, 128), [("marg", 0), ("marg", 1)])
    mt_sb = marg
    gt_sb = aal((128, 2, 128), [("gt", 0), ("gt", 1)])
    esc = aal((128, 128), [("esc",)])
    t1s = aal((128, 128), [("t1s",)])
    t2s = aal((128, 128), [("t2s",)])
    zs = aal((128, 256), [("zs",)])
    rbc = aal((128, TH), [("rbc",)])
    xnfh = aal((128, 8, 256), [("xnfh", c) for c in range(8)])
    hsh = aal((128, 4, 256), [("hsh", c) for c in range(4)])
    _sw_lo = _ov_ranges["A"][-2][0]
    _f_wlimit = 16384 + 8192
    assert _sw_lo >= _f_wlimit, (_sw_lo, _f_wlimit)

    for o0, o1, keys in _ov_ranges["F"]:
        for p0, p1, keys2 in _ov_ranges["A"]:
            if o0 < p1 and p0 < o1:
                for k in keys:
                    P.alias.setdefault(k, set()).update(keys2)
                for k in keys2:
                    P.alias.setdefault(k, set()).update(keys)

    ps = [nc.alloc_psum_tensor(f"ps{i}", [128, 512], F32).ap() for i in range(8)]

    R = lambda *k: k

    def T_op(fn, reads, writes):
        P.op("tensor", fn, reads=reads, writes=writes)

    def V_op(fn, reads, writes):
        P.op("vector", fn, reads=reads, writes=writes)

    def S_op(fn, reads, writes):
        P.op("scalar", fn, reads=reads, writes=writes)

    def ld(dst, src, wr):
        P.dma("gpsimd", lambda g, _d=dst, _s=src: g.dma_start(out=_d, in_=_s),
              writes=[wr])

    ld(ident[:, :], ident_d[:, :], R("ident"))
    ld(cuml[:, :], cuml_d[:, :], R("cuml"))
    ld(negsl[:, :], negsl_d[:, :], R("negsl"))
    ld(lowmask[:, :], lowm_d[:, :], R("lowmask"))
    ld(ones[:, :], ones_d[:, :], R("ones"))
    ld(negone[:, :], negone_d[:, :], R("negone"))
    ld(pairsel[:, :], pairsel_d[:, :], R("pairsel"))
    ld(constv[:, :], constv_d[:, :], R("constv"))
    ld(normw[:, :], normw_d[:, :], R("normw"))
    ld(postw[:, :], postw_d[:, :], R("postw"))
    ld(ssmw[:, :], ssmw_d[:, :], R("ssmw"))
    ld(convw[:, :, :], convw_d[:, :, :], R("convw"))
    ld(convb[:, :], convb_d[:, :], R("convb"))
    ld(dtb[:, :], dtb_d[:, :], R("dtb"))
    ld(alog[:, :], alog_d[:, :], R("alog"))
    ld(d2[:, :], d2_d[:, :], R("d2"))
    ld(gatew[:, :, :], gate8d[:, :, :], R("gatew"))
    ld(tokid[:, :], tokid_d[:, :], R("tokid"))
    ld(iota8[:, :], iota8_d[:, :], R("iota8"))
    ld(padf[:, :], padf_d[:, :], R("padf"))
    ld(baseft[:, :], baseft_d[:, :], R("baseft"))
    ld(biota[:, :], biota_d[:, :], R("biota"))
    S_op(lambda s: s.activation(eA[:, :], alog[:, :], AF.Exp),
         reads=[R("alog")], writes=[R("eA")])
    P.op("vector", lambda v: v.memset(S[:, :, :], 0.0),
         writes=[R("S", j) for j in range(16)])
    P.op("vector", lambda v: v.memset(tail[:, :, :], 0.0),
         writes=[R("tail", c) for c in range(18)])
    P.op("vector", lambda v: v.memset(dtxa[:, :], 0.0), writes=[R("dtxa")])
    P.op("vector", lambda v: v.memset(dtxb[:, :], 0.0), writes=[R("dtxb")])
    P.op("vector", lambda v: v.memset(tmsc[:, :], 0.0), writes=[R("tmsc")])
    P.dma("gpsimd", lambda g: g.dma_start(out=xmir[G_PAIR:G_PAIR + 128, :],
                                          in_=tmsc[:, :]),
          reads=[R("tmsc")], writes=[R("xmir")])

    def transpose_to(psi, out_ap, in_ap, pin, reads, writes, copy_eng="scalar"):
        n = in_ap.shape[-1]
        T_op(lambda t, _i=in_ap, _n=n, _p=pin, _b=psi:
             t.transpose(ps[_b][0:_n, 0:_p], _i, ident[0:_p, 0:_p]),
             reads=list(reads) + [R("ident")], writes=[R("ps", psi)])
        if copy_eng == "scalar":
            S_op(lambda s, _o=out_ap, _n=n, _p=pin, _b=psi:
                 s.activation(_o, ps[_b][0:_n, 0:_p], AF.Copy),
                 reads=[R("ps", psi)], writes=writes)
        else:
            V_op(lambda v, _o=out_ap, _n=n, _p=pin, _b=psi:
                 v.tensor_copy(_o, ps[_b][0:_n, 0:_p]),
                 reads=[R("ps", psi)], writes=writes)

    def emit_norm(x_ap, out_ap, sset, reads_x, writes_out, rms=False, eps=1e-5):
        width = D_MODEL
        sm = stat[:, 4 * sset + 0:4 * sset + 1]
        qs = stat[:, 4 * sset + 1:4 * sset + 2]
        rs = stat[:, 4 * sset + 2:4 * sset + 3]
        nb = stat[:, 4 * sset + 3:4 * sset + 4]
        rS = R("stat", sset)
        if not rms:
            V_op(lambda v, _x=x_ap, _o=sm: v.tensor_reduce(_o, _x, AX.X,
                                                           ALU.add),
                 reads=list(reads_x), writes=[rS])
        S_op(lambda s, _x=x_ap, _o=qs: s.activation(sqsc[:, :], _x, AF.Square,
                                                    accum_out=_o),
             reads=list(reads_x), writes=[R("tmsc"), rS])
        if not rms:
            V_op(lambda v: v.tensor_scalar(sm, sm, 1.0 / width, None, ALU.mult),
                 reads=[rS], writes=[rS])
            V_op(lambda v: v.scalar_tensor_tensor(rs, sm, -1.0, sm, ALU.mult,
                                                  ALU.mult),
                 reads=[rS], writes=[rS])
            V_op(lambda v: v.tensor_scalar(qs, qs, 1.0 / width, None, ALU.mult),
                 reads=[rS], writes=[rS])
            V_op(lambda v: v.tensor_tensor(qs, qs, rs, ALU.add),
                 reads=[rS], writes=[rS])
        else:
            V_op(lambda v: v.tensor_scalar(qs, qs, 1.0 / width, None, ALU.mult),
                 reads=[rS], writes=[rS])
        S_op(lambda s: s.activation(rs, qs, AF.Sqrt, bias=float(eps)),
             reads=[rS, R("constv")], writes=[rS])
        V_op(lambda v: v.reciprocal(rs, rs), reads=[rS], writes=[rS])
        if not rms:
            V_op(lambda v: v.scalar_tensor_tensor(nb, sm, -1.0, rs, ALU.mult,
                                                  ALU.mult),
                 reads=[rS], writes=[rS])
            S_op(lambda s, _x=x_ap, _o=out_ap: s.activation(_o, _x, AF.Identity,
                                                            bias=nb, scale=rs),
                 reads=list(reads_x) + [rS], writes=list(writes_out))
        else:
            S_op(lambda s, _x=x_ap, _o=out_ap: s.activation(_o, _x, AF.Identity,
                                                            scale=rs),
                 reads=list(reads_x) + [rS], writes=list(writes_out))

    def tm_to_fm(src_fn, src_reads, dst, dst_tag, t0, pb0=6, nch=KC):
        for c in range(nch):
            pb = pb0 + (c % 2)
            T_op(lambda t, _s=src_fn(c), _b=pb:
                 t.transpose(ps[_b][:, 0:128], _s, ident[:, :]),
                 reads=list(src_reads) + [R("ident")], writes=[R("ps", pb)])
            S_op(lambda s, _c=c, _b=pb, _t0=t0:
                 s.activation(dst[:, _c, _t0:_t0 + 128], ps[_b][:, 0:128],
                              AF.Copy),
                 reads=[R("ps", pb)], writes=[R(dst_tag, c)])

    regcell = {}

    def emit_ffn_sparse(which):
        for g in range(8):
            P.dma("gpsimd", lambda q, _g=g: q.dma_start(
                out=xmir[128 * _g:128 * (_g + 1), :], in_=xb[:, _g, :]),
                reads=[R("xb", g)], writes=[R("xmir")])
        P.dma("gpsimd", lambda q: q.dma_start(out=slotmap3[:, :, :],
                                              in_=padf[:, :]),
              reads=[R("padf")], writes=[R("slotmap")])
        V_op(lambda v: v.memset(crow[:, :], 0.0), reads=[], writes=[R("crow")])
        for ts in range(8):
            emit_norm(xb[:, ts, :], tmsc[:, :], ts % 4, [R("xb", ts)],
                      [R("tmsc")])
            xnf8 = xgfm[:, 0, :].rearrange("p (a b) -> p a b", a=8)
            for c in range(8):
                pb2 = 6 + (c % 2)
                T_op(lambda t, _c=c, _b=pb2:
                     t.transpose(ps[_b][:, 0:128],
                                 tmsc[:, 128 * _c:128 * (_c + 1)],
                                 ident[:, :]),
                     reads=[R("tmsc"), R("ident")], writes=[R("ps", pb2)])
                S_op(lambda s, _c=c, _b=pb2, _x=xnf8:
                     s.activation(_x[:, _c, :], ps[_b][:, 0:128], AF.Copy),
                     reads=[R("ps", pb2)], writes=[R("xgfm", 0)])
            pb = 5
            for kc in range(KC):
                T_op(lambda t, _kc=kc, _b=pb, _x=xnf8:
                     t.matmul(ps[_b][:, 0:8], _x[:, _kc, :],
                              gatew[:, _kc, :], start=(_kc == 0),
                              stop=(_kc == KC - 1)),
                     reads=[R("xgfm", 0), R("gatew")], writes=[R("ps", pb)])
            lg = rsc[:, 0:8]
            vals = rsc[:, 8:16]
            idxf = rsc[:, 16:24]
            mx = rsc[:, 24:25]
            winv = rsc[:, 25:26]
            rr = R("rsc")
            V_op(lambda v: v.tensor_reduce(mx, ps[pb][:, 0:8], AX.X, ALU.max,
                                           negate=True),
                 reads=[R("ps", pb)], writes=[rr])
            S_op(lambda s: s.activation(lg, ps[pb][:, 0:8], AF.Exp, bias=mx),
                 reads=[R("ps", pb), rr], writes=[rr])
            V_op(lambda v: v.max(out=vals, in_=lg), reads=[rr], writes=[rr])
            V_op(lambda v: v.max_index(out=rsu[:, :], in_max=vals,
                                       in_values=lg),
                 reads=[rr], writes=[R("rsu")])
            V_op(lambda v: v.tensor_copy(idxf, rsu[:, :]),
                 reads=[R("rsu")], writes=[rr])
            V_op(lambda v: v.tensor_tensor(winv, rsc[:, 8:9], rsc[:, 9:10],
                                           ALU.add),
                 reads=[rr], writes=[rr])
            V_op(lambda v: v.reciprocal(winv, winv), reads=[rr], writes=[rr])
            V_op(lambda v, _t=ts: v.tensor_scalar(sel1b[:, _t, :], iota8[:, :],
                                                  rsc[:, 16:17], None,
                                                  ALU.is_equal),
                 reads=[rr, R("iota8")], writes=[R("sel1b", ts)])
            V_op(lambda v, _t=ts: v.tensor_scalar(sel2b[:, _t, :], iota8[:, :],
                                                  rsc[:, 17:18], None,
                                                  ALU.is_equal),
                 reads=[rr, R("iota8")], writes=[R("sel2b", ts)])
            V_op(lambda v, _t=ts: v.tensor_tensor(asc[:, :], sel1b[:, _t, :],
                                                  sel2b[:, _t, :], ALU.add),
                 reads=[R("sel1b", ts), R("sel2b", ts)], writes=[R("asc")])
            T_op(lambda t: t.matmul(ps[6][:, 0:8], cuml[:, :], asc[:, :],
                                    start=True, stop=False),
                 reads=[R("cuml"), R("asc")], writes=[R("ps", 6)])
            T_op(lambda t: t.matmul(ps[6][:, 0:8], ones[0:1, :], crow[:, :],
                                    start=False, stop=True),
                 reads=[R("ones"), R("crow")], writes=[R("ps", 6)])
            V_op(lambda v, _t=ts: v.tensor_copy(pst[:, _t, :], ps[6][:, 0:8]),
                 reads=[R("ps", 6)], writes=[R("pst", ts)])
            T_op(lambda t: t.matmul(ps[7][0:1, 0:8], ones[:, 0:1], asc[:, :],
                                    start=True, stop=True),
                 reads=[R("ones"), R("asc")], writes=[R("ps", 7)])
            V_op(lambda v: v.tensor_tensor(crow[:, :], crow[:, :],
                                           ps[7][0:1, 0:8], ALU.add),
                 reads=[R("ps", 7), R("crow")], writes=[R("crow")])
            V_op(lambda v: v.tensor_tensor(asc[:, :], asc[:, :], lg, ALU.mult),
                 reads=[R("asc"), rr], writes=[R("asc")])
            V_op(lambda v, _t=ts: v.tensor_scalar(wib[:, _t, :], asc[:, :],
                                                  winv, 0.5, ALU.mult,
                                                  ALU.mult),
                 reads=[R("asc"), rr], writes=[R("wib", ts)])
        V_op(lambda v: v.tensor_scalar(r8a[:, :], crow[:, :], 0.0, None,
                                       ALU.is_gt),
             reads=[R("crow")], writes=[R("r8a")])
        for j in range(1, 8):
            V_op(lambda v, _j=j: v.scalar_tensor_tensor(
                 r8a[:, :], crow[:, :], 128.0 * _j, r8a[:, :], ALU.is_gt,
                 ALU.add),
                 reads=[R("crow"), R("r8a")], writes=[R("r8a")])
        V_op(lambda v: v.memset(zrow[:, :], 0.0), reads=[], writes=[R("zrow")])
        V_op(lambda v: v.tensor_tensor_scan(r8c[:, :], r8a[:, :], zrow[:, :],
                                            0.0, ALU.add, ALU.add),
             reads=[R("r8a"), R("zrow")], writes=[R("r8c")])
        V_op(lambda v: v.tensor_tensor(r8b[:, :], r8c[:, :], r8a[:, :],
                                       ALU.subtract),
             reads=[R("r8c"), R("r8a")], writes=[R("r8b")])
        V_op(lambda v: v.tensor_scalar(sbrow[:, :], r8b[:, :], 128.0, None,
                                       ALU.mult),
             reads=[R("r8b")], writes=[R("sbrow")])
        V_op(lambda v: v.tensor_scalar(ebrow[:, :], biota[:, :],
                                       r8b[0:1, 0:1], None, ALU.is_ge),
             reads=[R("biota"), R("r8b")], writes=[R("ebrow")])
        for e in range(1, 8):
            V_op(lambda v, _e=e: v.scalar_tensor_tensor(
                 ebrow[:, :], biota[:, :], r8b[0:1, _e:_e + 1], ebrow[:, :],
                 ALU.is_ge, ALU.add),
                 reads=[R("biota"), R("r8b"), R("ebrow")], writes=[R("ebrow")])
        V_op(lambda v: v.tensor_scalar(ebrow[:, :], ebrow[:, :], -1.0, 7.0,
                                       ALU.add, ALU.min),
             reads=[R("ebrow")], writes=[R("ebrow")])
        V_op(lambda v: v.tensor_copy(ebi[:, :], ebrow[:, :]),
             reads=[R("ebrow")], writes=[R("ebi")])
        V_op(lambda v: v.tensor_scalar(eb1024[:, :], ebrow[:, :], 1024.0,
                                       None, ALU.mult),
             reads=[R("ebrow")], writes=[R("eb1024")])
        T_op(lambda t: t.matmul(ps[5][:, 0:8], ones[0:1, :], sbrow[:, :],
                                start=True, stop=True),
             reads=[R("ones"), R("sbrow")], writes=[R("ps", 5)])
        V_op(lambda v: v.tensor_copy(sbbc[:, :], ps[5][:, 0:8]),
             reads=[R("ps", 5)], writes=[R("sbbc")])
        T_op(lambda t: t.matmul(ps[6][:, 0:NBLK], ones[0:1, :],
                                eb1024[0:1, :], start=True, stop=True),
             reads=[R("ones"), R("eb1024")], writes=[R("ps", 6)])
        for ftx in range(8):
            fp = ftx % 2
            V_op(lambda v, _f=ftx, _p=fp: v.tensor_scalar(
                 idxt[:, _p, :], ps[6][:, 0:NBLK], baseft[:, _f:_f + 1],
                 None, ALU.add),
                 reads=[R("ps", 6), R("baseft")], writes=[R("idxt", fp)])
            V_op(lambda v, _f=ftx, _p=fp: v.tensor_copy(
                 idxalli[:, _f, :], idxt[:, _p, :]),
                 reads=[R("idxt", fp)], writes=[R("idxalli")])
        for ts in range(8):
            for k in range(2):
                selb = sel1b if k == 0 else sel2b
                stag = ("sel1b", ts) if k == 0 else ("sel2b", ts)
                V_op(lambda v, _t=ts: v.tensor_tensor(asc[:, :],
                                                      pst[:, _t, :],
                                                      sbbc[:, :], ALU.add),
                     reads=[R("pst", ts), R("sbbc")], writes=[R("asc")])
                V_op(lambda v, _t=ts, _s=selb: v.tensor_tensor(
                     asc2[:, :], asc[:, :], _s[:, _t, :], ALU.mult),
                     reads=[R("asc"), R(*stag)], writes=[R("asc2")])
                V_op(lambda v, _t=ts, _k=k: v.tensor_reduce(
                     s12[:, _t, _k:_k + 1], asc2[:, :], AX.X, ALU.add),
                     reads=[R("asc2")], writes=[R("s12", ts)])
                V_op(lambda v, _t=ts, _k=k: v.tensor_scalar(
                     s12[:, _t, _k:_k + 1], s12[:, _t, _k:_k + 1], -1.0, None,
                     ALU.add),
                     reads=[R("s12", ts)], writes=[R("s12", ts)])
                V_op(lambda v, _t=ts, _s=selb: v.tensor_tensor(
                     asc2[:, :], wib[:, _t, :], _s[:, _t, :], ALU.mult),
                     reads=[R("wib", ts), R(*stag)], writes=[R("asc2")])
                V_op(lambda v, _t=ts, _k=k: v.tensor_reduce(
                     s12[:, _t, 2 + _k:3 + _k], asc2[:, :], AX.X, ALU.add),
                     reads=[R("asc2")], writes=[R("s12", ts)])
            V_op(lambda v, _t=ts: v.tensor_copy(sidxi[:, _t, :],
                                                s12[:, _t, 0:2]),
                 reads=[R("s12", ts)], writes=[R("sidxi", ts)])
            for k in range(2):
                V_op(lambda v, _t=ts: v.tensor_copy(stage2[:, 0:1],
                                                    tokid[:, _t:_t + 1]),
                     reads=[R("tokid")], writes=[R("stage2")])
                V_op(lambda v, _t=ts, _k=k: v.tensor_copy(
                     stage2[:, 1:2], s12[:, _t, 2 + _k:3 + _k]),
                     reads=[R("s12", ts)], writes=[R("stage2")])
                P.dma("gpsimd", lambda q, _t=ts, _k=k: q.indirect_dma_start(
                      out=slotmap[:, :],
                      out_offset=IndirectOffsetOnAxis(sidxi[:, _t, _k:_k + 1],
                                                      0),
                      in_=stage2[:, :], in_offset=None),
                      reads=[R("stage2"), R("sidxi", ts)],
                      writes=[R("slotmap")])
        P.dma("gpsimd", lambda q: q.dma_start(out=slotbuf[:, :, :],
                                              in_=slotmap3[:, :, :]),
              reads=[R("slotmap")], writes=[R("slotbuf")])
        dump("crow", crow[:, :], [R("crow")])
        dump("sbrow", sbrow[:, :], [R("sbrow")])
        dump("ebrow", ebrow[:, :], [R("ebrow")])
        dump("s12", s12[:, :, :], [R("s12", t) for t in range(8)])
        dump("slotbuf", slotbuf[:, :, :], [R("slotbuf")])
        dump("wib", wib[:, :, :], [R("wib", t) for t in range(8)])
        w13flat = w13t.rearrange("e f p m -> (e f p) m")
        w2flat = w2t.rearrange("e f p m -> (e f p) m")

        def emit_wgather(step):
            if step >= 8 * NBLK:
                return
            sb_, sf = divmod(step, 8)
            par = step % 2
            P.dma("gpsimd", lambda q, _b=sb_, _f=sf, _p=par:
                  q.indirect_dma_start(
                  out=w13sb[:, _p, :], out_offset=None, in_=w13flat,
                  in_offset=IndirectOffsetOnAxis(idxalli[:, _f, _b:_b + 1],
                                                 0)),
                  reads=[R("idxalli")], writes=[R("w13sb", par)])
            P.dma("gpsimd", lambda q, _b=sb_, _f=sf, _p=par:
                  q.indirect_dma_start(
                  out=w2sb[:, _p, :], out_offset=None, in_=w2flat,
                  in_offset=IndirectOffsetOnAxis(idxalli[:, _f, _b:_b + 1],
                                                 0)),
                  reads=[R("idxalli")], writes=[R("w2sb", par)])

        def emit_prologue_dma(b):
            p2 = b % 2
            V_op(lambda v, _b=b, _p=p2: v.tensor_copy(toki[:, _p:_p + 1],
                                                      slotbuf[:, _b, 0:1]),
                 reads=[R("slotbuf")], writes=[R("toki", p2)])
            P.dma("gpsimd", lambda q, _p=p2: q.indirect_dma_start(
                  out=xg[:, _p, :], out_offset=None, in_=xmir[:, :],
                  in_offset=IndirectOffsetOnAxis(toki[:, _p:_p + 1], 0)),
                  reads=[R("toki", p2), R("xmir")], writes=[R("xg", p2)])

        def emit_prologue_ln(b):
            p2 = b % 2
            emit_norm(xg[:, p2, :], xgn[:, :], b % 4, [R("xg", p2)],
                      [R("xgn")])
            xgfmb = xgfm[:, p2, :].rearrange("p (a b) -> p a b", a=8)
            for c in range(8):
                pb2 = 4 + 2 * p2 + (c % 2)
                T_op(lambda t, _c=c, _b=pb2:
                     t.transpose(ps[_b][:, 0:128],
                                 xgn[:, 128 * _c:128 * (_c + 1)],
                                 ident[:, :]),
                     reads=[R("xgn"), R("ident")], writes=[R("ps", pb2)])
                S_op(lambda s, _c=c, _b=pb2, _x=xgfmb, _p=p2:
                     s.activation(_x[:, _c, :], ps[_b][:, 0:128], AF.Copy),
                     reads=[R("ps", pb2)], writes=[R("xgfm", p2)])

        def emit_compute(b):
            p2 = b % 2
            tb = 4 + 2 * p2
            xgfmb = xgfm[:, p2, :].rearrange("p (a b) -> p a b", a=8)
            pend = []
            for ft in range(8):
                step = 8 * b + ft
                par = step % 2
                emit_wgather(step)
                w1v = w13sb[:, par, 0:4096].rearrange("p (k m) -> p k m", k=8)
                w3v = w13sb[:, par, 4096:8192].rearrange("p (k m) -> p k m",
                                                         k=8)
                w2v = w2sb[:, par, :].rearrange("p (k m) -> p k m", k=4)
                for fc in range(4):
                    bg = 2 * (fc % 2)
                    bu = bg + 1
                    for kc in range(KC):
                        T_op(lambda t, _w=w1v, _fc=fc, _kc=kc, _b=bg,
                             _x=xgfmb, _p=par:
                             t.matmul(ps[_b][:, 0:128],
                                      _w[:, _kc, 128 * _fc:128 * (_fc + 1)],
                                      _x[:, _kc, :], start=(_kc == 0),
                                      stop=(_kc == KC - 1)),
                             reads=[R("w13sb", par), R("xgfm", p2)],
                             writes=[R("ps", bg)])
                    for kc in range(KC):
                        T_op(lambda t, _w=w3v, _fc=fc, _kc=kc, _b=bu,
                             _x=xgfmb, _p=par:
                             t.matmul(ps[_b][:, 0:128],
                                      _w[:, _kc, 128 * _fc:128 * (_fc + 1)],
                                      _x[:, _kc, :], start=(_kc == 0),
                                      stop=(_kc == KC - 1)),
                             reads=[R("w13sb", par), R("xgfm", p2)],
                             writes=[R("ps", bu)])
                    S_op(lambda s, _fc=fc, _b=bg:
                         s.activation(hsp[:, _fc, :], ps[_b][:, 0:128],
                                      AF.Silu),
                         reads=[R("ps", bg)], writes=[R("hsp", fc)])
                    V_op(lambda v, _fc=fc, _b=bu:
                         v.tensor_tensor(hsp[:, _fc, :], hsp[:, _fc, :],
                                         ps[_b][:, 0:128], ALU.mult),
                         reads=[R("ps", bu), R("hsp", fc)],
                         writes=[R("hsp", fc)])
                    def down_emit(_w=w2v, _fc=fc, _ft=ft, _p=par):
                        for dh in range(2):
                            bd = tb + dh
                            T_op(lambda t, _w=_w, _fc=_fc, _dh=dh, _b=bd,
                                 _ft=_ft:
                                 t.matmul(ps[_b][:, :], hsp[:, _fc, :],
                                          _w[:, _fc,
                                             512 * _dh:512 * (_dh + 1)],
                                          start=(_ft == 0 and _fc == 0),
                                          stop=(_ft == 7 and _fc == 3)),
                                 reads=[R("hsp", _fc), R("w2sb", _p)],
                                 writes=[R("ps", bd)])
                    pend.append(down_emit)
                    if len(pend) > 1:
                        pend.pop(0)()
            while pend:
                pend.pop(0)()
            for dh in range(2):
                S_op(lambda s, _dh=dh, _b=tb + dh, _bb=b:
                     s.activation(stg[:, 512 * _dh:512 * (_dh + 1)],
                                  ps[_b][:, :], AF.Copy,
                                  scale=slotbuf[:, _bb, 1:2]),
                     reads=[R("ps", tb + dh), R("slotbuf")], writes=[R("stg")])
            dump("sod", stg[:, :], [R("stg")])
            P.dma("sync", lambda q, _b=b: q.dma_start(
                  out=slotout3[:, _b, :], in_=stg[:, :]),
                  reads=[R("stg")], writes=[R("slotout")])

        emit_prologue_dma(0)
        emit_prologue_dma(1)
        emit_prologue_ln(0)
        emit_prologue_ln(1)
        for b in range(NBLK):
            if b + 2 < NBLK:
                emit_prologue_dma(b + 2)
            emit_compute(b)
            if b + 2 < NBLK:
                emit_prologue_ln(b + 2)
        for ts in range(8):
            for k in range(2):
                P.dma("gpsimd", lambda q, _t=ts, _k=k: q.indirect_dma_start(
                      out=xg[:, _k, :], out_offset=None, in_=slotout[:, :],
                      in_offset=IndirectOffsetOnAxis(
                          sidxi[:, _t, _k:_k + 1], 0)),
                      reads=[R("sidxi", ts), R("slotout")],
                      writes=[R("xg", k)])
                V_op(lambda v, _t=ts, _k=k: v.tensor_tensor(
                     xb[:, _t, :], xb[:, _t, :], xg[:, _k, :], ALU.add),
                     reads=[R("xg", k), R("xb", ts)], writes=[R("xb", ts)])

    def emit_swiglu(gb):
        fts = [512] * 5 + [192]
        for hh in range(2):
            for i, tsu in enumerate((2 * hh, 2 * hh + 1)):
                emit_norm(vb[:, tsu, :], tmsc[:, :], tsu, [R("vb", tsu)],
                          [R("tmsc")], rms=True, eps=1e-6)
                V_op(lambda v: v.tensor_tensor(tmsc[:, :], tmsc[:, :],
                                               postw[:, :], ALU.mult),
                     reads=[R("tmsc"), R("postw")], writes=[R("tmsc")])
                tm_to_fm(lambda c: tmsc[:, 128 * c:128 * (c + 1)],
                         [R("tmsc")], xnfh, "xnfh", 128 * i, pb0=2)
            for ft in range(6):
                par = ft % 2
                P.dma("sync", lambda q, _s=gatewt[ft], _p=par: q.dma_start(
                      out=w13sb[:, _p, 0:4096], in_=_s),
                      writes=[R("w13sb", par)])
                P.dma("sync", lambda q, _s=upwt[ft], _p=par: q.dma_start(
                      out=w13sb[:, _p, 4096:8192], in_=_s),
                      writes=[R("w13sb", par)])
                P.dma("sync", lambda q, _s=downwt[ft], _p=par: q.dma_start(
                      out=w2sb[:, _p, :], in_=_s), writes=[R("w2sb", par)])
                w1v = w13sb[:, par, 0:4096].rearrange("p (k m) -> p k m", k=8)
                w3v = w13sb[:, par, 4096:8192].rearrange("p (k m) -> p k m",
                                                         k=8)
                w2v = w2sb[:, par, :].rearrange("p (k m) -> p k m", k=4)
                fsz = fts[ft]
                nfc = (fsz + 127) // 128
                for fc in range(nfc):
                    m = min(128, fsz - 128 * fc)
                    bg = 2 * (fc % 2)
                    bu = bg + 1
                    for kc in range(KC):
                        T_op(lambda t, _w=w1v, _fc=fc, _kc=kc, _m=m, _b=bg:
                             t.matmul(ps[_b][0:_m, 0:256],
                                      _w[:, _kc, 128 * _fc:128 * _fc + _m],
                                      xnfh[:, _kc, :], start=(_kc == 0),
                                      stop=(_kc == KC - 1)),
                             reads=[R("w13sb", par), R("xnfh", kc)],
                             writes=[R("ps", bg)])
                    for kc in range(KC):
                        T_op(lambda t, _w=w3v, _fc=fc, _kc=kc, _m=m, _b=bu:
                             t.matmul(ps[_b][0:_m, 0:256],
                                      _w[:, _kc, 128 * _fc:128 * _fc + _m],
                                      xnfh[:, _kc, :], start=(_kc == 0),
                                      stop=(_kc == KC - 1)),
                             reads=[R("w13sb", par), R("xnfh", kc)],
                             writes=[R("ps", bu)])
                    S_op(lambda s, _fc=fc, _m=m, _b=bg:
                         s.activation(hsh[0:_m, _fc, :], ps[_b][0:_m, 0:256],
                                      AF.Silu),
                         reads=[R("ps", bg)], writes=[R("hsh", fc)])
                    V_op(lambda v, _fc=fc, _m=m, _b=bu:
                         v.tensor_tensor(hsh[0:_m, _fc, :], hsh[0:_m, _fc, :],
                                         ps[_b][0:_m, 0:256], ALU.mult),
                         reads=[R("ps", bu), R("hsh", fc)],
                         writes=[R("hsh", fc)])
                    for tsl in range(2):
                        for dh in range(2):
                            bd = 4 + 2 * tsl + dh
                            T_op(lambda t, _w=w2v, _fc=fc, _dh=dh, _b=bd,
                                 _t=tsl, _m=m, _ft=ft, _n=nfc:
                                 t.matmul(ps[_b][:, :],
                                          hsh[0:_m, _fc,
                                              128 * _t:128 * (_t + 1)],
                                          _w[0:_m, _fc,
                                             512 * _dh:512 * (_dh + 1)],
                                          start=(_ft == 0 and _fc == 0),
                                          stop=(_ft == 5 and _fc == _n - 1)),
                                 reads=[R("hsh", fc), R("w2sb", par)],
                                 writes=[R("ps", bd)])
            for tsl in range(2):
                tsu = 2 * hh + tsl
                for dh in range(2):
                    bd = 4 + 2 * tsl + dh
                    xa = xb[:, gb + tsu, 512 * dh:512 * (dh + 1)]
                    V_op(lambda v, _b=bd, _x=xa, _g=gb + tsu:
                         v.tensor_tensor(_x, _x, ps[_b][:, :], ALU.add),
                         reads=[R("ps", bd), R("xb", gb + tsu)],
                         writes=[R("xb", gb + tsu)])
        for tsu in range(4):
            V_op(lambda v, _t=tsu, _g=gb + tsu:
                 v.tensor_tensor(xb[:, _g, :], xb[:, _g, :], vb[:, _t, :],
                                 ALU.add),
                 reads=[R("xb", gb + tsu), R("vb", tsu)],
                 writes=[R("xb", gb + tsu)])

    def emit_attn(gb):
        for tsu in range(4):
            emit_norm(xb[:, gb + tsu, :], vb[:, tsu, :], tsu,
                      [R("xb", gb + tsu)], [R("vb", tsu)])
        dump("u", vb[:, :, :], [R("vb", t) for t in range(4)])
        for half in range(2):
            for i, tsu in enumerate((2 * half, 2 * half + 1)):
                emit_norm(vb[:, tsu, :], tmsc[:, :], tsu, [R("vb", tsu)],
                          [R("tmsc")], rms=True, eps=1e-6)
                V_op(lambda v: v.tensor_tensor(tmsc[:, :], tmsc[:, :],
                                               normw[:, :], ALU.mult),
                     reads=[R("tmsc"), R("normw")], writes=[R("tmsc")])
                tm_to_fm(lambda c: tmsc[:, 128 * c:128 * (c + 1)], [R("tmsc")],
                         r1f, "r1f", 128 * i)
            for mt in range(9):
                par = mt % 2
                wr = R("inw", par)
                P.dma("sync", lambda g, _d=inw[:, par, :], _s=inwt[mt]:
                      g.dma_start(out=_d, in_=_s), writes=[wr])
                inwv = inw[:, par, :].rearrange("p (k m) -> p k m", k=8)
                msizes = [128] * 4 if mt < 8 else [128, 128, 32]
                for fs, msz in enumerate(msizes):
                    m0 = 512 * mt + 128 * fs
                    pb = fs % 2
                    for kc in range(KC):
                        T_op(lambda t, _w=inwv, _fs=fs, _kc=kc, _m=msz, _b=pb:
                             t.matmul(ps[_b][0:_m, 0:TH],
                                      _w[:, _kc, 128 * _fs:128 * _fs + _m],
                                      r1f[:, _kc, :], start=(_kc == 0),
                                      stop=(_kc == KC - 1)),
                             reads=[wr, R("r1f", kc)], writes=[R("ps", pb)])
                    if m0 < D_INNER:
                        S_op(lambda s, _c=m0 // 128, _b=pb:
                             s.activation(zf[:, _c, :], ps[_b][:, 0:TH],
                                          AF.Copy),
                             reads=[R("ps", pb)], writes=[R("zf", m0 // 128)])
                    elif m0 < D_INNER + CONV_DIM:
                        c = (m0 - D_INNER) // 128
                        S_op(lambda s, _c=c, _b=pb:
                             s.activation(ext[:, _c, 3:3 + TH],
                                          ps[_b][:, 0:TH], AF.Copy),
                             reads=[R("ps", pb)], writes=[R("ext", c)])
                    else:
                        S_op(lambda s, _b=pb:
                             s.activation(a_fm[:, 0:TH], ps[_b][0:32, 0:TH],
                                          AF.Exp, bias=dtb[:, 0:1]),
                             reads=[R("ps", pb), R("dtb")], writes=[R("a_fm")])
                        S_op(lambda s, _h=half:
                             s.activation(dtf[:, TH * _h:TH * (_h + 1)],
                                          a_fm[:, 0:TH], AF.Ln, bias=1.0),
                             reads=[R("a_fm"), R("constv")],
                             writes=[R("dtf", half)])
            for c in range(18):
                V_op(lambda v, _c=c: v.tensor_copy(ext[:, _c, 0:3],
                                                   tail[:, _c, :]),
                     reads=[R("tail", c)], writes=[R("ext", c)])
                V_op(lambda v, _c=c: v.tensor_copy(tail[:, _c, :],
                                                   ext[:, _c, TH:TH + 3]),
                     reads=[R("ext", c)], writes=[R("tail", c)])
                V_op(lambda v, _c=c: v.tensor_scalar(zs[:, 0:TH],
                                                     ext[:, _c, 0:TH],
                                                     convw[:, _c, 0:1], None,
                                                     ALU.mult),
                     reads=[R("ext", c), R("convw")], writes=[R("zs")])
                for k in range(1, 4):
                    V_op(lambda v, _c=c, _k=k:
                         v.scalar_tensor_tensor(zs[:, 0:TH],
                                                ext[:, _c, _k:_k + TH],
                                                convw[:, _c, _k:_k + 1],
                                                zs[:, 0:TH], ALU.mult,
                                                ALU.add),
                         reads=[R("ext", c), R("convw"), R("zs")],
                         writes=[R("zs")])
                if c < 16:
                    dst, wt = xsy[:, c, :], R("xsy", c)
                else:
                    dst, wt = bcf[:, c - 16, :], R("bcf", c - 16)
                S_op(lambda s, _d=dst, _c=c:
                     s.activation(_d, zs[:, 0:TH], AF.Silu,
                                  bias=convb[:, _c:_c + 1]),
                     reads=[R("zs"), R("convb")], writes=[wt])
            for lc in range(2):
                tc0 = 128 * lc
                dts = dtf[:, TH * half + tc0:TH * half + tc0 + 128]
                rdt = R("dtf", half)
                V_op(lambda v, _d=dts: v.tensor_scalar(a_fm[:, 0:128], _d,
                                                       eA[:, 0:1], None,
                                                       ALU.mult),
                     reads=[rdt, R("eA")], writes=[R("a_fm")])
                transpose_to(7, a_tm[:, :], a_fm[0:32, 0:128], 32, [R("a_fm")],
                             [R("a_tm")], copy_eng="vector")
                transpose_to(7, dt_tm[:, :], dts, 32, [rdt], [R("dt_tm")],
                             copy_eng="vector")
                T_op(lambda t: t.matmul(ps[6][:, 0:32], cuml[:, :], a_tm[:, :],
                                        start=True, stop=True),
                     reads=[R("cuml"), R("a_tm")], writes=[R("ps", 6)])
                V_op(lambda v: v.tensor_copy(pcs_tm[:, :], ps[6][:, 0:32]),
                     reads=[R("ps", 6)], writes=[R("pcs_tm")])
                T_op(lambda t: t.matmul(ps[6][:, 0:32], negsl[:, :],
                                        a_tm[:, :], start=True, stop=True),
                     reads=[R("negsl"), R("a_tm")], writes=[R("ps", 6)])
                S_op(lambda s: s.activation(dcol[:, :], ps[6][:, 0:32],
                                            AF.Exp),
                     reads=[R("ps", 6)], writes=[R("dcol")])
                V_op(lambda v: v.tensor_tensor(wdt_tm[:, :], dt_tm[:, :],
                                               dcol[:, :], ALU.mult),
                     reads=[R("dt_tm"), R("dcol")], writes=[R("wdt_tm")])
                T_op(lambda t: t.transpose(ps[6][0:32, 0:128], pcs_tm[:, :],
                                           ident[:, :]),
                     reads=[R("pcs_tm"), R("ident")], writes=[R("ps", 6)])
                S_op(lambda s: s.activation(npcs_fm[:, :], ps[6][0:32, 0:128],
                                            AF.Identity, scale=-1.0),
                     reads=[R("ps", 6), R("constv")], writes=[R("npcs_fm")])
                S_op(lambda s: s.activation(ecs_fm[:, :], npcs_fm[:, :],
                                            AF.Exp),
                     reads=[R("npcs_fm")], writes=[R("ecs_fm")])
                T_op(lambda t: t.matmul(ps[6][0:32, 0:1], a_tm[:, :],
                                        negone[:, :], start=True, stop=True),
                     reads=[R("a_tm"), R("negone")], writes=[R("ps", 6)])
                S_op(lambda s: s.activation(lam_col[:, :], ps[6][0:32, 0:1],
                                            AF.Exp),
                     reads=[R("ps", 6)], writes=[R("lam_col")])
                V_op(lambda v: v.tensor_scalar(lam_dg[:, :], ident[0:32, 0:32],
                                               lam_col[:, 0:1], None,
                                               ALU.mult),
                     reads=[R("ident"), R("lam_col")], writes=[R("lam_dg")])
                T_op(lambda t: t.matmul(ps[6][:, 0:32], ones[0:32, :],
                                        lam_dg[:, :], start=True, stop=True),
                     reads=[R("ones"), R("lam_dg")], writes=[R("ps", 6)])
                V_op(lambda v: v.tensor_copy(lam_rep[:, :], ps[6][:, 0:32]),
                     reads=[R("ps", 6)], writes=[R("lam_rep")])
                Bc = bcf[:, 0, tc0:tc0 + 128]
                Cc = bcf[:, 1, tc0:tc0 + 128]
                T_op(lambda t, _B=Bc, _C=Cc: t.matmul(ps[6][:, 0:128], _B, _C,
                                                      start=True, stop=True),
                     reads=[R("bcf", 0), R("bcf", 1)], writes=[R("ps", 6)])
                S_op(lambda s: s.activation(bct[:, :], ps[6][:, 0:128],
                                            AF.Copy),
                     reads=[R("ps", 6)], writes=[R("bct")])
                transpose_to(6, b_tm[:, :], Bc, 128, [R("bcf", 0)],
                             [R("b_tm")])
                for j in range(16):
                    xs_ap = xsy[:, j, tc0:tc0 + 128]
                    rxs = R("xsy", j)
                    transpose_to(0, xs_tm[:, :], xs_ap, 128, [rxs],
                                 [R("xs_tm")])
                    V_op(lambda v, _j=j: v.tensor_scalar(
                         dtxa[:, 0:64], xs_tm[:, 0:64],
                         dt_tm[:, 2 * _j:2 * _j + 1], None, ALU.mult),
                         reads=[R("xs_tm"), R("dt_tm")], writes=[R("dtxa")])
                    V_op(lambda v, _j=j: v.tensor_scalar(
                         dtxb[:, 64:128], xs_tm[:, 64:128],
                         dt_tm[:, 2 * _j + 1:2 * _j + 2], None, ALU.mult),
                         reads=[R("xs_tm"), R("dt_tm")], writes=[R("dtxb")])
                    V_op(lambda v, _j=j: v.tensor_scalar(
                         wdtx[:, 0:64], xs_tm[:, 0:64],
                         wdt_tm[:, 2 * _j:2 * _j + 1], None, ALU.mult),
                         reads=[R("xs_tm"), R("wdt_tm")], writes=[R("wdtx")])
                    V_op(lambda v, _j=j: v.tensor_scalar(
                         wdtx[:, 64:128], xs_tm[:, 64:128],
                         wdt_tm[:, 2 * _j + 1:2 * _j + 2], None, ALU.mult),
                         reads=[R("xs_tm"), R("wdt_tm")], writes=[R("wdtx")])
                    for hh in range(2):
                        h = 2 * j + hh
                        pb = 1 + hh
                        T_op(lambda t, _h=h, _b=pb:
                             t.matmul(ps[_b][:, 0:128],
                                      ident[0:32, _h:_h + 1]
                                      .broadcast_to((32, 128)),
                                      npcs_fm[:, :], start=True, stop=True),
                             reads=[R("ident"), R("npcs_fm")],
                             writes=[R("ps", pb)])
                        V_op(lambda v, _hh=hh, _b=pb:
                             v.tensor_tensor(marg[:, _hh, :],
                                             ps[_b][:, 0:128],
                                             lowmask[:, :], ALU.add),
                             reads=[R("ps", pb), R("lowmask")],
                             writes=[R("marg", hh)])
                        S_op(lambda s, _hh=hh, _h=h:
                             s.activation(mt_sb[:, _hh, :], marg[:, _hh, :],
                                          AF.Exp, bias=pcs_tm[:, _h:_h + 1]),
                             reads=[R("marg", hh), R("pcs_tm")],
                             writes=[R("mt", hh)])
                        V_op(lambda v, _hh=hh:
                             v.tensor_tensor(gt_sb[:, _hh, :],
                                             mt_sb[:, _hh, :], bct[:, :],
                                             ALU.mult),
                             reads=[R("mt", hh), R("bct")],
                             writes=[R("gt", hh)])
                    T_op(lambda t: t.matmul(ps[3][:, 0:128], dtxa[:, :],
                                            gt_sb[:, 0, :], start=True,
                                            stop=False),
                         reads=[R("dtxa"), R("gt", 0)], writes=[R("ps", 3)])
                    T_op(lambda t: t.matmul(ps[3][:, 0:128], dtxb[:, :],
                                            gt_sb[:, 1, :], start=False,
                                            stop=True),
                         reads=[R("dtxb"), R("gt", 1)], writes=[R("ps", 3)])
                    T_op(lambda t: t.matmul(ps[4][:, 0:128], b_tm[:, :],
                                            wdtx[:, :], start=True,
                                            stop=True),
                         reads=[R("b_tm"), R("wdtx")], writes=[R("ps", 4)])
                    T_op(lambda t, _j=j, _C=Cc:
                         t.matmul(ps[5][:, 0:128], S[:, _j, :], _C,
                                  start=True, stop=True),
                         reads=[R("S", j), R("bcf", 1)], writes=[R("ps", 5)])
                    T_op(lambda t, _j=j:
                         t.matmul(ps[6][:, 0:128],
                                  pairsel[:, 128 * _j:128 * (_j + 1)],
                                  ecs_fm[:, :], start=True, stop=True),
                         reads=[R("pairsel"), R("ecs_fm")],
                         writes=[R("ps", 6)])
                    S_op(lambda s: s.activation(esc[:, :], ps[6][:, 0:128],
                                                AF.Copy),
                         reads=[R("ps", 6)], writes=[R("esc")])
                    V_op(lambda v: v.tensor_tensor(t1s[:, :], esc[:, :],
                                                   ps[5][:, 0:128], ALU.mult),
                         reads=[R("esc"), R("ps", 5)], writes=[R("t1s")])
                    V_op(lambda v, _j=j, _x=xs_ap:
                         v.scalar_tensor_tensor(t2s[:, :], _x,
                                                d2[:, _j:_j + 1], t1s[:, :],
                                                ALU.mult, ALU.add),
                         reads=[rxs, R("d2"), R("t1s")], writes=[R("t2s")])
                    V_op(lambda v, _x=xs_ap: v.tensor_tensor(_x, t2s[:, :],
                                                             ps[3][:, 0:128],
                                                             ALU.add),
                         reads=[R("t2s"), R("ps", 3)], writes=[rxs])
                    for hh in range(2):
                        sl = slice(64 * hh, 64 * (hh + 1))
                        V_op(lambda v, _j=j, _sl=sl, _h=2 * j + hh:
                             v.scalar_tensor_tensor(S[:, _j, _sl],
                                                    S[:, _j, _sl],
                                                    lam_rep[:, _h:_h + 1],
                                                    ps[4][:, _sl], ALU.mult,
                                                    ALU.add),
                             reads=[R("S", j), R("lam_rep"), R("ps", 4)],
                             writes=[R("S", j)])
            for j in range(16):
                S_op(lambda s, _j=j: s.activation(zs[:, 0:TH], zf[:, _j, :],
                                                  AF.Silu),
                     reads=[R("zf", j)], writes=[R("zs")])
                V_op(lambda v, _j=j: v.tensor_tensor(xsy[:, _j, :],
                                                     xsy[:, _j, :],
                                                     zs[:, 0:TH], ALU.mult),
                     reads=[R("xsy", j), R("zs")], writes=[R("xsy", j)])
            for j in range(16):
                S_op(lambda s, _j=j: s.activation(zs[:, 0:TH], xsy[:, _j, :],
                                                  AF.Square),
                     reads=[R("xsy", j)], writes=[R("zs")])
                T_op(lambda t, _j=j: t.matmul(ps[7][0:1, 0:TH], ones[:, 0:1],
                                              zs[:, 0:TH], start=(_j == 0),
                                              stop=(_j == 15)),
                     reads=[R("zs"), R("ones")], writes=[R("ps", 7)])
            S_op(lambda s: s.activation(rowstat[0:1, :], ps[7][0:1, 0:TH],
                                        AF.Sqrt, scale=1.0 / D_INNER,
                                        bias=1e-6),
                 reads=[R("ps", 7), R("constv")], writes=[R("rowstat")])
            V_op(lambda v: v.reciprocal(rowstat[0:1, :], rowstat[0:1, :]),
                 reads=[R("rowstat")], writes=[R("rowstat")])
            T_op(lambda t: t.matmul(ps[7][:, 0:TH], ones[0:1, :],
                                    rowstat[0:1, :], start=True, stop=True),
                 reads=[R("rowstat"), R("ones")], writes=[R("ps", 7)])
            S_op(lambda s: s.activation(rbc[:, :], ps[7][:, 0:TH], AF.Copy),
                 reads=[R("ps", 7)], writes=[R("rbc")])
            for j in range(16):
                V_op(lambda v, _j=j:
                     v.scalar_tensor_tensor(xsy[:, _j, :], xsy[:, _j, :],
                                            ssmw[:, _j:_j + 1], rbc[:, :],
                                            ALU.mult, ALU.mult),
                     reads=[R("xsy", j), R("ssmw"), R("rbc")],
                     writes=[R("xsy", j)])
            for slot in range(2):
                P.dma("sync", lambda g, _d=opw[:, slot, :], _s=opwt[slot]:
                      g.dma_start(out=_d, in_=_s), writes=[R("opw", slot)])
            for j in range(16):
                for tsl in range(2):
                    for dh in range(2):
                        bd = 4 + 2 * tsl + dh
                        T_op(lambda t, _j=j, _b=bd, _t=tsl, _dh=dh:
                             t.matmul(ps[_b][:, :],
                                      xsy[:, _j, 128 * _t:128 * (_t + 1)],
                                      opw[:, _j % 2,
                                          512 * _dh:512 * (_dh + 1)],
                                      start=(_j == 0), stop=(_j == 15)),
                             reads=[R("xsy", j), R("opw", j % 2)],
                             writes=[R("ps", bd)])
                if j + 2 <= 15:
                    P.dma("sync",
                          lambda g, _d=opw[:, j % 2, :], _s=opwt[j + 2]:
                          g.dma_start(out=_d, in_=_s),
                          writes=[R("opw", j % 2)])
            for tsl in range(2):
                tsu = 2 * half + tsl
                for dh in range(2):
                    bd = 4 + 2 * tsl + dh
                    V_op(lambda v, _b=bd, _t=tsu, _dh=dh:
                         v.tensor_tensor(vb[:, _t,
                                            512 * _dh:512 * (_dh + 1)],
                                         vb[:, _t,
                                            512 * _dh:512 * (_dh + 1)],
                                         ps[_b][:, :], ALU.add),
                         reads=[R("ps", bd), R("vb", tsu)],
                         writes=[R("vb", tsu)])
        dump("v", vb[:, :, :], [R("vb", t) for t in range(4)])
        emit_swiglu(gb)

    store_regions = []
    for pair in range(n_pairs):
        for g in range(8):
            P.dma("gpsimd",
                  lambda q, _p=pair, _g=g: q.dma_start(
                      out=xb[:, _g, :],
                      in_=xdr[G_PAIR * _p + 128 * _g:
                              G_PAIR * _p + 128 * (_g + 1), :]),
                  writes=[R("xb", g)])
        emit_ffn_sparse(0)
        dump("xffn1", xb[:, :, :], [R("xb", g) for g in range(8)])
        emit_attn(0)
        emit_attn(4)
        dump("xattn", xb[:, :, :], [R("xb", g) for g in range(8)])
        emit_ffn_sparse(1)
        for g in range(8):
            sr = R("stored", pair, g)
            store_regions.append(sr)
            P.dma("gpsimd",
                  lambda q, _p=pair, _g=g: q.dma_start(
                      out=odr[G_PAIR * _p + 128 * _g:
                              G_PAIR * _p + 128 * (_g + 1), :],
                      in_=xb[:, _g, :]),
                  reads=[R("xb", g)], writes=[sr])
    P.op("gpsimd", lambda g: g.nop(), reads=store_regions)

    with contextlib.ExitStack() as ctx:
        sems = {}
        for e in ENGS:
            sems[(e, False)] = ctx.enter_context(nc.semaphore(f"sc_{e}"))
            sems[(e, True)] = ctx.enter_context(nc.semaphore(f"sq_{e}"))
        block = ctx.enter_context(nc.Block())
        P.emit(block, sems)
    return nc



def _host_layouts(inputs):
    f = np.float32
    c = np.ascontiguousarray
    w1 = np.asarray(inputs["L0_ffn_w1"], f)
    w3 = np.asarray(inputs["L0_ffn_w3"], f)
    w2 = np.asarray(inputs["L0_ffn_w2"], f)
    hw = {}
    w1l = w1.reshape(8, 8, 128, 8, 512).transpose(0, 3, 2, 1, 4) \
        .reshape(8, 8, 128, 8 * 512)
    w3l = w3.reshape(8, 8, 128, 8, 512).transpose(0, 3, 2, 1, 4) \
        .reshape(8, 8, 128, 8 * 512)
    hw["w13t"] = c(np.concatenate([w1l, w3l], axis=-1))
    hw["w2t"] = c(w2.reshape(8, 8, 4, 128, 1024).transpose(0, 1, 3, 2, 4)
                  .reshape(8, 8, 128, 4 * 1024))
    gw = np.asarray(inputs["L0_ffn_gate_w"], f)
    hw["gate8"] = c(gw.reshape(8, 128, 8).transpose(1, 0, 2))
    ipw = np.asarray(inputs["L0_attn_in_proj_w"], f)
    t = np.zeros((1024, 4608), f)
    t[:, :4384] = ipw.T
    hw["inwt"] = c(t.reshape(8, 128, 9, 512).transpose(2, 1, 0, 3)
                   .reshape(9, 128, 8 * 512))
    opw = np.asarray(inputs["L0_attn_out_proj_w"], f)
    hw["opwt"] = c(opw.T.reshape(16, 128, 1024))
    gw2 = np.asarray(inputs["L0_attn_gate_w"], f)
    uw = np.asarray(inputs["L0_attn_up_w"], f)
    dw = np.asarray(inputs["L0_attn_down_w"], f)

    def updown_lhsT(w):
        t = np.zeros((1024, 3072), f)
        t[:, :2752] = w.T
        return c(t.reshape(8, 128, 6, 512).transpose(2, 1, 0, 3)
                 .reshape(6, 128, 8 * 512))

    hw["gatewt"] = updown_lhsT(gw2)
    hw["upwt"] = updown_lhsT(uw)
    t = np.zeros((3072, 1024), f)
    t[:2752, :] = dw.T
    hw["downwt"] = c(t.reshape(6, 4, 128, 1024).transpose(0, 2, 1, 3)
                     .reshape(6, 128, 4 * 1024))
    cw = np.asarray(inputs["L0_attn_conv_w"], f)
    hw["convw"] = c(cw.reshape(18, 128, 4).transpose(1, 0, 2))
    cb = np.asarray(inputs["L0_attn_conv_b"], f)
    hw["convb"] = c(cb.reshape(18, 128).T)
    hw["normw_bc"] = c(np.broadcast_to(
        np.asarray(inputs["L0_attn_norm_w"], f), (128, 1024)))
    hw["postw_bc"] = c(np.broadcast_to(
        np.asarray(inputs["L0_attn_post_norm_w"], f), (128, 1024)))
    hw["ssmw_col"] = c(np.asarray(inputs["L0_attn_ssm_norm_w"], f)
                       .reshape(16, 128).T)
    hw["dtb"] = c(np.asarray(inputs["L0_attn_dt_bias"], f).reshape(32, 1))
    hw["alog"] = c(np.asarray(inputs["L0_attn_A_log"], f).reshape(32, 1))
    Dv = np.asarray(inputs["L0_attn_D"], f)
    p = np.arange(128)[:, None] // 64
    j = np.arange(16)[None, :]
    hw["d2"] = c(Dv[2 * j + p].astype(f))
    hw["constv"] = c(np.tile(np.array(
        [1e-5, 1e-6, 1.0, -1.0, 1.0 / 2048.0, 0.0, 0.0, 0.0], f), (128, 1)))
    hw["identc"] = np.eye(128, dtype=f)
    si = np.arange(128)[:, None]
    ti = np.arange(128)[None, :]
    hw["cuml"] = (si <= ti).astype(f)
    hw["negsl"] = -((si > ti).astype(f))
    hw["lowm"] = np.where(si <= ti, 0.0, -1e30).astype(f)
    hw["onesc"] = np.ones((128, 128), f)
    hw["negone"] = np.full((128, 1), -1.0, f)
    hhh = np.arange(32)[:, None]
    jj = np.arange(16 * 128)[None, :]
    hw["pairsel"] = (hhh == (2 * (jj // 128) + (jj % 128) // 64)).astype(f)
    pp = np.arange(128)[:, None]
    tt = np.arange(8)[None, :]
    hw["tokid"] = c((128 * tt + pp).astype(f))
    hw["iota8"] = c(np.tile(np.arange(8, dtype=f), (128, 1)))
    hw["padf"] = c(np.tile(np.array([float(G_PAIR), 0.0], f), (128, NBLK)))
    hw["biota"] = c(np.arange(NBLK, dtype=f).reshape(1, NBLK))
    hw["baseft"] = c((128 * tt + pp).astype(f))
    return hw


_NC_CACHE = {}


def kernel(**inputs):
    x = np.ascontiguousarray(np.asarray(inputs["L0_attn_hidden_states"],
                                        np.float32))
    hw = _host_layouts(inputs)
    if "nc" not in _NC_CACHE:
        _NC_CACHE["nc"] = build()
    nc = _NC_CACHE["nc"]
    in_maps = [dict(hw, x=np.ascontiguousarray(x[c0])) for c0 in range(8)]
    res = run_bass_kernel_spmd(nc, in_maps, core_ids=list(range(8)))
    return np.stack([res.results[c0]["out"] for c0 in range(8)], axis=0)
